# version 07
# baseline (speedup 1.0000x reference)
import math
import os
os.environ["GAUGE_NTFF_IGNORE_DMA"] = "1"
import numpy as np
import concourse.bass as bass
import concourse.mybir as mybir
from concourse.bass_utils import run_bass_kernel_spmd

F32 = mybir.dt.float32
F32R = mybir.dt.float32r
AF = mybir.ActivationFunctionType
ALU = mybir.AluOpType
AX = mybir.AxisListType

D = 2048
NCH = 16
HEAD128 = 16
NPASS = 3
F_FFN = 8192
NFC = F_FFN // 128
EPS_A = 1e-6
EPS_LN = 1e-5
EPS_RMS = 1e-6
EPS_GN = 1e-5 * 64.0

class Stream:
    def __init__(self, name, engine_attr):
        self.name = name
        self.engine_attr = engine_attr
        self.cmp_count = 0
        self.dma_count = 0
        self.ops = []

class Tok:
    __slots__ = ("stream", "dma", "count")
    def __init__(self, stream, dma, count):
        self.stream = stream
        self.dma = dma
        self.count = count

class Sched:
    NSEM_DMA = 6

    def __init__(self, nc):
        self.t_drain = False
        self.nc = nc
        self.streams = {}
        for name, attr in [("T", "tensor"), ("V", "vector"), ("S", "scalar"),
                           ("G", "gpsimd"), ("Y", "sync")]:
            self.streams[name] = Stream(name, attr)

    def op(self, stream_name, emit_fn, deps=(), dma=False):
        st = self.streams[stream_name]
        if dma:
            st.dma_count += 1
            cnt = st.dma_count
        else:
            st.cmp_count += 1
            cnt = st.cmp_count
        waits = []
        need_drain = 0
        for d in deps:
            if d is None:
                continue
            if d.stream is st and not d.dma:
                if stream_name != "T" or self.t_drain:
                    need_drain = max(need_drain, d.count)
                continue
            waits.append(d)
        best = {}
        for d in waits:
            key = (d.stream.name, d.dma)
            if key not in best or best[key].count < d.count:
                best[key] = d
        st.ops.append((emit_fn, dma, cnt, list(best.values()), need_drain))
        return Tok(st, dma, cnt)

    def barrier_tokens(self):
        toks = []
        for st in self.streams.values():
            if st.cmp_count:
                toks.append(Tok(st, False, st.cmp_count))
            if st.dma_count:
                toks.append(Tok(st, True, st.dma_count))
        return toks

    def emit(self, block, sems_cmp, sems_dma):
        awaited = {name: set() for name in self.streams}
        for st in self.streams.values():
            for _emit_fn, _dma, _cnt, waits, _nd in st.ops:
                for d in waits:
                    if not d.dma:
                        awaited[d.stream.name].add(d.count)
        rank = {}
        for name in self.streams:
            for i, c in enumerate(sorted(awaited[name])):
                rank[(name, c)] = i + 1

        for name, st in self.streams.items():
            if not st.ops:
                continue
            eng_stream_factory = getattr(block, st.engine_attr)

            def make_body(st=st, name=name):
                def body(eng):
                    hw = {}
                    last_drain = 0
                    for emit_fn, dma, cnt, waits, need_drain in st.ops:
                        if need_drain > last_drain and not dma:
                            eng.drain()
                            last_drain = cnt - 1
                        for d in waits:
                            if d.dma:
                                k = d.count
                                sem_i = (k - 1) % Sched.NSEM_DMA
                                per = 16 * ((k + Sched.NSEM_DMA - 1) // Sched.NSEM_DMA)
                                key = (d.stream.name, "d", sem_i)
                                if hw.get(key, -1) < per:
                                    eng.wait_ge(sems_dma[d.stream.name][sem_i], per)
                                    hw[key] = per
                            else:
                                val = rank[(d.stream.name, d.count)]
                                key = (d.stream.name, "c", 0)
                                if hw.get(key, -1) < val:
                                    eng.wait_ge(sems_cmp[d.stream.name], val)
                                    hw[key] = val
                        inst = emit_fn(eng)
                        if dma:
                            sem_i = (cnt - 1) % Sched.NSEM_DMA
                            inst.then_inc(sems_dma[st.name][sem_i], 16)
                        elif cnt in awaited[name]:
                            if inst is None:
                                inst = eng.nop()
                            inst.then_inc(sems_cmp[st.name], 1)
                return body
            eng_stream_factory(make_body())



class Buf:
    __slots__ = ("writer", "readers", "name")
    def __init__(self, name=""):
        self.writer = None
        self.readers = []
        self.name = name

    def rd(self):
        return [self.writer] if self.writer else []

    def wr(self):
        deps = list(self.readers)
        if self.writer:
            deps.append(self.writer)
        return deps

    def did_read(self, tok):
        self.readers.append(tok)

    def did_write(self, tok):
        self.writer = tok
        self.readers = []


def deps_rw(reads, writes):
    d = []
    for b in reads:
        d += b.rd()
    for b in writes:
        d += b.wr()
    return d

def commit_rw(tok, reads, writes):
    for b in reads:
        b.did_read(tok)
    for b in writes:
        b.did_write(tok)


RSQ128 = 1.0 / math.sqrt(128.0)

class MK:
    def __init__(self, T_core=2048, npass=NPASS, debug_taps=()):
        self.T = T_core
        self.NP = npass
        self.nc = bass.Bass("TRN2", target_bir_lowering=False, debug=False)
        self.sc = Sched(self.nc)
        self.debug_taps = set(debug_taps)
        self.dram = {}
        self.sbuf = {}
        self.bufs = {}
        self._ctx = []
        self._psum_banks = None

    def dram_t(self, name, shape, dtype=F32, kind="Internal"):
        if kind == "Internal" and name in self.debug_taps:
            kind = "ExternalOutput"
        h = self.nc.dram_tensor(name, shape, dtype, kind=kind)
        self.dram[name] = h
        self.bufs[("dram", name)] = Buf(name)
        return h

    def dbuf(self, name):
        return self.bufs[("dram", name)]

    def sbuf_t(self, name, shape, dtype=F32):
        h = self.nc.alloc_sbuf_tensor(name, shape, dtype)
        self.sbuf[name] = h
        self.bufs[("sb", name)] = Buf(name)
        return h

    def sbbuf(self, name):
        return self.bufs[("sb", name)]

    def _op(self, stream, emit, reads, writes, dma=False):
        tok = self.sc.op(stream, emit, deps=deps_rw(reads, writes), dma=dma)
        commit_rw(tok, reads, writes)
        return tok

    def load(self, dst_ap, src_ap, rbuf, wbuf, q="Y", slow=False):
        return self._op(q, lambda e: e.dma_start(out=dst_ap, in_=src_ap, allow_slow_non_contiguous=slow), [rbuf], [wbuf], dma=True)

    def store(self, dst_ap, src_ap, rbuf, wbuf, q="G", slow=False):
        return self._op(q, lambda e: e.dma_start(out=dst_ap, in_=src_ap, allow_slow_non_contiguous=slow), [rbuf], [wbuf], dma=True)

    def vop(self, emit, reads, writes, stream="V"):
        return self._op(stream, emit, reads, writes)

    def sop(self, emit, reads, writes):
        return self._op("S", emit, list(reads) + [self.kc_buf], writes)

    def gop(self, emit, reads, writes):
        return self._op("G", emit, reads, writes)

    def top(self, emit, reads, writes):
        return self._op("T", emit, reads, writes)

    def split(self, hi_ap, lo_ap, src_ap, rbufs, whi, wlo, stream="V"):
        self.sop(lambda e: e.activation(hi_ap, src_ap, AF.Copy), rbufs, [whi])
        def emit(e):
            return e.tensor_tensor(lo_ap, src_ap, hi_ap.bitcast(F32), ALU.subtract)
        return self._op(stream, emit, list(rbufs) + [whi], [wlo])

    def mm3(self, ps_ap, pairs, rbufs, wbuf, start=True, stop=True, tp=None):
        np_ = self.NP
        def emit(e):
            last = None
            n = len(pairs)
            for i, (lh, ll, rh, rl) in enumerate(pairs):
                first = (i == 0) and start
                plist = [(lh, rh)]
                if np_ >= 2 and rl is not None:
                    plist.append((lh, rl))
                if np_ >= 3 and ll is not None:
                    plist.append((ll, rh))
                for j, (a, b) in enumerate(plist):
                    is_first = first and j == 0
                    is_last = stop and (i == n - 1) and (j == len(plist) - 1)
                    try:
                        last = e.matmul(ps_ap, a, b, start=is_first, stop=is_last,
                                        tile_position=tp)
                    except Exception as ex:
                        raise RuntimeError(f"matmul fail: lhsT={a} rhs={b} out={ps_ap}") from ex
            return last
        return self._op("T", emit, rbufs, [wbuf])

    def mm1(self, ps_ap, pairs, rbufs, wbuf, start=True, stop=True):
        def emit(e):
            last = None
            for i, (a, b) in enumerate(pairs):
                last = e.matmul(ps_ap, a, b,
                                start=(start and i == 0), stop=(stop and i == len(pairs) - 1))
            return last
        return self._op("T", emit, rbufs, [wbuf])

    ARENA_BYTES = 174 * 1024

    def _init_arena(self):
        self.arena = self.nc.alloc_sbuf_tensor("arena", [128, self.ARENA_BYTES], mybir.dt.uint8)
        self.arena_addr = self.nc.lookup_mloc(self.arena).addr
        self.views = {}
        self._stage_tag = 0

    def stage(self, layout, keep=(), auto_base=0):
        toks = self.sc.barrier_tokens()
        bar = self.sc.op("G", lambda e: None, deps=toks)
        old_views = self.views
        self._stage_tag += 1
        self.views = {}
        off = auto_base
        for entry in layout:
            if len(entry) == 4:
                name, shape, dtype, fixed = entry
            else:
                (name, shape, dtype), fixed = entry, None
            per_part = int(np.prod(shape[1:])) * mybir.dt.size(dtype)
            if fixed is not None:
                this_off = fixed
            else:
                off = (off + 63) & ~63
                this_off = off
                off += per_part
            assert this_off + per_part <= self.ARENA_BYTES, f"arena overflow at {name}"
            vname = f"vw{self._stage_tag}_{name}"
            hnd = self.nc.alloc_sbuf_tensor_at(vname, list(shape), dtype,
                                               offset=self.arena_addr + this_off)
            if name in keep and name in old_views:
                b = old_views[name][1]
            else:
                b = Buf(vname)
                b.writer = bar
            self.views[name] = (hnd, b)
        return bar

    def v_ap(self, name):
        return self.views[name][0].ap()

    def v_hnd(self, name):
        return self.views[name][0]

    def v_buf(self, name):
        return self.views[name][1]

    def _init_psum(self):
        self.psum = self.nc.alloc_sbuf_tensor
        self._psum_guard = self.nc.psum_tensor("ps_all", [128, 4096], F32)
        self.psum_h = self._psum_guard.__enter__()
        self.psb = [Buf(f"psum{b}") for b in range(8)]

    def ps_ap(self, bank, cols=512, off=0):
        return self.psum_h[:, bank * 512 + off: bank * 512 + off + cols]

    def ps_ap_p(self, bank, p0, pn, cols, off=0):
        return self.psum_h[p0:p0 + pn, bank * 512 + off: bank * 512 + off + cols]

    FCONSTS = [EPS_LN, EPS_RMS, EPS_GN, EPS_A, 1e-30]

    def _init_fconsts(self):
        nc = self.nc
        vals = np.array(self.FCONSTS, np.float32)
        data = np.repeat(vals[None, :], 128, 0)
        t = nc.inline_tensor(data, name="c_fvals")
        self.k_fc = self.sbuf_t("k_fc", [128, len(self.FCONSTS)], F32)
        self.load(self.k_fc.ap(), t.ap(), Buf("z0"), self.sbbuf("k_fc"))
        for i, v in enumerate(self.FCONSTS):
            nc.const_aps.register(F32, float(v), self.k_fc.ap()[:, i:i+1])
        self.kc_buf = self.sbbuf("k_fc")

    def _init_consts(self):
        nc = self.nc
        T = self.T
        ident = np.eye(128, dtype=np.float32)
        tri = (np.arange(128)[:, None] < np.arange(128)[None, :]).astype(np.float32)
        strict = tri.copy()
        ones = np.ones((128, 128), dtype=np.float32)
        mm_ = np.zeros((128, 4 * 512), dtype=np.float32)
        am_ = np.zeros((128, 4 * 512), dtype=np.float32)
        for sub in range(4):
            keep = (np.arange(512)[None, :] >= (np.arange(128)[:, None] + sub * 128))
            mm_[:, sub * 512:(sub + 1) * 512] = keep
            am_[:, sub * 512:(sub + 1) * 512] = np.where(keep, 0.0, -1e30)
        half = 64
        theta = (10000.0 ** (-np.arange(half, dtype=np.float64) * 2.0 / 128.0))
        pos = np.arange(T, dtype=np.float64)
        ang = pos[None, :] * theta[:, None]
        cos_t = np.cos(ang)
        sin_t = np.sin(ang)
        cos_ext = np.concatenate([cos_t, cos_t], 0).astype(np.float32)
        sin_sgn = np.concatenate([-sin_t, sin_t], 0).astype(np.float32)
        self.c_id = nc.inline_tensor(ident, name="c_id")
        self.c_tri = nc.inline_tensor(tri, name="c_tri")
        self.c_strict = nc.inline_tensor(strict, name="c_strict")
        self.c_strictu = nc.inline_tensor(strict.astype(np.uint8), name="c_strictu")
        self.c_ones = nc.inline_tensor(ones, name="c_ones")
        self.c_mm = nc.inline_tensor(mm_, name="c_mm")
        self.c_am = nc.inline_tensor(am_, name="c_am")
        self.c_cos = nc.inline_tensor(cos_ext, name="c_cos")
        self.c_sin = nc.inline_tensor(sin_sgn, name="c_sin")
        self.k_id = self.sbuf_t("k_id", [128, 128], F32)
        self.k_tri = self.sbuf_t("k_tri", [128, 128], F32R)
        self.k_strict = self.sbuf_t("k_strict", [128, 128], F32)
        self.k_strictu = self.sbuf_t("k_strictu", [128, 128], mybir.dt.uint8)
        self.k_ones = self.sbuf_t("k_ones", [128, 128], F32R)
        self.k_zero = self.sbuf_t("k_zero", [128, 128], F32)
        self.k_mm = self.sbuf_t("k_mm", [128, 4 * 512], F32)
        self.k_am = self.sbuf_t("k_am", [128, 4 * 512], F32)
        zb = Buf("zero")
        self.gop(lambda e: e.memset(self.k_zero.ap(), 0.0), [], [self.sbbuf("k_zero")])
        self.load(self.k_strictu.ap(), self.c_strictu.ap(), zb, self.sbbuf("k_strictu"))
        for dst, src, q in [(self.k_id, self.c_id, "Y"), (self.k_strict, self.c_strict, "Y"),
                            (self.k_mm, self.c_mm, "Y"), (self.k_am, self.c_am, "Y"),
                            (self.k_tri, self.c_tri, "G"), (self.k_ones, self.c_ones, "G")]:
            self.load(dst.ap(), src.ap(), zb, self.sbbuf(dst.name), q=q)

    def _decl_inputs(self):
        nc = self.nc
        T = self.T
        dt_in = [
            ("x", [T, D]),
            ("wq_a", [D, D]), ("wqs_a", [D, D]), ("wk_a", [D, D]), ("wks_a", [D, D]),
            ("wv_a", [D, D]), ("wo_a", [D, D]),
            ("wq_f", [D, D]), ("wk_f", [D, D]), ("wv_f", [D, D]), ("wo_f_nat", [D, D]),
            ("a_f", [D, 16]), ("b_f", [16, F_FFN]),
            ("win", [D, F_FFN]), ("wup", [D, F_FFN]), ("wdn", [F_FFN, D]),
            ("ln1_row", [1, D]), ("ln2_row", [1, D]),
            ("wr_r", [D, D]), ("wk_r", [D, D]), ("wv_r", [D, D]), ("wg_r", [D, D]),
            ("wo_r", [D, D]),
            ("maa_w1", [D, 160]), ("maa_w2", [5, 32, D]),
            ("dec_w1", [D, 64]), ("dec_w2", [64, D]),
            ("maa_cols", [128, 6 * 16]),
            ("decay_row", [1, D]),
            ("u_col", [128, 16]),
            ("lnr_cols", [128, 32]),
        ]
        for name, shape in dt_in:
            self.dram_t(name, shape, F32, kind="ExternalInput")
        self.dram_t("out", [T, D], F32, kind="ExternalOutput")

    def _decl_scratch(self):
        T = self.T
        self.dram_t("R", [T, D])
        self.dram_t("XN", [1 + T, D])
        self.dram_t("XNF_H", [NCH, 128, 1 + T], F32R)
        self.dram_t("XNF_L", [NCH, 128, 1 + T], F32R)
        self.dram_t("XNF_F", [NCH, 128, 1 + T])
        self.dram_t("KF_H", [NCH, 128, T], F32R)
        self.dram_t("KF_L", [NCH, 128, T], F32R)
        self.dram_t("VT_H", [T, D], F32R)
        self.dram_t("VT_L", [T, D], F32R)
        self.dram_t("AOF_H", [NCH, 128, T], F32R)
        self.dram_t("AOF_L", [NCH, 128, T], F32R)
        self.dram_t("BF_H", [NCH, 128, T], F32R)
        self.dram_t("BF_L", [NCH, 128, T], F32R)
        self.dram_t("BF2_H", [NCH, 128, T], F32R)
        self.dram_t("BF2_L", [NCH, 128, T], F32R)
        self.dram_t("RW_R", [NCH, 128, T])
        self.dram_t("RW_K", [T, D])
        self.dram_t("RW_V", [T, D])
        self.dram_t("RW_G", [NCH, 128, T])
        self.dram_t("RW_W", [T, D])
        self.dram_t("ROWTMP", [16, D])
        self.dram_t("ZF_H", [16, T], F32R)
        self.dram_t("ZF_L", [16, T], F32R)

    def transpose_group(self, bank, src_aps, rbufs):
        idap = self.k_id.ap()
        def emit(e):
            last = None
            for i, ap in enumerate(src_aps):
                last = e.transpose(self.ps_ap(bank, 128, i * 128), ap, idap)
            return last
        return self._op("T", emit, rbufs + [self.sbbuf("k_id")], [self.psb[bank]])

    def ln_norm_tile(self, vx, vsq, vst, eps, kind="ln"):
        xap = self.v_ap(vx)
        sqap = self.v_ap(vsq)
        st = self.v_ap(vst)
        bx, bsq, bst = self.v_buf(vx), self.v_buf(vsq), self.v_buf(vst)
        self.sop(lambda e: e.activation(sqap, xap, AF.Square), [bx], [bsq])
        if kind == "ln":
            def red(e):
                e.tensor_reduce(st[:, 0:1], xap, AX.X, ALU.add)
                return e.tensor_reduce(st[:, 1:2], sqap, AX.X, ALU.add)
            self.vop(red, [bx, bsq], [bst])
            def stats(e):
                e.tensor_scalar(st[:, 0:1], st[:, 0:1], 1.0 / D, None, ALU.mult)
                e.tensor_scalar(st[:, 1:2], st[:, 1:2], 1.0 / D, None, ALU.mult)
                e.drain()
                e.tensor_tensor(st[:, 2:3], st[:, 0:1], st[:, 0:1], ALU.mult)
                e.drain()
                return e.tensor_tensor(st[:, 2:3], st[:, 1:2], st[:, 2:3], ALU.subtract)
            self.vop(stats, [bst], [bst])
        else:
            self.vop(lambda e: e.tensor_reduce(st[:, 1:2], sqap, AX.X, ALU.add), [bsq], [bst])
            self.vop(lambda e: e.tensor_scalar(st[:, 2:3], st[:, 1:2], 1.0 / D, None, ALU.mult), [bst], [bst])
        def rs(e):
            e.activation(st[:, 3:4], st[:, 2:3], AF.Ln, bias=float(eps))
            e.drain()
            return e.activation(st[:, 3:4], st[:, 3:4], AF.Exp, scale=-0.5)
        self.sop(rs, [bst], [bst])
        if kind == "ln":
            self.vop(lambda e: e.tensor_scalar(xap, xap, st[:, 0:1], st[:, 3:4],
                                               ALU.subtract, ALU.mult), [bst, bx], [bx])
        else:
            self.vop(lambda e: e.tensor_scalar(xap, xap, st[:, 3:4], None, ALU.mult),
                     [bst, bx], [bx])

    def stage_ln(self, src_name, write_f32=False, eps=EPS_LN):
        T = self.T
        layout = [(f"x{tt}", [128, D], F32) for tt in range(4)]
        layout += [("sq", [128, D], F32), ("st", [128, 8], F32),
                   ("th", [128, 512], F32R), ("tl", [128, 512], F32R),
                   ("tf", [128, 512], F32)]
        self.stage(layout)
        src = self.dram[src_name]
        dsrc = self.dbuf(src_name)
        XN, XNF_H, XNF_L, XNF_F = (self.dram[n] for n in ("XN", "XNF_H", "XNF_L", "XNF_F"))
        bXN, bH, bL, bF = (self.dbuf(n) for n in ("XN", "XNF_H", "XNF_L", "XNF_F"))
        for g in range(T // 512):
            for tt in range(4):
                t0 = g * 512 + tt * 128
                vx = f"x{tt}"
                self.load(self.v_ap(vx), src.ap()[t0:t0 + 128, :], dsrc, self.v_buf(vx))
                self.ln_norm_tile(vx, "sq", "st", eps, kind="ln")
                self.store(XN.ap()[1 + t0: 1 + t0 + 128, :], self.v_ap(vx),
                           self.v_buf(vx), bXN)
            for c in range(NCH):
                bank = c % 2
                srcs = [self.v_ap(f"x{tt}")[:, c * 128:(c + 1) * 128] for tt in range(4)]
                self.transpose_group(bank, srcs, [self.v_buf(f"x{tt}") for tt in range(4)])
                th, tl, tf = self.v_ap("th"), self.v_ap("tl"), self.v_ap("tf")
                bth, btl, btf = self.v_buf("th"), self.v_buf("tl"), self.v_buf("tf")
                self.split(th, tl, self.ps_ap(bank), [self.psb[bank]], bth, btl)
                self.store(XNF_H.ap()[c, :, 1 + g * 512: 1 + g * 512 + 512], th,
                           bth, bH)
                self.store(XNF_L.ap()[c, :, 1 + g * 512: 1 + g * 512 + 512], tl,
                           btl, bL, q="S")
                if write_f32:
                    self.vop(lambda e, tf=tf, bank=bank: e.tensor_copy(tf, self.ps_ap(bank)),
                             [self.psb[bank]], [btf])
                    self.store(XNF_F.ap()[c, :, 1 + g * 512: 1 + g * 512 + 512], tf,
                               btf, bF)

    def stage_zero_pads(self):
        self.stage([("z", [128, D], F32)])
        z = self.v_ap("z")
        bz = self.v_buf("z")
        self.vop(lambda e: e.memset(z, 0.0), [], [bz])
        self.store(self.dram["XN"].ap()[0:1, :], z[0:1, :],
                   bz, self.dbuf("XN"))
        for name in ("XNF_H", "XNF_L", "XNF_F"):
            self.store(self.dram[name].ap()[:, :, 0:1].rearrange("c p o -> p (c o)"),
                       z[:, 0:NCH], bz, self.dbuf(name), q="G", slow=True)

    def stage_copy_out(self, src="R"):
        T = self.T
        self.stage([("t0", [128, D], F32), ("t1", [128, D], F32)])
        for tt in range(T // 128):
            v = f"t{tt % 2}"
            self.load(self.v_ap(v), self.dram[src].ap()[tt * 128:(tt + 1) * 128, :],
                      self.dbuf(src), self.v_buf(v))
            self.store(self.dram["out"].ap()[tt * 128:(tt + 1) * 128, :], self.v_ap(v),
                       self.v_buf(v), self.dbuf("out"))

    def finalize(self):
        nc = self.nc
        toks = self.sc.barrier_tokens()
        self.sc.op("G", lambda e: None, deps=toks)
        self._ctx_sems = []
        sems_cmp = {}
        sems_dma = {}
        with_list = []
        for name in ("T", "V", "S", "G", "Y"):
            g = nc.semaphore(f"sc_{name}")
            h = g.__enter__()
            self._ctx_sems.append(g)
            sems_cmp[name] = h
            lst = []
            for i in range(Sched.NSEM_DMA):
                gg = nc.semaphore(f"sd_{name}{i}")
                hh = gg.__enter__()
                self._ctx_sems.append(gg)
                lst.append(hh)
            sems_dma[name] = lst
        blk = nc.Block()
        block = blk.__enter__()
        self.sc.emit(block, sems_cmp, sems_dma)
        blk.__exit__(None, None, None)
        for g in reversed(self._ctx_sems):
            g.__exit__(None, None, None)
        self._psum_guard.__exit__(None, None, None)
        return nc

    def wcol_fm(self, wname, j0, w, vst, vhi, vlo, kcn=NCH, scale_col=None, scale_const=None):
        W = self.dram[wname]
        bW = self.dbuf(wname)
        nview = self.v_ap(vst).shape[1]
        st_ap = self.v_ap(vst).rearrange("p (k w) -> p k w", k=kcn)[:, :, 0:w]
        src = W.ap().rearrange("(k p) o -> p k o", p=128)[:, 0:kcn, j0:j0 + w]
        bst, bhi, blo = self.v_buf(vst), self.v_buf(vhi), self.v_buf(vlo)
        self.load(st_ap, src, bW, bst)
        extra_r = []
        if scale_col is not None or scale_const is not None:
            sap, sbuf = scale_col if scale_col is not None else (None, None)
            full_ap = self.v_ap(vst)
            def sc(e, full_ap=full_ap):
                last = None
                for kc in range(kcn):
                    sl = st_ap[:, kc, :]
                    if sap is not None:
                        last = e.tensor_scalar(sl, sl, sap[:, kc:kc + 1], None, ALU.mult)
                if scale_const is not None:
                    e.drain()
                    last = e.tensor_scalar(full_ap, full_ap, float(scale_const), None, ALU.mult)
                return last
            rr = [bst] + ([sbuf] if sbuf is not None else [])
            self.vop(sc, rr, [bst])
        self.split(self.v_ap(vhi), self.v_ap(vlo), self.v_ap(vst), [bst], bhi, blo)

    def wtile8(self, wname, kc0, c0, vst, vhi, vlo, q="Y"):
        W = self.dram[wname]
        st = self.v_ap(vst).rearrange("p (k w) -> p k w", k=4)
        src = W.ap().rearrange("(k p) o -> p k o", p=128)[:, kc0:kc0 + 4, c0:c0 + 512]
        self.load(st, src, self.dbuf(wname), self.v_buf(vst), q=q)
        self.split(self.v_ap(vhi), self.v_ap(vlo), self.v_ap(vst),
                   [self.v_buf(vst)], self.v_buf(vhi), self.v_buf(vlo))

    def wtile(self, wname, r0, rn, c0, cn, vst, vhi, vlo, q="Y"):
        W = self.dram[wname]
        st = self.v_ap(vst)[:, 0:cn]
        self.load(st[0:rn, :], W.ap()[r0:r0 + rn, c0:c0 + cn], self.dbuf(wname),
                  self.v_buf(vst), q=q)
        self.split(self.v_ap(vhi)[0:rn, 0:cn], self.v_ap(vlo)[0:rn, 0:cn], st[0:rn, :],
                   [self.v_buf(vst)], self.v_buf(vhi), self.v_buf(vlo))

    def mm3_pairs(self, vwh, vwl, vxh, vxl, kcn, msl=None, nsl=None):
        wh = self.v_ap(vwh).rearrange("p (k w) -> p k w", k=kcn)
        wl = self.v_ap(vwl).rearrange("p (k w) -> p k w", k=kcn)
        xh = self.v_ap(vxh).rearrange("p (k w) -> p k w", k=kcn)
        xl = self.v_ap(vxl).rearrange("p (k w) -> p k w", k=kcn)
        pairs = []
        for kc in range(kcn):
            a, al = wh[:, kc, :], wl[:, kc, :]
            b, bl = xh[:, kc, :], xl[:, kc, :]
            if msl is not None:
                a, al = a[:, msl], al[:, msl]
            if nsl is not None:
                b, bl = b[:, nsl], bl[:, nsl]
            pairs.append((a, al, b, bl))
        return pairs

    def rope_swap(self, bank, t1, b1, t2, b2, t3, b3, cos_ap, sin_ap):
        self.vop(lambda e, t1=t1, bank=bank: e.tensor_copy(t1, self.ps_ap(bank)),
                 [self.psb[bank]], [b1])
        self.store(t2[64:128, :], t1[0:64, :], b1, b2, q="G")
        self.store(t2[0:64, :], t1[64:128, :], b1, b2, q="G")
        def rp(e, t1=t1, t2=t2, t3=t3, cos_ap=cos_ap, sin_ap=sin_ap):
            e.tensor_tensor(t3, t1, cos_ap, ALU.mult)
            e.tensor_tensor(t2, t2, sin_ap, ALU.mult)
            e.drain()
            return e.tensor_tensor(t3, t3, t2, ALU.add)
        self.vop(rp, [b1, b2, self.v_buf("cosb"), self.v_buf("sinb")], [b2, b3])

    def stage_attn_a(self, resid_src="x", resid_dst="R"):
        T = self.T
        NB = T // 512
        self.stage([
            ("xnh", [128, NCH * 512], F32R), ("xnl", [128, NCH * 512], F32R),
            ("wst", [128, 2048], F32), ("wh", [128, 2048], F32R), ("wl", [128, 2048], F32R),
            ("vwst", [128, 4 * 512], F32), ("vwh", [128, 4 * 512], F32R), ("vwl", [128, 4 * 512], F32R),
            ("cosb", [128, 512], F32), ("sinb", [128, 512], F32),
            ("t1", [128, 512], F32), ("t2", [128, 512], F32), ("t3", [128, 512], F32),
            ("sh", [128, 512], F32R), ("sl", [128, 512], F32R),
        ])
        XNH, XNL = self.dram["XNF_H"], self.dram["XNF_L"]
        bXNH, bXNL = self.dbuf("XNF_H"), self.dbuf("XNF_L")
        for kb in range(NB):
            t0 = kb * 512
            xh_ap = self.v_ap("xnh").rearrange("p (k w) -> p k w", k=NCH)
            xl_ap = self.v_ap("xnl").rearrange("p (k w) -> p k w", k=NCH)
            self.load(xh_ap, XNH.ap().rearrange("c p t -> p c t")[:, :, 1 + t0:1 + t0 + 512],
                      bXNH, self.v_buf("xnh"))
            self.load(xl_ap, XNL.ap().rearrange("c p t -> p c t")[:, :, 1 + t0:1 + t0 + 512],
                      bXNL, self.v_buf("xnl"), q="S")
            self.load(self.v_ap("cosb"), self.c_cos.ap()[:, t0:t0 + 512], Buf(""), self.v_buf("cosb"))
            self.load(self.v_ap("sinb"), self.c_sin.ap()[:, t0:t0 + 512], Buf(""), self.v_buf("sinb"))
            for h in range(HEAD128):
                self.wcol_fm("wk_a", h * 128, 128, "wst", "wh", "wl")
                pairs = self.mm3_pairs("wh", "wl", "xnh", "xnl", NCH)
                self.mm3(self.ps_ap(0), pairs,
                         [self.v_buf("wh"), self.v_buf("wl"), self.v_buf("xnh"), self.v_buf("xnl")],
                         self.psb[0])
                t1, t2, t3 = self.v_ap("t1"), self.v_ap("t2"), self.v_ap("t3")
                b1, b2, b3 = self.v_buf("t1"), self.v_buf("t2"), self.v_buf("t3")
                cos_ap, sin_ap = self.v_ap("cosb"), self.v_ap("sinb")
                self.rope_swap(0, t1, b1, t2, b2, t3, b3, cos_ap, sin_ap)
                self.split(self.v_ap("sh"), self.v_ap("sl"), t3, [b3],
                           self.v_buf("sh"), self.v_buf("sl"))
                self.store(self.dram["KF_H"].ap()[h, :, t0:t0 + 512], self.v_ap("sh"),
                           self.v_buf("sh"), self.dbuf("KF_H"))
                self.store(self.dram["KF_L"].ap()[h, :, t0:t0 + 512], self.v_ap("sl"),
                           self.v_buf("sl"), self.dbuf("KF_L"), q="S")
            for of in range(4):
                for kh_ in range(4):
                    self.wtile8("wv_a", kh_ * 4, of * 512, "vwst", "vwh", "vwl")
                    vwh = self.v_ap("vwh").rearrange("p (k w) -> p k w", k=4)
                    vwl = self.v_ap("vwl").rearrange("p (k w) -> p k w", k=4)
                    for mt in range(4):
                        msl = slice(mt * 128, mt * 128 + 128)
                        pairs = [(xh_ap[:, kh_ * 4 + kc, msl], xl_ap[:, kh_ * 4 + kc, msl],
                                  vwh[:, kc, :], vwl[:, kc, :]) for kc in range(4)]
                        self.mm3(self.ps_ap(2 + mt), pairs,
                                 [self.v_buf("vwh"), self.v_buf("vwl"),
                                  self.v_buf("xnh"), self.v_buf("xnl")],
                                 self.psb[2 + mt], start=(kh_ == 0), stop=(kh_ == 3))
                for mt in range(4):
                    self.split(self.v_ap("sh"), self.v_ap("sl"), self.ps_ap(2 + mt),
                               [self.psb[2 + mt]], self.v_buf("sh"), self.v_buf("sl"))
                    r0 = t0 + mt * 128
                    self.store(self.dram["VT_H"].ap()[r0:r0 + 128, of * 512:of * 512 + 512],
                               self.v_ap("sh"), self.v_buf("sh"), self.dbuf("VT_H"))
                    self.store(self.dram["VT_L"].ap()[r0:r0 + 128, of * 512:of * 512 + 512],
                               self.v_ap("sl"), self.v_buf("sl"), self.dbuf("VT_L"), q="S")

        self.stage([
            ("xnh", [128, NCH * 512], F32R), ("xnl", [128, NCH * 512], F32R),
            ("wst", [128, 2048], F32), ("wh", [128, 2048], F32R), ("wl", [128, 2048], F32R),
            ("kh", [128, 2048], F32R), ("kl", [128, 2048], F32R),
            ("vh", [128, 16 * 128], F32R), ("vl", [128, 16 * 128], F32R),
            ("qh", [128, 512], F32R), ("ql", [128, 512], F32R),
            ("cosb", [128, 512], F32), ("sinb", [128, 512], F32),
            ("t1", [128, 512], F32), ("t2", [128, 512], F32), ("t3", [128, 512], F32),
            ("t4", [128, 512], F32), ("drec", [128, 512], F32),
            ("sh", [128, 512], F32R), ("sl", [128, 512], F32R),
            ("oh", [128, 512], F32R), ("ol", [128, 512], F32R),
        ])
        for qb in range(NB):
            q0 = qb * 512
            xh_ap = self.v_ap("xnh").rearrange("p (k w) -> p k w", k=NCH)
            xl_ap = self.v_ap("xnl").rearrange("p (k w) -> p k w", k=NCH)
            self.load(xh_ap, XNH.ap().rearrange("c p t -> p c t")[:, :, 1 + q0:1 + q0 + 512],
                      bXNH, self.v_buf("xnh"))
            self.load(xl_ap, XNL.ap().rearrange("c p t -> p c t")[:, :, 1 + q0:1 + q0 + 512],
                      bXNL, self.v_buf("xnl"), q="S")
            self.load(self.v_ap("cosb"), self.c_cos.ap()[:, q0:q0 + 512], Buf(""), self.v_buf("cosb"))
            self.load(self.v_ap("sinb"), self.c_sin.ap()[:, q0:q0 + 512], Buf(""), self.v_buf("sinb"))
            for h in range(HEAD128):
                self.wcol_fm("wq_a", h * 128, 128, "wst", "wh", "wl")
                pairs = self.mm3_pairs("wh", "wl", "xnh", "xnl", NCH)
                self.mm3(self.ps_ap(0), pairs,
                         [self.v_buf("wh"), self.v_buf("wl"), self.v_buf("xnh"), self.v_buf("xnl")],
                         self.psb[0])
                t1, t2, t3, t4 = (self.v_ap(x) for x in ("t1", "t2", "t3", "t4"))
                b1, b2, b3, b4 = (self.v_buf(x) for x in ("t1", "t2", "t3", "t4"))
                cos_ap, sin_ap = self.v_ap("cosb"), self.v_ap("sinb")
                self.rope_swap(0, t1, b1, t2, b2, t3, b3, cos_ap, sin_ap)
                self.split(self.v_ap("qh"), self.v_ap("ql"), t3, [b3],
                           self.v_buf("qh"), self.v_buf("ql"))
                first_av = True
                nkt = 4 * (qb + 1)
                self.load(self.v_ap("kh")[:, 0:512 * (qb + 1)],
                          self.dram["KF_H"].ap()[h, :, 0:512 * (qb + 1)],
                          self.dbuf("KF_H"), self.v_buf("kh"))
                self.load(self.v_ap("kl")[:, 0:512 * (qb + 1)],
                          self.dram["KF_L"].ap()[h, :, 0:512 * (qb + 1)],
                          self.dbuf("KF_L"), self.v_buf("kl"), q="S")
                vh_ap = self.v_ap("vh").rearrange("p (k w) -> p k w", k=16)
                vl_ap = self.v_ap("vl").rearrange("p (k w) -> p k w", k=16)
                self.load(vh_ap[:, 0:nkt, :], self.dram["VT_H"].ap()
                          .rearrange("(kt p) d -> p kt d", p=128)[:, 0:nkt, h * 128:h * 128 + 128],
                          self.dbuf("VT_H"), self.v_buf("vh"))
                self.load(vl_ap[:, 0:nkt, :], self.dram["VT_L"].ap()
                          .rearrange("(kt p) d -> p kt d", p=128)[:, 0:nkt, h * 128:h * 128 + 128],
                          self.dbuf("VT_L"), self.v_buf("vl"), q="S")
                for kb2 in range(qb + 1):
                    k0 = kb2 * 512
                    for kt in range(4):
                        gkt = 4 * kb2 + kt
                        sb_ = 4 + (kt % 2)
                        pairs = [(self.v_ap("kh")[:, gkt * 128:gkt * 128 + 128],
                                  self.v_ap("kl")[:, gkt * 128:gkt * 128 + 128],
                                  self.v_ap("qh"), self.v_ap("ql"))]
                        self.mm3(self.ps_ap(sb_), pairs,
                                 [self.v_buf("kh"), self.v_buf("kl"), self.v_buf("qh"), self.v_buf("ql")],
                                 self.psb[sb_])
                        glob_kt = 4 * kb2 + kt
                        sub = glob_kt - 4 * qb
                        self.sop(lambda e, sb_=sb_, t1=t1: e.activation(t1, self.ps_ap(sb_), AF.Square),
                                 [self.psb[sb_]], [b1])
                        if 0 <= sub < 4:
                            mask_ap = self.k_mm.ap()[:, sub * 512:sub * 512 + 512]
                            self.vop(lambda e, t1=t1, mask_ap=mask_ap: e.tensor_tensor(t1, t1, mask_ap, ALU.mult),
                                     [b1, self.sbbuf("k_mm")], [b1])
                        self.sop(lambda e, t1=t1, t2=t2: e.activation(t2, t1, AF.Square),
                                 [b1], [b2])
                        self.split(self.v_ap("sh"), self.v_ap("sl"), t2, [b2],
                                   self.v_buf("sh"), self.v_buf("sl"))
                        pairs = [(vh_ap[:, gkt, :], vl_ap[:, gkt, :],
                                  self.v_ap("sh"), self.v_ap("sl"))]
                        last_av = (kb2 == qb and kt == 3)
                        self.mm3(self.ps_ap(2), pairs,
                                 [self.v_buf("vh"), self.v_buf("vl"), self.v_buf("sh"), self.v_buf("sl")],
                                 self.psb[2], start=first_av, stop=last_av)
                        self.mm1(self.ps_ap(3), [(self.k_ones.ap(), self.v_ap("sh")),
                                                 (self.k_ones.ap(), self.v_ap("sl"))],
                                 [self.sbbuf("k_ones"), self.v_buf("sh"), self.v_buf("sl")],
                                 self.psb[3], start=first_av, stop=last_av)
                        first_av = False
                drec, bdr = self.v_ap("drec"), self.v_buf("drec")
                def dn(e, drec=drec):
                    e.tensor_scalar(drec, self.ps_ap(3), float(EPS_A), None, ALU.max)
                    e.drain()
                    return e.reciprocal(drec, drec)
                self.vop(dn, [self.psb[3]], [bdr])
                self.vop(lambda e, t4=t4, drec=drec: e.tensor_tensor(t4, self.ps_ap(2), drec, ALU.mult),
                         [self.psb[2], bdr], [b4])
                self.split(self.v_ap("oh"), self.v_ap("ol"), t4, [b4],
                           self.v_buf("oh"), self.v_buf("ol"))
                self.store(self.dram["AOF_H"].ap()[h, :, q0:q0 + 512], self.v_ap("oh"),
                           self.v_buf("oh"), self.dbuf("AOF_H"))
                self.store(self.dram["AOF_L"].ap()[h, :, q0:q0 + 512], self.v_ap("ol"),
                           self.v_buf("ol"), self.dbuf("AOF_L"), q="S")

        self.pass_out_proj("AOF_H", "AOF_L", "wo_a", resid_src, resid_dst)

    def pass_out_proj(self, aof_h, aof_l, wname, resid_src, resid_dst, extra_tm_add=None):
        T = self.T
        self.stage([
            ("aoh", [128, NCH * 512], F32R), ("aol", [128, NCH * 512], F32R),
            ("vwst", [128, 4 * 512], F32), ("vwh", [128, 4 * 512], F32R), ("vwl", [128, 4 * 512], F32R),
            ("rt", [128, 512], F32), ("rt2", [128, 512], F32),
        ])
        AH, AL = self.dram[aof_h], self.dram[aof_l]
        for sc in range(T // 512):
            s0 = sc * 512
            ah = self.v_ap("aoh").rearrange("p (k w) -> p k w", k=NCH)
            al = self.v_ap("aol").rearrange("p (k w) -> p k w", k=NCH)
            off = 1 if AH.shape[2] == self.T + 1 else 0
            self.load(ah, AH.ap().rearrange("c p t -> p c t")[:, :, off + s0: off + s0 + 512],
                      self.dbuf(aof_h), self.v_buf("aoh"))
            self.load(al, AL.ap().rearrange("c p t -> p c t")[:, :, off + s0: off + s0 + 512],
                      self.dbuf(aof_l), self.v_buf("aol"), q="S")
            for of in range(4):
                for kh_ in range(4):
                    self.wtile8(wname, kh_ * 4, of * 512, "vwst", "vwh", "vwl")
                    vwh = self.v_ap("vwh").rearrange("p (k w) -> p k w", k=4)
                    vwl = self.v_ap("vwl").rearrange("p (k w) -> p k w", k=4)
                    for mt in range(4):
                        msl = slice(mt * 128, mt * 128 + 128)
                        pairs = [(ah[:, kh_ * 4 + kc, msl], al[:, kh_ * 4 + kc, msl],
                                  vwh[:, kc, :], vwl[:, kc, :]) for kc in range(4)]
                        self.mm3(self.ps_ap(2 + mt), pairs,
                                 [self.v_buf("vwh"), self.v_buf("vwl"),
                                  self.v_buf("aoh"), self.v_buf("aol")],
                                 self.psb[2 + mt], start=(kh_ == 0), stop=(kh_ == 3))
                for mt in range(4):
                    r0 = s0 + mt * 128
                    rt, brt = self.v_ap("rt"), self.v_buf("rt")
                    self.load(rt, self.dram[resid_src].ap()[r0:r0 + 128, of * 512:of * 512 + 512],
                              self.dbuf(resid_src), brt)
                    rt2, brt2 = self.v_ap("rt2"), self.v_buf("rt2")
                    if extra_tm_add is not None:
                        exn, exoff = extra_tm_add
                        self.load(rt2, self.dram[exn].ap()[exoff + r0: exoff + r0 + 128,
                                                           of * 512:of * 512 + 512],
                                  self.dbuf(exn), brt2, q="S")
                        def ev(e, mt=mt, rt=rt, rt2=rt2):
                            e.tensor_tensor(rt, rt, rt2, ALU.add)
                            e.drain()
                            return e.tensor_tensor(rt, rt, self.ps_ap(2 + mt), ALU.add)
                        self.vop(ev, [self.psb[2 + mt], brt, brt2], [brt])
                    else:
                        self.vop(lambda e, mt=mt, rt=rt: e.tensor_tensor(rt, rt, self.ps_ap(2 + mt), ALU.add),
                                 [self.psb[2 + mt], brt], [brt])
                    self.store(self.dram[resid_dst].ap()[r0:r0 + 128, of * 512:of * 512 + 512],
                               rt, brt, self.dbuf(resid_dst))

    def stage_woa(self):
        self.k_m = self.sbuf_t("k_m", [128, 256], F32)
        self.k_mh = self.sbuf_t("k_mh", [128, 256], F32R)
        self.k_ml = self.sbuf_t("k_ml", [128, 256], F32R)
        self.stage([
            ("wst", [128, 128], F32), ("wh", [128, 128], F32R), ("wl", [128, 128], F32R),
            ("ast", [128, 16 * 16], F32), ("ah", [128, 256], F32R), ("al", [128, 256], F32R),
        ])
        a3 = self.dram["a_f"].ap().rearrange("(k p) r -> p k r", p=128)
        self.load(self.v_ap("ast").rearrange("p (k r) -> p k r", k=16), a3,
                  self.dbuf("a_f"), self.v_buf("ast"))
        self.split(self.v_ap("ah"), self.v_ap("al"), self.v_ap("ast"),
                   [self.v_buf("ast")], self.v_buf("ah"), self.v_buf("al"))
        ah = self.v_ap("ah").rearrange("p (k r) -> p k r", k=16)
        al = self.v_ap("al").rearrange("p (k r) -> p k r", k=16)
        for ic in range(16):
            for oc in range(16):
                self.wtile("wo_f_nat", oc * 128, 128, ic * 128, 128, "wst", "wh", "wl")
                pairs = [(self.v_ap("wh")[:, 0:128], self.v_ap("wl")[:, 0:128],
                          ah[:, oc, :], al[:, oc, :])]
                self.mm3(self.ps_ap(0, 16, ic * 16), pairs,
                         [self.v_buf("wh"), self.v_buf("wl"), self.v_buf("ah"), self.v_buf("al")],
                         self.psb[0], start=(oc == 0), stop=(oc == 15))
            self.vop(lambda e, ic=ic: e.tensor_copy(self.k_m.ap()[:, ic * 16:ic * 16 + 16],
                                                    self.ps_ap(0, 16, ic * 16)),
                     [self.psb[0]], [self.sbbuf("k_m")])
        self.split(self.k_mh.ap(), self.k_ml.ap(), self.k_m.ap(),
                   [self.sbbuf("k_m")], self.sbbuf("k_mh"), self.sbbuf("k_ml"))

    def stage_rms_base(self):
        T = self.T
        layout = [(f"x{tt}", [128, D], F32) for tt in range(4)]
        layout += [("sq", [128, D], F32), ("st", [128, 8], F32),
                   ("tf", [128, 512], F32),
                   ("th", [128, 512], F32R), ("tl", [128, 512], F32R),
                   ("w1c", [128, 16], F32), ("w2c", [128, 16], F32)]
        self.stage(layout)
        self._row_to_col16("ln1_row", self.v_ap("w1c"), self.v_buf("w1c"))
        self._row_to_col16("ln2_row", self.v_ap("w2c"), self.v_buf("w2c"))
        XN = self.dram["XN"]
        for g in range(T // 512):
            for tt in range(4):
                t0 = g * 512 + tt * 128
                vx = f"x{tt}"
                self.load(self.v_ap(vx), XN.ap()[1 + t0:1 + t0 + 128, :], self.dbuf("XN"),
                          self.v_buf(vx))
                self.ln_norm_tile(vx, "sq", "st", EPS_RMS, kind="rms")
            for c in range(NCH):
                bank = c % 2
                srcs = [self.v_ap(f"x{tt}")[:, c * 128:(c + 1) * 128] for tt in range(4)]
                self.transpose_group(bank, srcs, [self.v_buf(f"x{tt}") for tt in range(4)])
                tf, th, tl = self.v_ap("tf"), self.v_ap("th"), self.v_ap("tl")
                for (wv, dh, dl, qq) in (("w1c", "BF_H", "BF_L", "G"),
                                         ("w2c", "BF2_H", "BF2_L", "S")):
                    wap = self.v_ap(wv)
                    self.vop(lambda e, tf=tf, bank=bank, wap=wap, c=c: e.tensor_scalar(
                        tf, self.ps_ap(bank), wap[:, c:c + 1], None, ALU.mult),
                        [self.psb[bank], self.v_buf(wv)], [self.v_buf("tf")])
                    self.split(th, tl, tf, [self.v_buf("tf")],
                               self.v_buf("th"), self.v_buf("tl"))
                    self.store(self.dram[dh].ap()[c, :, g * 512: g * 512 + 512], th,
                               self.v_buf("th"), self.dbuf(dh), q=qq)
                    self.store(self.dram[dl].ap()[c, :, g * 512: g * 512 + 512], tl,
                               self.v_buf("tl"), self.dbuf(dl), q=qq)

    def _row_to_col16(self, row_name, dst_ap, dst_buf):
        src = self.dram[row_name].ap()[0:1, :].rearrange("o (c p) -> (o p) c", p=128)
        self.load(dst_ap, src, self.dbuf(row_name), dst_buf, slow=True)

    def stage_ffn_attn(self):
        T = self.T
        NB = T // 512
        BFH, BFL = self.dram["BF_H"], self.dram["BF_L"]
        bBFH, bBFL = self.dbuf("BF_H"), self.dbuf("BF_L")
        self.stage([
            ("xnh", [128, NCH * 512], F32R), ("xnl", [128, NCH * 512], F32R),
            ("wst", [128, 2048], F32), ("wh", [128, 2048], F32R), ("wl", [128, 2048], F32R),
            ("vwst", [128, 4 * 512], F32), ("vwh", [128, 4 * 512], F32R), ("vwl", [128, 4 * 512], F32R),
            ("t1", [128, 512], F32), ("t1r", [128, 512], F32R),
            ("sh", [128, 512], F32R), ("sl", [128, 512], F32R),
            ("kssb", [128, 1], F32),
        ])
        self.k_kss = getattr(self, "k_kss", None) or self.sbuf_t("k_kss", [128, 16], F32)
        self.vop(lambda e: e.memset(self.k_kss.ap(), 0.0), [], [self.sbbuf("k_kss")])
        for kb in range(NB):
            t0 = kb * 512
            xh_ap = self.v_ap("xnh").rearrange("p (k w) -> p k w", k=NCH)
            xl_ap = self.v_ap("xnl").rearrange("p (k w) -> p k w", k=NCH)
            self.load(xh_ap, BFH.ap().rearrange("c p t -> p c t")[:, :, t0:t0 + 512],
                      bBFH, self.v_buf("xnh"))
            self.load(xl_ap, BFL.ap().rearrange("c p t -> p c t")[:, :, t0:t0 + 512],
                      bBFL, self.v_buf("xnl"), q="S")
            for h in range(HEAD128):
                self.wcol_fm("wk_f", h * 128, 128, "wst", "wh", "wl")
                pairs = self.mm3_pairs("wh", "wl", "xnh", "xnl", NCH)
                self.mm3(self.ps_ap(0), pairs,
                         [self.v_buf("wh"), self.v_buf("wl"), self.v_buf("xnh"), self.v_buf("xnl")],
                         self.psb[0])
                t1, b1 = self.v_ap("t1"), self.v_buf("t1")
                t1r, b1r = self.v_ap("t1r"), self.v_buf("t1r")
                self.sop(lambda e, t1=t1: e.activation(t1, self.ps_ap(0), AF.Square),
                         [self.psb[0]], [b1])
                self.vop(lambda e, t1=t1, t1r=t1r: e.tensor_copy(t1r, t1), [b1], [b1r])
                self.mm1(self.ps_ap(1), [(self.k_ones.ap(), t1r)],
                         [self.sbbuf("k_ones"), b1r], self.psb[1])
                kssb, bkss = self.v_ap("kssb"), self.v_buf("kssb")
                self.vop(lambda e, kssb=kssb: e.tensor_reduce(kssb, self.ps_ap(1), AX.X, ALU.max),
                         [self.psb[1]], [bkss])
                kap = self.k_kss.ap()[:, h:h + 1]
                self.vop(lambda e, kssb=kssb, kap=kap: e.tensor_tensor(kap, kap, kssb, ALU.max),
                         [bkss, self.sbbuf("k_kss")], [self.sbbuf("k_kss")])
                self.split(self.v_ap("sh"), self.v_ap("sl"), self.ps_ap(0),
                           [self.psb[0]], self.v_buf("sh"), self.v_buf("sl"))
                self.store(self.dram["KF_H"].ap()[h, :, t0:t0 + 512], self.v_ap("sh"),
                           self.v_buf("sh"), self.dbuf("KF_H"))
                self.store(self.dram["KF_L"].ap()[h, :, t0:t0 + 512], self.v_ap("sl"),
                           self.v_buf("sl"), self.dbuf("KF_L"), q="S")
            for of in range(4):
                for kh_ in range(4):
                    self.wtile8("wv_f", kh_ * 4, of * 512, "vwst", "vwh", "vwl")
                    vwh = self.v_ap("vwh").rearrange("p (k w) -> p k w", k=4)
                    vwl = self.v_ap("vwl").rearrange("p (k w) -> p k w", k=4)
                    for mt in range(4):
                        msl = slice(mt * 128, mt * 128 + 128)
                        pairs = [(xh_ap[:, kh_ * 4 + kc, msl], xl_ap[:, kh_ * 4 + kc, msl],
                                  vwh[:, kc, :], vwl[:, kc, :]) for kc in range(4)]
                        self.mm3(self.ps_ap(2 + mt), pairs,
                                 [self.v_buf("vwh"), self.v_buf("vwl"),
                                  self.v_buf("xnh"), self.v_buf("xnl")],
                                 self.psb[2 + mt], start=(kh_ == 0), stop=(kh_ == 3))
                for mt in range(4):
                    self.split(self.v_ap("sh"), self.v_ap("sl"), self.ps_ap(2 + mt),
                               [self.psb[2 + mt]], self.v_buf("sh"), self.v_buf("sl"))
                    r0 = t0 + mt * 128
                    self.store(self.dram["VT_H"].ap()[r0:r0 + 128, of * 512:of * 512 + 512],
                               self.v_ap("sh"), self.v_buf("sh"), self.dbuf("VT_H"))
                    self.store(self.dram["VT_L"].ap()[r0:r0 + 128, of * 512:of * 512 + 512],
                               self.v_ap("sl"), self.v_buf("sl"), self.dbuf("VT_L"), q="S")
        self.stage([
            ("xnh", [128, NCH * 512], F32R), ("xnl", [128, NCH * 512], F32R),
            ("wst", [128, 2048], F32), ("wh", [128, 2048], F32R), ("wl", [128, 2048], F32R),
            ("kh", [128, 2048], F32R), ("kl", [128, 2048], F32R),
            ("vh", [128, 16 * 128], F32R), ("vl", [128, 16 * 128], F32R),
            ("qh", [128, 512], F32R), ("ql", [128, 512], F32R),
            ("t1", [128, 512], F32), ("t2", [128, 512], F32),
            ("t4", [128, 512], F32), ("drec", [128, 512], F32),
            ("sh", [128, 512], F32R), ("sl", [128, 512], F32R),
            ("oh", [128, 512], F32R), ("ol", [128, 512], F32R),
            ("t1r", [128, 512], F32R), ("bnd", [128, 8], F32),
        ])
        for qb in range(NB):
            q0 = qb * 512
            xh_ap = self.v_ap("xnh").rearrange("p (k w) -> p k w", k=NCH)
            xl_ap = self.v_ap("xnl").rearrange("p (k w) -> p k w", k=NCH)
            self.load(xh_ap, BFH.ap().rearrange("c p t -> p c t")[:, :, q0:q0 + 512],
                      bBFH, self.v_buf("xnh"))
            self.load(xl_ap, BFL.ap().rearrange("c p t -> p c t")[:, :, q0:q0 + 512],
                      bBFL, self.v_buf("xnl"), q="S")
            for h in range(HEAD128):
                self.wcol_fm("wq_f", h * 128, 128, "wst", "wh", "wl", scale_const=RSQ128)
                pairs = self.mm3_pairs("wh", "wl", "xnh", "xnl", NCH)
                self.mm3(self.ps_ap(0), pairs,
                         [self.v_buf("wh"), self.v_buf("wl"), self.v_buf("xnh"), self.v_buf("xnl")],
                         self.psb[0])
                t1, b1 = self.v_ap("t1"), self.v_buf("t1")
                t2, b2 = self.v_ap("t2"), self.v_buf("t2")
                t4, b4 = self.v_ap("t4"), self.v_buf("t4")
                t1r, b1r = self.v_ap("t1r"), self.v_buf("t1r")
                bnd, bbnd = self.v_ap("bnd"), self.v_buf("bnd")
                self.sop(lambda e, t1=t1: e.activation(t1, self.ps_ap(0), AF.Square),
                         [self.psb[0]], [b1])
                self.vop(lambda e, t1=t1, t1r=t1r: e.tensor_copy(t1r, t1), [b1], [b1r])
                self.mm1(self.ps_ap(1), [(self.k_ones.ap(), t1r)],
                         [self.sbbuf("k_ones"), b1r], self.psb[1])
                kap = self.k_kss.ap()[:, h:h + 1]
                def mkbound(e, bnd=bnd, kap=kap):
                    e.tensor_reduce(bnd[:, 0:1], self.ps_ap(1), AX.X, ALU.max)
                    e.drain()
                    e.tensor_tensor(bnd[:, 1:2], bnd[:, 0:1], kap, ALU.mult)
                    return None
                self.vop(mkbound, [self.psb[1], self.sbbuf("k_kss")], [bbnd])
                def mkbound2(e, bnd=bnd):
                    e.activation(bnd[:, 2:3], bnd[:, 1:2], AF.Ln, bias=1e-30)
                    e.drain()
                    e.activation(bnd[:, 3:4], bnd[:, 2:3], AF.Exp, scale=0.5)
                    e.drain()
                    return None
                self.sop(mkbound2, [bbnd], [bbnd])
                self.vop(lambda e, bnd=bnd: e.tensor_scalar(bnd[:, 4:5], bnd[:, 3:4],
                                                            -1.0001, -0.01, ALU.mult, ALU.add),
                         [bbnd], [bbnd])
                self.split(self.v_ap("qh"), self.v_ap("ql"), self.ps_ap(0),
                           [self.psb[0]], self.v_buf("qh"), self.v_buf("ql"))
                first_av = True
                nkt = 4 * (qb + 1)
                self.load(self.v_ap("kh")[:, 0:512 * (qb + 1)],
                          self.dram["KF_H"].ap()[h, :, 0:512 * (qb + 1)],
                          self.dbuf("KF_H"), self.v_buf("kh"))
                self.load(self.v_ap("kl")[:, 0:512 * (qb + 1)],
                          self.dram["KF_L"].ap()[h, :, 0:512 * (qb + 1)],
                          self.dbuf("KF_L"), self.v_buf("kl"), q="S")
                vh_ap = self.v_ap("vh").rearrange("p (k w) -> p k w", k=16)
                vl_ap = self.v_ap("vl").rearrange("p (k w) -> p k w", k=16)
                self.load(vh_ap[:, 0:nkt, :], self.dram["VT_H"].ap()
                          .rearrange("(kt p) d -> p kt d", p=128)[:, 0:nkt, h * 128:h * 128 + 128],
                          self.dbuf("VT_H"), self.v_buf("vh"))
                self.load(vl_ap[:, 0:nkt, :], self.dram["VT_L"].ap()
                          .rearrange("(kt p) d -> p kt d", p=128)[:, 0:nkt, h * 128:h * 128 + 128],
                          self.dbuf("VT_L"), self.v_buf("vl"), q="S")
                for kb2 in range(qb + 1):
                    k0 = kb2 * 512
                    for kt in range(4):
                        gkt = 4 * kb2 + kt
                        sb_ = 4 + (kt % 2)
                        pairs = [(self.v_ap("kh")[:, gkt * 128:gkt * 128 + 128],
                                  self.v_ap("kl")[:, gkt * 128:gkt * 128 + 128],
                                  self.v_ap("qh"), self.v_ap("ql"))]
                        self.mm3(self.ps_ap(sb_), pairs,
                                 [self.v_buf("kh"), self.v_buf("kl"), self.v_buf("qh"), self.v_buf("ql")],
                                 self.psb[sb_])
                        glob_kt = 4 * kb2 + kt
                        sub = glob_kt - 4 * qb
                        nb_ap = bnd[:, 4:5]
                        if 0 <= sub < 4:
                            mask_ap = self.k_am.ap()[:, sub * 512:sub * 512 + 512]
                            self.vop(lambda e, sb_=sb_, t1=t1, mask_ap=mask_ap:
                                     e.tensor_tensor(t1, self.ps_ap(sb_), mask_ap, ALU.add),
                                     [self.psb[sb_], self.sbbuf("k_am")], [b1])
                            self.sop(lambda e, t1=t1, t2=t2, nb_ap=nb_ap:
                                     e.activation(t2, t1, AF.Exp, bias=nb_ap),
                                     [b1, bbnd], [b2])
                        else:
                            self.sop(lambda e, sb_=sb_, t2=t2, nb_ap=nb_ap:
                                     e.activation(t2, self.ps_ap(sb_), AF.Exp, bias=nb_ap),
                                     [self.psb[sb_], bbnd], [b2])
                        self.split(self.v_ap("sh"), self.v_ap("sl"), t2, [b2],
                                   self.v_buf("sh"), self.v_buf("sl"))
                        last_av = (kb2 == qb and kt == 3)
                        pairs = [(vh_ap[:, gkt, :], vl_ap[:, gkt, :],
                                  self.v_ap("sh"), self.v_ap("sl"))]
                        self.mm3(self.ps_ap(2), pairs,
                                 [self.v_buf("vh"), self.v_buf("vl"), self.v_buf("sh"), self.v_buf("sl")],
                                 self.psb[2], start=first_av, stop=last_av)
                        self.mm1(self.ps_ap(3), [(self.k_ones.ap(), self.v_ap("sh")),
                                                 (self.k_ones.ap(), self.v_ap("sl"))],
                                 [self.sbbuf("k_ones"), self.v_buf("sh"), self.v_buf("sl")],
                                 self.psb[3], start=first_av, stop=last_av)
                        first_av = False
                drec, bdr = self.v_ap("drec"), self.v_buf("drec")
                def dn(e, drec=drec):
                    e.tensor_scalar(drec, self.ps_ap(3), float(EPS_A), None, ALU.max)
                    e.drain()
                    return e.reciprocal(drec, drec)
                self.vop(dn, [self.psb[3]], [bdr])
                self.vop(lambda e, t4=t4, drec=drec: e.tensor_tensor(t4, self.ps_ap(2), drec, ALU.mult),
                         [self.psb[2], bdr], [b4])
                self.split(self.v_ap("oh"), self.v_ap("ol"), t4, [b4],
                           self.v_buf("oh"), self.v_buf("ol"))
                self.store(self.dram["AOF_H"].ap()[h, :, q0:q0 + 512], self.v_ap("oh"),
                           self.v_buf("oh"), self.dbuf("AOF_H"))
                self.store(self.dram["AOF_L"].ap()[h, :, q0:q0 + 512], self.v_ap("ol"),
                           self.v_buf("ol"), self.dbuf("AOF_L"), q="S")

    def stage_modz(self):
        T = self.T
        self.stage([
            ("aoh", [128, NCH * 512], F32R), ("aol", [128, NCH * 512], F32R),
            ("bfh", [128, NCH * 512], F32R), ("bfl", [128, NCH * 512], F32R),
            ("ast", [128, 16 * 16], F32), ("ah", [128, 256], F32R), ("al", [128, 256], F32R),
            ("tz", [16, 512], F32), ("tm", [16, 512], F32),
            ("zh", [16, 512], F32R), ("zl", [16, 512], F32R),
        ])
        a3 = self.dram["a_f"].ap().rearrange("(k p) r -> p k r", p=128)
        ast = self.v_ap("ast").rearrange("p (k r) -> p k r", k=16)
        self.load(ast, a3, self.dbuf("a_f"), self.v_buf("ast"))
        self.split(self.v_ap("ah"), self.v_ap("al"), self.v_ap("ast"),
                   [self.v_buf("ast")], self.v_buf("ah"), self.v_buf("al"))
        ah = self.v_ap("ah").rearrange("p (k r) -> p k r", k=16)
        al = self.v_ap("al").rearrange("p (k r) -> p k r", k=16)
        mh = self.k_mh.ap().rearrange("p (k r) -> p k r", k=16)
        ml = self.k_ml.ap().rearrange("p (k r) -> p k r", k=16)
        AOH, AOL = self.dram["AOF_H"], self.dram["AOF_L"]
        for qb in range(T // 512):
            q0 = qb * 512
            aoh = self.v_ap("aoh").rearrange("p (k w) -> p k w", k=NCH)
            aol = self.v_ap("aol").rearrange("p (k w) -> p k w", k=NCH)
            bfh = self.v_ap("bfh").rearrange("p (k w) -> p k w", k=NCH)
            bfl = self.v_ap("bfl").rearrange("p (k w) -> p k w", k=NCH)
            self.load(aoh, AOH.ap().rearrange("c p t -> p c t")[:, :, q0:q0 + 512],
                      self.dbuf("AOF_H"), self.v_buf("aoh"))
            self.load(aol, AOL.ap().rearrange("c p t -> p c t")[:, :, q0:q0 + 512],
                      self.dbuf("AOF_L"), self.v_buf("aol"), q="S")
            self.load(bfh, self.dram["BF2_H"].ap().rearrange("c p t -> p c t")[:, :, q0:q0 + 512],
                      self.dbuf("BF2_H"), self.v_buf("bfh"))
            self.load(bfl, self.dram["BF2_L"].ap().rearrange("c p t -> p c t")[:, :, q0:q0 + 512],
                      self.dbuf("BF2_L"), self.v_buf("bfl"), q="S")
            pairs = [(mh[:, kc, :], ml[:, kc, :], aoh[:, kc, :], aol[:, kc, :])
                     for kc in range(NCH)]
            self.mm3(self.ps_ap_p(0, 0, 16, 512), pairs,
                     [self.sbbuf("k_mh"), self.sbbuf("k_ml"), self.v_buf("aoh"), self.v_buf("aol")],
                     self.psb[0])
            pairs = [(ah[:, kc, :], al[:, kc, :], bfh[:, kc, :], bfl[:, kc, :])
                     for kc in range(NCH)]
            self.mm3(self.ps_ap_p(1, 0, 16, 512), pairs,
                     [self.v_buf("ah"), self.v_buf("al"), self.v_buf("bfh"), self.v_buf("bfl")],
                     self.psb[1])
            tz, btz = self.v_ap("tz"), self.v_buf("tz")
            tm_, btm = self.v_ap("tm"), self.v_buf("tm")
            self.sop(lambda e, tm_=tm_: e.activation(tm_, self.ps_ap_p(0, 0, 16, 512), AF.Tanh),
                     [self.psb[0]], [btm])
            self.vop(lambda e, tz=tz, tm_=tm_: e.tensor_tensor(tz, tm_, self.ps_ap_p(1, 0, 16, 512), ALU.mult),
                     [btm, self.psb[1]], [btz])
            self.split(self.v_ap("zh"), self.v_ap("zl"), tz, [btz],
                       self.v_buf("zh"), self.v_buf("zl"))
            self.store(self.dram["ZF_H"].ap()[:, q0:q0 + 512], self.v_ap("zh"),
                       self.v_buf("zh"), self.dbuf("ZF_H"))
            self.store(self.dram["ZF_L"].ap()[:, q0:q0 + 512], self.v_ap("zl"),
                       self.v_buf("zl"), self.dbuf("ZF_L"), q="S")

    def stage_mlp(self, resid_src, resid_dst):
        T = self.T
        BF_OFF, GU_OFF, ACC_OFF = 0, 64 * 1024, 96 * 1024
        for blk in range(T // 512):
            t0 = blk * 512
            for q in range(4):
                f0 = q * 16
                self.stage([
                    ("bfh", [128, NCH * 512], F32R, BF_OFF),
                    ("bfl", [128, NCH * 512], F32R, BF_OFF + 32 * 1024),
                    ("gu", [128, 16 * 512], F32, GU_OFF),
                    ("w1st", [128, 1024], F32), ("w1h", [128, 1024], F32R), ("w1l", [128, 1024], F32R),
                    ("w2h", [128, 1024], F32R), ("w2l", [128, 1024], F32R),
                    ("bst", [16, 128], F32), ("bh", [16, 128], F32R), ("bl", [16, 128], F32R),
                    ("zh", [16, 512], F32R), ("zl", [16, 512], F32R),
                    ("gax", [128, 512], F32),
                ], keep=(("bfh", "bfl") if q > 0 else ()), auto_base=ACC_OFF + 32 * 1024)
                bfh = self.v_ap("bfh").rearrange("p (k w) -> p k w", k=NCH)
                bfl = self.v_ap("bfl").rearrange("p (k w) -> p k w", k=NCH)
                if q == 0:
                    self.load(bfh, self.dram["BF2_H"].ap().rearrange("c p t -> p c t")[:, :, t0:t0 + 512],
                              self.dbuf("BF2_H"), self.v_buf("bfh"))
                    self.load(bfl, self.dram["BF2_L"].ap().rearrange("c p t -> p c t")[:, :, t0:t0 + 512],
                              self.dbuf("BF2_L"), self.v_buf("bfl"), q="S")
                self.load(self.v_ap("zh"), self.dram["ZF_H"].ap()[:, t0:t0 + 512],
                          self.dbuf("ZF_H"), self.v_buf("zh"))
                self.load(self.v_ap("zl"), self.dram["ZF_L"].ap()[:, t0:t0 + 512],
                          self.dbuf("ZF_L"), self.v_buf("zl"), q="S")
                gu = self.v_ap("gu").rearrange("p (f w) -> p f w", f=16)
                bgu = self.v_buf("gu")
                for fi in range(16):
                    fc = f0 + fi
                    bank_g, bank_u = fi % 2, 2 + fi % 2
                    wbufs = [self.v_buf("w1h"), self.v_buf("w1l"),
                             self.v_buf("bfh"), self.v_buf("bfl")]
                    def hpairs(hname, lname, kc0):
                        wh = self.v_ap(hname).rearrange("p (k w) -> p k w", k=8)
                        wl = self.v_ap(lname).rearrange("p (k w) -> p k w", k=8)
                        return [(wh[:, kc, :], wl[:, kc, :],
                                 bfh[:, kc0 + kc, :], bfl[:, kc0 + kc, :]) for kc in range(8)]
                    self.wcol_fm("win", fc * 128, 128, "w1st", "w1h", "w1l", kcn=8)
                    self.mm3(self.ps_ap(bank_g), hpairs("w1h", "w1l", 0), wbufs,
                             self.psb[bank_g], start=True, stop=False)
                    self.wcol_fm2("win", fc * 128, 128, "w1st", "w1h", "w1l", kc0=8)
                    self.wtile_b("b_f", fc, "bst", "bh", "bl")
                    pairs = hpairs("w1h", "w1l", 8)
                    pairs.append((self.v_ap("bh")[0:16, 0:128], self.v_ap("bl")[0:16, 0:128],
                                  self.v_ap("zh"), self.v_ap("zl")))
                    self.mm3(self.ps_ap(bank_g), pairs,
                             wbufs + [self.v_buf("bh"), self.v_buf("bl"),
                                      self.v_buf("zh"), self.v_buf("zl")],
                             self.psb[bank_g], start=False, stop=True)
                    self.wcol_fm("wup", fc * 128, 128, "w1st", "w2h", "w2l", kcn=8)
                    self.mm3(self.ps_ap(bank_u),
                             [(self.v_ap("w2h").rearrange("p (k w) -> p k w", k=8)[:, kc, :],
                               self.v_ap("w2l").rearrange("p (k w) -> p k w", k=8)[:, kc, :],
                               bfh[:, kc, :], bfl[:, kc, :]) for kc in range(8)],
                             [self.v_buf("w2h"), self.v_buf("w2l"), self.v_buf("bfh"), self.v_buf("bfl")],
                             self.psb[bank_u], start=True, stop=False)
                    self.wcol_fm2("wup", fc * 128, 128, "w1st", "w2h", "w2l", kc0=8)
                    self.mm3(self.ps_ap(bank_u),
                             [(self.v_ap("w2h").rearrange("p (k w) -> p k w", k=8)[:, kc, :],
                               self.v_ap("w2l").rearrange("p (k w) -> p k w", k=8)[:, kc, :],
                               bfh[:, 8 + kc, :], bfl[:, 8 + kc, :]) for kc in range(8)],
                             [self.v_buf("w2h"), self.v_buf("w2l"), self.v_buf("bfh"), self.v_buf("bfl")],
                             self.psb[bank_u], start=False, stop=True)
                    gax, bgax = self.v_ap("gax"), self.v_buf("gax")
                    self.sop(lambda e, gax=gax, bank_g=bank_g: e.activation(gax, self.ps_ap(bank_g), AF.Silu),
                             [self.psb[bank_g]], [bgax])
                    self.vop(lambda e, fi=fi, gax=gax, bank_u=bank_u, gu=gu:
                             e.tensor_tensor(gu[:, fi, :], self.ps_ap(bank_u), gax, ALU.mult),
                             [self.psb[bank_u], bgax], [bgu])
                self.stage([
                    ("bfh", [128, NCH * 512], F32R, BF_OFF),
                    ("bfl", [128, NCH * 512], F32R, BF_OFF + 32 * 1024),
                    ("gu", [128, 16 * 512], F32, GU_OFF),
                    ("acc", [128, 4 * 2048], F32, ACC_OFF),
                    ("dwst", [128, 1024], F32), ("dwh", [128, 1024], F32R), ("dwl", [128, 1024], F32R),
                    ("guh", [128, 512], F32R), ("gul", [128, 512], F32R),
                    ("rt", [128, 512], F32), ("rt2", [128, 512], F32),
                ], keep=("gu", "bfh", "bfl") + (("acc",) if q > 0 else ()),
                    auto_base=ACC_OFF + 32 * 1024)
                gu = self.v_ap("gu").rearrange("p (f w) -> p f w", f=16)
                bgu = self.v_buf("gu")
                acc = self.v_ap("acc").rearrange("p (m w) -> p m w", m=4)
                bacc = self.v_buf("acc")
                for ofh in range(2):
                    o0 = ofh * 1024
                    dwh = self.v_ap("dwh").rearrange("p (n w) -> p n w", n=2)
                    dwl = self.v_ap("dwl").rearrange("p (n w) -> p n w", n=2)
                    for fi in range(16):
                        fc = f0 + fi
                        self.wtile("wdn", fc * 128, 128, o0, 1024, "dwst", "dwh", "dwl")
                        self.split(self.v_ap("guh"), self.v_ap("gul"), gu[:, fi, :],
                                   [bgu], self.v_buf("guh"), self.v_buf("gul"))
                        for mt in range(4):
                            for nf in range(2):
                                bank = mt * 2 + nf
                                pairs = [(self.v_ap("guh")[:, mt * 128:mt * 128 + 128],
                                          self.v_ap("gul")[:, mt * 128:mt * 128 + 128],
                                          dwh[:, nf, :], dwl[:, nf, :])]
                                self.mm3(self.ps_ap(bank), pairs,
                                         [self.v_buf("guh"), self.v_buf("gul"),
                                          self.v_buf("dwh"), self.v_buf("dwl")],
                                         self.psb[bank], start=(fi == 0), stop=(fi == 15))
                    for mt in range(4):
                        for nf in range(2):
                            bank = mt * 2 + nf
                            of0 = o0 + nf * 512
                            if q == 0:
                                self.vop(lambda e, mt=mt, of0=of0, bank=bank, acc=acc:
                                         e.tensor_copy(acc[:, mt, of0:of0 + 512], self.ps_ap(bank)),
                                         [self.psb[bank]], [bacc])
                            elif q < 3:
                                self.vop(lambda e, mt=mt, of0=of0, bank=bank, acc=acc:
                                         e.tensor_tensor(acc[:, mt, of0:of0 + 512],
                                                         acc[:, mt, of0:of0 + 512],
                                                         self.ps_ap(bank), ALU.add),
                                         [self.psb[bank], bacc], [bacc])
                            else:
                                r0 = t0 + mt * 128
                                rt, brt = self.v_ap("rt"), self.v_buf("rt")
                                rt2, brt2 = self.v_ap("rt2"), self.v_buf("rt2")
                                self.load(rt, self.dram[resid_src].ap()[r0:r0 + 128, of0:of0 + 512],
                                          self.dbuf(resid_src), brt)
                                self.load(rt2, self.dram["XN"].ap()[1 + r0:1 + r0 + 128, of0:of0 + 512],
                                          self.dbuf("XN"), brt2, q="S")
                                def ev(e, bank=bank, rt=rt, rt2=rt2, mt=mt, of0=of0, acc=acc):
                                    e.tensor_tensor(rt, rt, rt2, ALU.add)
                                    e.drain()
                                    e.tensor_tensor(rt, rt, acc[:, mt, of0:of0 + 512], ALU.add)
                                    e.drain()
                                    return e.tensor_tensor(rt, rt, self.ps_ap(bank), ALU.add)
                                self.vop(ev, [self.psb[bank], brt, brt2, bacc], [brt])
                                self.store(self.dram[resid_dst].ap()[r0:r0 + 128, of0:of0 + 512],
                                           rt, brt, self.dbuf(resid_dst))

    def wcol_fm2(self, wname, j0, w, vst, vhi, vlo, kc0, scale_col=None):
        W = self.dram[wname]
        st_ap = self.v_ap(vst).rearrange("p (k w) -> p k w", k=8)
        src = W.ap().rearrange("(k p) o -> p k o", p=128)[:, kc0:kc0 + 8, j0:j0 + w]
        self.load(st_ap, src, self.dbuf(wname), self.v_buf(vst))
        if scale_col is not None:
            sap, sbuf = scale_col
            def sc(e):
                last = None
                for kc in range(8):
                    last = e.tensor_scalar(st_ap[:, kc, :], st_ap[:, kc, :],
                                           sap[:, kc:kc + 1], None, ALU.mult)
                return last
            self.vop(sc, [self.v_buf(vst), sbuf], [self.v_buf(vst)])
        self.split(self.v_ap(vhi), self.v_ap(vlo), self.v_ap(vst),
                   [self.v_buf(vst)], self.v_buf(vhi), self.v_buf(vlo))

    def wtile_b(self, wname, fc, vst, vhi, vlo):
        W = self.dram[wname]
        st = self.v_ap(vst)
        self.load(st[0:16, :], W.ap()[0:16, fc * 128:fc * 128 + 128], self.dbuf(wname),
                  self.v_buf(vst))
        self.vop(lambda e, st=st: e.tensor_scalar(st[0:16, :], st[0:16, :], 0.1, None, ALU.mult),
                 [self.v_buf(vst)], [self.v_buf(vst)])
        self.split(self.v_ap(vhi)[0:16, :], self.v_ap(vlo)[0:16, :], st[0:16, :],
                   [self.v_buf(vst)], self.v_buf(vhi), self.v_buf(vlo))

    def stage_ffn(self, resid_src, resid_dst):
        self.stage_rms_base()
        self.stage_ffn_attn()
        self.stage_modz()
        self.stage_mlp(resid_src, resid_dst)

    def stage_rwkv(self, resid_src="R", resid_dst="R"):
        self.sc.t_drain = True
        T = self.T
        NCHK = T // 128
        dbg = "DBG1" in self.debug_taps
        if dbg:
            for i in (1, 2, 3, 4):
                self.dram_t(f"DBG{i}", [T, D])
        self.k_S = getattr(self, "k_S", None) or self.sbuf_t("k_S", [128, 1024], F32)
        self.k_Sh = getattr(self, "k_Sh", None) or self.sbuf_t("k_Sh", [128, 1024], F32R)
        self.k_Sl = getattr(self, "k_Sl", None) or self.sbuf_t("k_Sl", [128, 1024], F32R)
        bS, bSh, bSl = self.sbbuf("k_S"), self.sbbuf("k_Sh"), self.sbbuf("k_Sl")
        self.vop(lambda e: e.memset(self.k_S.ap(), 0.0), [], [bS])

        for g in range(T // 256):
            t0 = g * 256
            self.stage([
                ("xf", [128, NCH * 257], F32),
                ("xx", [128, NCH * 256], F32),
                ("xs", [128, NCH * 256], F32),
                ("xsh", [128, NCH * 256], F32R), ("xsl", [128, NCH * 256], F32R),
                ("mc", [128, 96], F32),
                ("w1st", [128, 16 * 160], F32), ("w1h", [128, 16 * 160], F32R), ("w1l", [128, 16 * 160], F32R),
                ("aa", [128, 512], F32), ("aah", [128, 512], F32R), ("aal", [128, 512], F32R),
                ("wb2st", [128, 2048], F32), ("wb2h", [128, 2048], F32R), ("wb2l", [128, 2048], F32R),
                ("vwst", [128, 4 * 512], F32), ("vwh", [128, 4 * 512], F32R), ("vwl", [128, 4 * 512], F32R),
                ("d1", [64, 256], F32), ("d1h", [64, 256], F32R), ("d1l", [64, 256], F32R),
                                ("dkbs", [128, 512], F32),
                ("t1", [128, 512], F32),
            ])
            bXF = self.dbuf("XNF_F")
            xf = self.v_ap("xf").rearrange("p (c w) -> p c w", c=NCH)
            self.load(xf, self.dram["XNF_F"].ap().rearrange("c p t -> p c t")[:, :, t0:t0 + 257],
                      bXF, self.v_buf("xf"))
            self.load(self.v_ap("mc"), self.dram["maa_cols"].ap(), self.dbuf("maa_cols"),
                      self.v_buf("mc"))
            xn = xf[:, :, 1:257]
            xsh_v = xf[:, :, 0:256]
            xx = self.v_ap("xx").rearrange("p (c w) -> p c w", c=NCH)
            self.vop(lambda e, xx=xx, xn=xn, xsh_v=xsh_v: e.tensor_tensor(xx, xsh_v, xn, ALU.subtract),
                     [self.v_buf("xf")], [self.v_buf("xx")])
            xs = self.v_ap("xs").rearrange("p (c w) -> p c w", c=NCH)
            mc = self.v_ap("mc")
            def lerp1(e, mcol, xs=xs, xx=xx, mc=mc):
                last = None
                for c in range(NCH):
                    last = e.tensor_scalar(xs[:, c, :], xx[:, c, :],
                                           mc[:, mcol * 16 + c:mcol * 16 + c + 1], None, ALU.mult)
                return last
            self.vop(lambda e: lerp1(e, 0), [self.v_buf("xx"), self.v_buf("mc")], [self.v_buf("xs")])
            self.vop(lambda e, xs=xs, xn=xn: e.tensor_tensor(xs, xs, xn, ALU.add),
                     [self.v_buf("xf"), self.v_buf("xs")], [self.v_buf("xs")])
            self.split(self.v_ap("xsh"), self.v_ap("xsl"), self.v_ap("xs"),
                       [self.v_buf("xs")], self.v_buf("xsh"), self.v_buf("xsl"))
            self.wcol_fm("maa_w1", 0, 160, "w1st", "w1h", "w1l", kcn=NCH)
            w1h = self.v_ap("w1h").rearrange("p (k w) -> p k w", k=NCH)
            w1l = self.v_ap("w1l").rearrange("p (k w) -> p k w", k=NCH)
            xshr = self.v_ap("xsh").rearrange("p (c w) -> p c w", c=NCH)
            xslr = self.v_ap("xsl").rearrange("p (c w) -> p c w", c=NCH)
            pairsA = [(w1h[:, kc, 0:128], w1l[:, kc, 0:128], xshr[:, kc, :], xslr[:, kc, :])
                      for kc in range(NCH)]
            pairsB = [(w1h[:, kc, 128:160], w1l[:, kc, 128:160], xshr[:, kc, :], xslr[:, kc, :])
                      for kc in range(NCH)]
            wb = [self.v_buf("w1h"), self.v_buf("w1l"), self.v_buf("xsh"), self.v_buf("xsl")]
            self.mm3(self.ps_ap(0, 256), pairsA, wb, self.psb[0])
            self.mm3(self.ps_ap_p(1, 0, 32, 256), pairsB, wb, self.psb[1])
            aa, baa = self.v_ap("aa"), self.v_buf("aa")
            self.sop(lambda e, aa=aa: e.activation(aa[:, 0:256], self.ps_ap(0, 256), AF.Tanh),
                     [self.psb[0]], [baa])
            self.sop(lambda e, aa=aa: e.activation(aa[0:32, 256:512], self.ps_ap_p(1, 0, 32, 256), AF.Tanh),
                     [self.psb[1]], [baa])
            self.split(self.v_ap("aah"), self.v_ap("aal"), aa, [baa],
                       self.v_buf("aah"), self.v_buf("aal"))
            XSTAR = ("xs",)
            for f in range(5):
                pb = f * 32 if f < 4 else 0
                tpf = (96, 0) if f == 3 else None
                w2s = self.v_ap("wb2st")[pb:pb + 32, :]
                self.load(w2s, self.dram["maa_w2"].ap()[f, :, :], self.dbuf("maa_w2"),
                          self.v_buf("wb2st"))
                self.split(self.v_ap("wb2h")[pb:pb + 32, :], self.v_ap("wb2l")[pb:pb + 32, :],
                           w2s, [self.v_buf("wb2st")],
                           self.v_buf("wb2h"), self.v_buf("wb2l"))
                if f < 4:
                    ash = self.v_ap("aah")[f * 32:f * 32 + 32, 0:256]
                    asl = self.v_ap("aal")[f * 32:f * 32 + 32, 0:256]
                else:
                    ash = self.v_ap("aah")[0:32, 256:512]
                    asl = self.v_ap("aal")[0:32, 256:512]
                for mcch in range(NCH):
                    bank = 2 + (mcch % 2)
                    pairs = [(self.v_ap("wb2h")[pb:pb + 32, mcch * 128:mcch * 128 + 128],
                              self.v_ap("wb2l")[pb:pb + 32, mcch * 128:mcch * 128 + 128], ash, asl)]
                    self.mm3(self.ps_ap(bank, 256), pairs,
                             [self.v_buf("wb2h"), self.v_buf("wb2l"), baa,
                              self.v_buf("aah"), self.v_buf("aal")],
                             self.psb[bank], tp=tpf)
                    t1, b1 = self.v_ap("t1"), self.v_buf("t1")
                    def fuse(e, bank=bank, f=f, mcch=mcch, t1=t1, xs=xs, xx=xx, xn=xn, mc=mc):
                        e.tensor_scalar(t1[:, 0:256], self.ps_ap(bank, 256),
                                        mc[:, (f + 1) * 16 + mcch:(f + 1) * 16 + mcch + 1],
                                        None, ALU.add)
                        e.drain()
                        e.tensor_tensor(t1[:, 0:256], xx[:, mcch, :], t1[:, 0:256], ALU.mult)
                        e.drain()
                        return e.tensor_tensor(xs[:, mcch, :], xn[:, mcch, :], t1[:, 0:256], ALU.add)
                    self.vop(fuse, [self.psb[bank], self.v_buf("mc"), self.v_buf("xx"),
                                    self.v_buf("xf")], [b1, self.v_buf("xs")])
                self.split(self.v_ap("xsh"), self.v_ap("xsl"), self.v_ap("xs"),
                           [self.v_buf("xs")], self.v_buf("xsh"), self.v_buf("xsl"))
                if f == 0:
                    self.wcol_fm("dec_w1", 0, 64, "w1st", "w1h", "w1l", kcn=NCH)
                    w1h2 = self.v_ap("w1h").rearrange("p (k w) -> p k w", k=NCH)
                    w1l2 = self.v_ap("w1l").rearrange("p (k w) -> p k w", k=NCH)
                    pairs = [(w1h2[:, kc, 0:64], w1l2[:, kc, 0:64], xshr[:, kc, :], xslr[:, kc, :])
                             for kc in range(NCH)]
                    self.mm3(self.ps_ap_p(4, 0, 64, 256), pairs, wb, self.psb[4])
                    d1, bd1 = self.v_ap("d1"), self.v_buf("d1")
                    self.sop(lambda e, d1=d1: e.activation(d1, self.ps_ap_p(4, 0, 64, 256), AF.Tanh),
                             [self.psb[4]], [bd1])
                    self.split(self.v_ap("d1h"), self.v_ap("d1l"), d1, [bd1],
                               self.v_buf("d1h"), self.v_buf("d1l"))
                    self.load(self.v_ap("wb2st")[0:64, :], self.dram["dec_w2"].ap(), self.dbuf("dec_w2"),
                              self.v_buf("wb2st"))
                    self.split(self.v_ap("wb2h")[0:64, :], self.v_ap("wb2l")[0:64, :],
                               self.v_ap("wb2st")[0:64, :],
                               [self.v_buf("wb2st")], self.v_buf("wb2h"), self.v_buf("wb2l"))
                    for mt in range(2):
                        for nf in range(4):
                            bank = 4 + (mt * 4 + nf) % 2
                            pairs = [(self.v_ap("d1h")[:, mt * 128:mt * 128 + 128],
                                      self.v_ap("d1l")[:, mt * 128:mt * 128 + 128],
                                      self.v_ap("wb2h")[0:64, nf * 512:nf * 512 + 512],
                                      self.v_ap("wb2l")[0:64, nf * 512:nf * 512 + 512])]
                            self.mm3(self.ps_ap(bank), pairs,
                                     [self.v_buf("d1h"), self.v_buf("d1l"),
                                      self.v_buf("wb2h"), self.v_buf("wb2l")],
                                     self.psb[bank])
                            t1, b1 = self.v_ap("t1"), self.v_buf("t1")
                            dkbs, bdkbs = self.v_ap("dkbs"), self.v_buf("dkbs")
                            self.load(dkbs, self.dram["decay_row"].ap()
                                      [0:1, nf * 512:nf * 512 + 512].broadcast_to([128, 512]),
                                      self.dbuf("decay_row"), bdkbs)
                            self.vop(lambda e, bank=bank, t1=t1, dkbs=dkbs:
                                     e.tensor_tensor(t1, self.ps_ap(bank), dkbs, ALU.add),
                                     [self.psb[bank], bdkbs], [b1])
                            r0 = t0 + mt * 128
                            self.store(self.dram["RW_W"].ap()[r0:r0 + 128, nf * 512:nf * 512 + 512],
                                       t1, b1, self.dbuf("RW_W"))
                elif f in (1, 2):
                    wname = "wk_r" if f == 1 else "wv_r"
                    dstn = "RW_K" if f == 1 else "RW_V"
                    for of in range(4):
                        for kh_ in range(4):
                            self.wtile8(wname, kh_ * 4, of * 512, "vwst", "vwh", "vwl")
                            vwh = self.v_ap("vwh").rearrange("p (k w) -> p k w", k=4)
                            vwl = self.v_ap("vwl").rearrange("p (k w) -> p k w", k=4)
                            for mt in range(2):
                                msl = slice(mt * 128, mt * 128 + 128)
                                pairs = [(xshr[:, kh_ * 4 + kc, msl], xslr[:, kh_ * 4 + kc, msl],
                                          vwh[:, kc, :], vwl[:, kc, :]) for kc in range(4)]
                                self.mm3(self.ps_ap(4 + mt), pairs,
                                         [self.v_buf("vwh"), self.v_buf("vwl"),
                                          self.v_buf("xsh"), self.v_buf("xsl")],
                                         self.psb[4 + mt], start=(kh_ == 0), stop=(kh_ == 3))
                        for mt in range(2):
                            t1, b1 = self.v_ap("t1"), self.v_buf("t1")
                            self.vop(lambda e, mt=mt, t1=t1: e.tensor_copy(t1, self.ps_ap(4 + mt)),
                                     [self.psb[4 + mt]], [b1])
                            r0 = t0 + mt * 128
                            self.store(self.dram[dstn].ap()[r0:r0 + 128, of * 512:of * 512 + 512],
                                       t1, b1, self.dbuf(dstn))
                else:
                    wname = "wr_r" if f == 3 else "wg_r"
                    dstn = "RW_R" if f == 3 else "RW_G"
                    for h in range(NCH):
                        self.wcol_fm(wname, h * 128, 128, "w1st", "w1h", "w1l", kcn=NCH)
                        w1h3 = self.v_ap("w1h").rearrange("p (k w) -> p k w", k=NCH)
                        w1l3 = self.v_ap("w1l").rearrange("p (k w) -> p k w", k=NCH)
                        pairs = [(w1h3[:, kc, 0:128], w1l3[:, kc, 0:128],
                                  xshr[:, kc, :], xslr[:, kc, :]) for kc in range(NCH)]
                        bank = 4 + h % 2
                        self.mm3(self.ps_ap(bank, 256), pairs, wb, self.psb[bank])
                        t1, b1 = self.v_ap("t1"), self.v_buf("t1")
                        if f == 4:
                            self.sop(lambda e, bank=bank, t1=t1: e.activation(t1[:, 0:256],
                                                                              self.ps_ap(bank, 256), AF.Silu),
                                     [self.psb[bank]], [b1])
                        else:
                            self.vop(lambda e, bank=bank, t1=t1: e.tensor_copy(t1[:, 0:256],
                                                                               self.ps_ap(bank, 256)),
                                     [self.psb[bank]], [b1])
                        self.store(self.dram[dstn].ap()[h, :, t0:t0 + 256], t1[:, 0:256],
                                   b1, self.dbuf(dstn))

        H64 = 32
        for cc in range(NCHK):
            c0 = cc * 128
            self.stage(
                [(f"F{i}", [128, 2048], F32) for i in range(10)] +
                [(f"R{i}", [128, 2048], F32R) for i in range(8)] +
                [("sm", [128, 128], F32), ("smh", [128, 128], F32R), ("sml", [128, 128], F32R),
                 ("dv", [128, 32], F32), ("gst", [128, 128], F32),
                 ("pcol", [128, 16], F32), ("pcol2", [128, 16], F32),
                 ("ucol", [128, 16], F32), ("lng", [128, 32], F32)])
            A = self.v_ap
            B = self.v_buf
            self.load(A("F0").rearrange("p (c w) -> p c w", c=NCH),
                      self.dram["RW_R"].ap().rearrange("c p t -> p c t")[:, :, c0:c0 + 128],
                      self.dbuf("RW_R"), B("F0"))
            self.load(A("F1"), self.dram["RW_K"].ap()[c0:c0 + 128, :], self.dbuf("RW_K"), B("F1"))
            self.load(A("F2"), self.dram["RW_V"].ap()[c0:c0 + 128, :], self.dbuf("RW_V"), B("F2"), q="S")
            self.load(A("F3"), self.dram["RW_W"].ap()[c0:c0 + 128, :], self.dbuf("RW_W"), B("F3"), q="S")
            self.load(A("ucol"), self.dram["u_col"].ap(), self.dbuf("u_col"), B("ucol"))
            self.load(A("lng"), self.dram["lnr_cols"].ap(), self.dbuf("lnr_cols"), B("lng"))
            self.sop(lambda e, a=A("F3"): e.activation(a, a, AF.Exp), [B("F3")], [B("F3")])
            self.split(A("R0"), A("R1"), A("F3"), [B("F3")], B("R0"), B("R1"))
            for nf in range(4):
                pairs = [(self.k_tri.ap(), A("R0")[:, nf * 512:nf * 512 + 512]),
                         (self.k_tri.ap(), A("R1")[:, nf * 512:nf * 512 + 512])]
                self.mm1(self.ps_ap(nf), pairs,
                         [self.sbbuf("k_tri"), B("R0"), B("R1")], self.psb[nf])
            for nf in range(4):
                pairs = [(self.k_ones.ap(), A("R0")[:, nf * 512:nf * 512 + 512]),
                         (self.k_ones.ap(), A("R1")[:, nf * 512:nf * 512 + 512])]
                self.mm1(self.ps_ap(4 + nf), pairs,
                         [self.sbbuf("k_ones"), B("R0"), B("R1")], self.psb[4 + nf])
            def totcp(e, t=A("F4")):
                last = None
                for nf in range(4):
                    last = e.tensor_copy(t[:, nf * 512:nf * 512 + 512], self.ps_ap(4 + nf))
                return last
            self.vop(totcp, [self.psb[4], self.psb[5], self.psb[6], self.psb[7]], [B("F4")])
            self.sop(lambda e, a=A("F5"), t=A("F4"): e.activation(a, t, AF.Exp, scale=-1.0),
                     [B("F4")], [B("F5")])
            self.store(self.dram["ROWTMP"].ap()[1:2, :], A("F5")[0:1, :], B("F5"), self.dbuf("ROWTMP"))
            self.load(A("pcol"), self.dram["ROWTMP"].ap()[1:2, :].rearrange("o (c p) -> (o p) c", p=128),
                      self.dbuf("ROWTMP"), B("pcol"), slow=True)
            self.vop(lambda e, t=A("F5"), s=A("F4"): e.tensor_scalar(t, s, 0.5, None, ALU.mult),
                     [B("F4")], [B("F5")])
            def aarg(e, a=A("F6"), toth=A("F5")):
                last = None
                for nf in range(4):
                    last = e.tensor_tensor(a[:, nf * 512:nf * 512 + 512],
                                           toth[:, nf * 512:nf * 512 + 512],
                                           self.ps_ap(nf), ALU.subtract)
                return last
            self.vop(aarg, [self.psb[0], self.psb[1], self.psb[2], self.psb[3], B("F5")], [B("F6")])
            self.sop(lambda e, a=A("F6"): e.activation(a, a, AF.Exp), [B("F6")], [B("F6")])
            def iarg(e, a=A("F7"), wc=A("F3"), toth=A("F5")):
                for nf in range(4):
                    sl = slice(nf * 512, nf * 512 + 512)
                    e.tensor_tensor(a[:, sl], self.ps_ap(nf), wc[:, sl], ALU.add)
                e.drain()
                return e.tensor_tensor(a, a, toth, ALU.subtract)
            self.vop(iarg, [self.psb[0], self.psb[1], self.psb[2], self.psb[3], B("F3"), B("F5")], [B("F7")])
            self.sop(lambda e, a=A("F7"): e.activation(a, a, AF.Exp), [B("F7")], [B("F7")])
            if dbg:
                self.store(self.dram["DBG1"].ap()[c0:c0 + 128, :], A("F6"), B("F6"), self.dbuf("DBG1"))
                self.store(self.dram["DBG2"].ap()[c0:c0 + 128, :], A("F7"), B("F7"), self.dbuf("DBG2"))
            self.sop(lambda e, a=A("F9"), t=A("F5"): e.activation(a, t, AF.Exp, scale=-1.0),
                     [B("F5")], [B("F9")])
            self.store(self.dram["ROWTMP"].ap()[2:3, :], A("F9")[0:1, :], B("F9"), self.dbuf("ROWTMP"))
            self.load(A("pcol2"), self.dram["ROWTMP"].ap()[2:3, :].rearrange("o (c p) -> (o p) c", p=128),
                      self.dbuf("ROWTMP"), B("pcol2"), slow=True)
            self.vop(lambda e, o=A("F8"), k=A("F1"), iv=A("F7"): e.tensor_tensor(o, k, iv, ALU.mult),
                     [B("F1"), B("F7")], [B("F8")])
            self.vop(lambda e, o=A("F9"), kh=A("F8"): e.tensor_tensor(o, kh, o, ALU.mult),
                     [B("F8"), B("F9")], [B("F9")])
            for (srcv, dstv) in [("F6", "F7"), ("F8", "F8"), ("F1", "F3")]:
                sap, dap = A(srcv), A(dstv)
                sb_, db_ = B(srcv), B(dstv)
                for grp in range(4):
                    bank = 4 + grp % 2
                    srcs = [sap[:, (grp * 4 + i) * 128:(grp * 4 + i) * 128 + 128] for i in range(4)]
                    self.transpose_group(bank, srcs, [sb_])
                    self.vop(lambda e, dap=dap, grp=grp, bank=bank:
                             e.tensor_copy(dap[:, grp * 512:grp * 512 + 512], self.ps_ap(bank)),
                             [self.psb[bank]], [db_])
            def mkrku(e, o=A("F1"), r=A("F0"), kfm=A("F3"), ucol=A("ucol")):
                e.tensor_tensor(o, r, kfm, ALU.mult)
                e.drain()
                last = None
                for c in range(NCH):
                    last = e.tensor_scalar(o[:, c * 128:c * 128 + 128], o[:, c * 128:c * 128 + 128],
                                           ucol[:, c:c + 1], None, ALU.mult)
                return last
            self.vop(mkrku, [B("F0"), B("F3"), B("ucol")], [B("F1")])
            self.split(A("R2"), A("R3"), A("F1"), [B("F1")], B("R2"), B("R3"))
            for h in range(H64):
                par = (h % 2) * 64
                hp = h // 2
                pairs = [(A("R2")[par:par + 64, hp * 128:hp * 128 + 128],
                          self.k_ones.ap()[par:par + 64, 0:2]),
                         (A("R3")[par:par + 64, hp * 128:hp * 128 + 128],
                          self.k_ones.ap()[par:par + 64, 0:2])]
                self.mm1(self.ps_ap(5, 2, 2 * h), pairs,
                         [B("R2"), B("R3"), self.sbbuf("k_ones")], self.psb[5])
            self.vop(lambda e, dv=A("dv"): e.tensor_copy(
                dv, self.ps_ap(5, 64, 0).rearrange("p (c two) -> p c two", two=2)[:, :, 0]),
                [self.psb[5]], [B("dv")])
            self.split(A("R0"), A("R1"), A("F8"), [B("F8")], B("R0"), B("R1"))
            self.split(A("R2"), A("R3"), A("F9"), [B("F9")], B("R2"), B("R3"))
            self.split(A("R4"), A("R5"), A("F2"), [B("F2")], B("R4"), B("R5"))
            self.vop(lambda e, r=A("F0"), afm=A("F7"): e.tensor_tensor(r, r, afm, ALU.mult),
                     [B("F7"), B("F0")], [B("F0")])
            self.split(A("R6"), A("R7"), A("F0"), [B("F0")], B("R6"), B("R7"))
            p2b = A("pcol2")[:, :, None].broadcast_to([128, 16, 64])
            self.vop(lambda e, o=A("F5"), p2b=p2b: e.tensor_tensor(
                o[:, 0:1024].rearrange("p (m i) -> p m i", m=16),
                self.k_S.ap().rearrange("p (m i) -> p m i", m=16), p2b, ALU.mult),
                [bS, B("pcol2")], [B("F5")])
            self.split(self.k_Sh.ap(), self.k_Sl.ap(), A("F5")[:, 0:1024], [B("F5")], bSh, bSl)
            for h in range(H64):
                par = (h % 2) * 64
                hp = h // 2
                sb_sc = 6 + h % 2
                pairs = [(A("R0")[par:par + 64, hp * 128:hp * 128 + 128],
                          A("R1")[par:par + 64, hp * 128:hp * 128 + 128],
                          A("R6")[par:par + 64, hp * 128:hp * 128 + 128],
                          A("R7")[par:par + 64, hp * 128:hp * 128 + 128])]
                self.mm3(self.ps_ap(sb_sc, 128), pairs,
                         [B("R0"), B("R1"), B("R6"), B("R7")], self.psb[sb_sc])
                self.vop(lambda e, sm=A("sm"), sb_sc=sb_sc: e.select(
                    sm, self.k_strictu.ap(), self.ps_ap(sb_sc, 128), self.k_zero.ap()),
                    [self.psb[sb_sc], self.sbbuf("k_strictu"), self.sbbuf("k_zero")], [B("sm")])
                self.split(A("smh"), A("sml"), A("sm"), [B("sm")], B("smh"), B("sml"))
                ybank = h // 8
                ycol = (h % 8) * 64
                pairs = [(A("smh"), A("sml"),
                          A("R4")[:, h * 64:h * 64 + 64], A("R5")[:, h * 64:h * 64 + 64])]
                self.mm3(self.ps_ap(ybank, 64, ycol), pairs,
                         [B("smh"), B("sml"), B("R4"), B("R5")],
                         self.psb[ybank], start=True, stop=False)
                pairs = [(A("R6")[par:par + 64, hp * 128:hp * 128 + 128],
                          A("R7")[par:par + 64, hp * 128:hp * 128 + 128],
                          self.k_Sh.ap()[par:par + 64, hp * 64:hp * 64 + 64],
                          self.k_Sl.ap()[par:par + 64, hp * 64:hp * 64 + 64])]
                self.mm3(self.ps_ap(ybank, 64, ycol), pairs,
                         [B("R6"), B("R7"), bSh, bSl],
                         self.psb[ybank], start=False, stop=True)
            for rnd in range(2):
                for mp in range(rnd * 8, rnd * 8 + 8):
                    k = mp % 8
                    bank = 4 + k // 4
                    dcol = (k % 4) * 128
                    pairs = [(A("R2")[:, mp * 128:mp * 128 + 128],
                              A("R3")[:, mp * 128:mp * 128 + 128],
                              A("R4")[:, mp * 128:mp * 128 + 128],
                              A("R5")[:, mp * 128:mp * 128 + 128])]
                    self.mm3(self.ps_ap(bank, 128, dcol), pairs,
                             [B("R2"), B("R3"), B("R4"), B("R5")], self.psb[bank])
                scol = rnd * 512
                Sap = self.k_S.ap()[:, scol:scol + 512]
                pcol_b = (A("pcol")[:, rnd * 8:rnd * 8 + 8, None].broadcast_to([128, 8, 64]))
                def supd(e, Sap=Sap, pcol_b=pcol_b, rnd=rnd):
                    e.tensor_tensor(Sap.rearrange("p (m i) -> p m i", m=8),
                                    Sap.rearrange("p (m i) -> p m i", m=8), pcol_b, ALU.mult)
                    e.drain()
                    ps45 = self.psum_h[:, 4 * 512: 6 * 512]
                    even = ps45[0:64, :].rearrange("p (m i) -> p m i", i=128)[:, :, 0:64]
                    odd = ps45[64:128, :].rearrange("p (m i) -> p m i", i=128)[:, :, 64:128]
                    e.tensor_tensor(Sap[0:64, :].rearrange("p (m i) -> p m i", m=8),
                                    Sap[0:64, :].rearrange("p (m i) -> p m i", m=8), even, ALU.add)
                    e.drain()
                    return e.tensor_tensor(Sap[64:128, :].rearrange("p (m i) -> p m i", m=8),
                                           Sap[64:128, :].rearrange("p (m i) -> p m i", m=8), odd, ALU.add)
                self.vop(supd, [self.psb[4], self.psb[5], B("pcol"), bS], [bS])
            dvb = A("dv")[:, :, None].broadcast_to([128, 32, 64])
            self.vop(lambda e, yt=A("F3"), dvb=dvb, vc_=A("F2"):
                     e.tensor_tensor(yt.rearrange("p (c i) -> p c i", c=32),
                                     vc_.rearrange("p (c i) -> p c i", c=32), dvb, ALU.mult),
                     [B("dv"), B("F2")], [B("F3")])
            def yev(e, yf=A("F6"), yt=A("F3")):
                last = None
                for b in range(4):
                    last = e.tensor_tensor(yf[:, b * 512:b * 512 + 512], self.ps_ap(b),
                                           yt[:, b * 512:b * 512 + 512], ALU.add)
                return last
            self.vop(yev, [B("F3"), self.psb[0], self.psb[1], self.psb[2], self.psb[3]], [B("F6")])
            if dbg:
                self.store(self.dram["DBG3"].ap()[c0:c0 + 128, :], A("F6"), B("F6"), self.dbuf("DBG3"))
            yf, byf = A("F6"), B("F6")
            yt, byt = A("F3"), B("F3")
            gst, bgst = A("gst"), B("gst")
            yf3 = yf.rearrange("p (c i) -> p c i", c=32)
            self.sop(lambda e, yt=yt, yf=yf: e.activation(yt, yf, AF.Square), [byf], [byt])
            yt3 = yt.rearrange("p (c i) -> p c i", c=32)
            def gred(e, gst=gst, yf3=yf3, yt3=yt3):
                e.tensor_reduce(gst[:, 0:32], yf3, AX.X, ALU.add)
                e.tensor_reduce(gst[:, 32:64], yt3, AX.X, ALU.add)
                e.drain()
                e.tensor_scalar(gst[:, 0:32], gst[:, 0:32], 1.0 / 64, None, ALU.mult)
                e.tensor_scalar(gst[:, 32:64], gst[:, 32:64], 1.0 / 64, None, ALU.mult)
                e.drain()
                e.tensor_tensor(gst[:, 64:96], gst[:, 0:32], gst[:, 0:32], ALU.mult)
                e.drain()
                return e.tensor_tensor(gst[:, 64:96], gst[:, 32:64], gst[:, 64:96], ALU.subtract)
            self.vop(gred, [byf, byt], [bgst])
            def grs(e, gst=gst):
                e.activation(gst[:, 96:128], gst[:, 64:96], AF.Ln, bias=float(EPS_GN))
                e.drain()
                return e.activation(gst[:, 96:128], gst[:, 96:128], AF.Exp, scale=-0.5)
            self.sop(grs, [bgst], [bgst])
            def ynorm(e, yf3=yf3, gst=gst):
                e.tensor_tensor(yf3, yf3,
                                gst[:, 0:32][:, :, None].broadcast_to([128, 32, 64]),
                                ALU.subtract)
                e.drain()
                return e.tensor_tensor(yf3, yf3,
                                       gst[:, 96:128][:, :, None].broadcast_to([128, 32, 64]),
                                       ALU.mult)
            self.vop(ynorm, [bgst, byf], [byf])
            self.load(A("F4").rearrange("p (c w) -> p c w", c=NCH),
                      self.dram["RW_G"].ap().rearrange("c p t -> p c t")[:, :, c0:c0 + 128],
                      self.dbuf("RW_G"), B("F4"))
            lng = A("lng")
            for c in range(NCH):
                bank = 4 + c % 2
                self.transpose_group(bank, [yf[:, c * 128:c * 128 + 128]], [byf])
                def ynev(e, bank=bank, c=c, t1=A("sm"), lng=lng, gc_=A("F4")):
                    e.tensor_scalar(t1, self.ps_ap(bank, 128), lng[:, c:c + 1],
                                    lng[:, 16 + c:16 + c + 1], ALU.mult, ALU.add)
                    e.drain()
                    return e.tensor_tensor(t1, t1, gc_[:, c * 128:c * 128 + 128], ALU.mult)
                self.vop(ynev, [self.psb[bank], B("lng"), B("F4")], [B("sm")])
                self.split(A("smh"), A("sml"), A("sm"), [B("sm")], B("smh"), B("sml"))
                self.store(self.dram["AOF_H"].ap()[c, :, c0:c0 + 128], A("smh"),
                           B("smh"), self.dbuf("AOF_H"))
                self.store(self.dram["AOF_L"].ap()[c, :, c0:c0 + 128], A("sml"),
                           B("sml"), self.dbuf("AOF_L"), q="S")
        self.sc.t_drain = False
        self.pass_out_proj("AOF_H", "AOF_L", "wo_r", resid_src, resid_dst)


def build_full(T_core=2048, npass=NPASS, debug_taps=()):
    m = MK(T_core=T_core, npass=npass, debug_taps=debug_taps)
    m._init_psum()
    m._init_arena()
    m._init_fconsts()
    m._decl_inputs()
    m._decl_scratch()
    m._init_consts()
    m.stage_zero_pads()
    m.stage_woa()
    m.stage_ln("x")
    m.stage_attn_a(resid_src="x", resid_dst="R")
    m.stage_ln("R")
    m.stage_ffn(resid_src="R", resid_dst="R")
    m.stage_ln("R", write_f32=True)
    m.stage_rwkv(resid_src="R", resid_dst="R")
    m.stage_ln("R")
    m.stage_ffn(resid_src="R", resid_dst="out")
    nc = m.finalize()
    return nc, m


def make_inmap(m, x_seq, w):
    perm = np.concatenate([np.arange(64, 128), np.arange(64)])
    permD = (np.arange(D).reshape(NCH, 128)[:, perm]).reshape(-1)
    def fmcol(v):
        return np.ascontiguousarray(v.reshape(NCH, 128).T)
    T_ = lambda a: np.ascontiguousarray(np.asarray(a).T)
    C_ = lambda a: np.ascontiguousarray(np.asarray(a))
    maa = np.concatenate([fmcol(np.asarray(w[f"time_maa_{k}"]).reshape(-1))
                          for k in ("x", "w", "k", "v", "r", "g")], 1)
    return {
        "x": np.ascontiguousarray(x_seq),
        "wq_a": T_(w["a_Wq"]), "wqs_a": T_(np.asarray(w["a_Wq"])[permD]),
        "wk_a": T_(w["a_Wk"]), "wks_a": T_(np.asarray(w["a_Wk"])[permD]),
        "wv_a": T_(w["a_Wv"]), "wo_a": T_(w["a_Wo"]),
        "wq_f": T_(w["f_Wq"]), "wk_f": T_(w["f_Wk"]), "wv_f": T_(w["f_Wv"]),
        "wo_f_nat": C_(w["f_Wo"]), "a_f": C_(w["f_A"]), "b_f": C_(w["f_B"]),
        "win": T_(w["f_Win"]), "wup": T_(w["f_Wup"]), "wdn": T_(w["f_Wdn"]),
        "ln1_row": np.asarray(w["f_ln1"]).reshape(1, D).astype(np.float32),
        "ln2_row": np.asarray(w["f_ln2"]).reshape(1, D).astype(np.float32),
        "wr_r": T_(w["r_Wr"]), "wk_r": T_(w["r_Wk"]), "wv_r": T_(w["r_Wv"]),
        "wg_r": T_(w["r_Wg"]), "wo_r": T_(w["r_Wo"]),
        "maa_w1": C_(w["time_maa_w1"]), "maa_w2": C_(w["time_maa_w2"]),
        "dec_w1": C_(w["time_decay_w1"]), "dec_w2": C_(w["time_decay_w2"]),
        "maa_cols": maa,
        "decay_row": np.asarray(w["time_decay"]).reshape(1, D).astype(np.float32),
        "u_col": fmcol(np.asarray(w["time_faaaa"]).reshape(-1)),
        "lnr_cols": np.concatenate([fmcol(np.asarray(w["r_ln_gamma"]).reshape(-1)),
                                    fmcol(np.asarray(w["r_ln_beta"]).reshape(-1))], 1),
    }

_CACHE = {}

def kernel(**inputs):
    if "nc" not in _CACHE:
        _CACHE["nc"], _CACHE["m"] = build_full(T_core=2048)
    nc, m = _CACHE["nc"], _CACHE["m"]
    w = {
        "a_Wq": inputs["L0_attn_a_Wq"], "a_Wk": inputs["L0_attn_a_Wk"],
        "a_Wv": inputs["L0_attn_a_Wv"], "a_Wo": inputs["L0_attn_a_Wo"],
        "f_Wq": inputs["L0_ffn_Wq"], "f_Wk": inputs["L0_ffn_Wk"], "f_Wv": inputs["L0_ffn_Wv"],
        "f_Wo": inputs["L0_ffn_Wo"], "f_A": inputs["L0_ffn_A"], "f_B": inputs["L0_ffn_B"],
        "f_Win": inputs["L0_ffn_W_in_base"], "f_Wup": inputs["L0_ffn_W_up_base"],
        "f_Wdn": inputs["L0_ffn_W_down"], "f_ln1": inputs["L0_ffn_ln1_w"],
        "f_ln2": inputs["L0_ffn_ln2_w"],
        "r_Wr": inputs["L0_attn_b_W_r"], "r_Wk": inputs["L0_attn_b_W_k"],
        "r_Wv": inputs["L0_attn_b_W_v"], "r_Wg": inputs["L0_attn_b_W_g"],
        "r_Wo": inputs["L0_attn_b_W_o"],
        "r_ln_gamma": inputs["L0_attn_b_ln_gamma"], "r_ln_beta": inputs["L0_attn_b_ln_beta"],
        "time_maa_x": inputs["L0_attn_b_time_maa_x"], "time_maa_w": inputs["L0_attn_b_time_maa_w"],
        "time_maa_k": inputs["L0_attn_b_time_maa_k"], "time_maa_v": inputs["L0_attn_b_time_maa_v"],
        "time_maa_r": inputs["L0_attn_b_time_maa_r"], "time_maa_g": inputs["L0_attn_b_time_maa_g"],
        "time_maa_w1": inputs["L0_attn_b_time_maa_w1"], "time_maa_w2": inputs["L0_attn_b_time_maa_w2"],
        "time_decay": inputs["L0_attn_b_time_decay"], "time_decay_w1": inputs["L0_attn_b_time_decay_w1"],
        "time_decay_w2": inputs["L0_attn_b_time_decay_w2"], "time_faaaa": inputs["L0_attn_b_time_faaaa"],
    }
    x = np.asarray(inputs["L0_attn_a_x"])
    base = make_inmap(m, x[0], w)
    in_maps = []
    for c in range(8):
        s = c // 2
        im = dict(base)
        im["x"] = np.ascontiguousarray(x[s])
        in_maps.append(im)
    res = run_bass_kernel_spmd(nc, in_maps, core_ids=list(range(8)))
    return np.stack([np.asarray(res.results[2 * s]["out"]) for s in range(4)], 0)


# version 08
# speedup vs baseline: 1.0805x; 1.0805x over previous
import math
import os
os.environ["GAUGE_NTFF_IGNORE_DMA"] = "1"
import numpy as np
import concourse.bass as bass
import concourse.mybir as mybir
from concourse.bass_utils import run_bass_kernel_spmd

F32 = mybir.dt.float32
F32R = mybir.dt.float32r
AF = mybir.ActivationFunctionType
ALU = mybir.AluOpType
AX = mybir.AxisListType

D = 2048
NCH = 16
HEAD128 = 16
NPASS = 3
F_FFN = 8192
NFC = F_FFN // 128
EPS_A = 1e-6
EPS_LN = 1e-5
EPS_RMS = 1e-6
EPS_GN = 1e-5 * 64.0

class Stream:
    def __init__(self, name, engine_attr):
        self.name = name
        self.engine_attr = engine_attr
        self.cmp_count = 0
        self.dma_count = 0
        self.ops = []

class Tok:
    __slots__ = ("stream", "dma", "count")
    def __init__(self, stream, dma, count):
        self.stream = stream
        self.dma = dma
        self.count = count

class Sched:
    NSEM_DMA = 6

    def __init__(self, nc):
        self.t_drain = False
        self.nc = nc
        self.gseq = 0
        self.tok_gidx = {}
        self.streams = {}
        for name, attr in [("T", "tensor"), ("V", "vector"), ("S", "scalar"),
                           ("G", "gpsimd"), ("Y", "sync")]:
            self.streams[name] = Stream(name, attr)

    def op(self, stream_name, emit_fn, deps=(), dma=False):
        st = self.streams[stream_name]
        if dma:
            st.dma_count += 1
            cnt = st.dma_count
        else:
            st.cmp_count += 1
            cnt = st.cmp_count
        waits = []
        need_drain = 0
        sdeps = set()
        for d in deps:
            if d is None:
                continue
            if d.stream is st and not d.dma:
                sdeps.add(d.count)
                if stream_name != "T" or self.t_drain:
                    need_drain = max(need_drain, d.count)
                continue
            waits.append(d)
        best = {}
        for d in waits:
            key = (d.stream.name, d.dma)
            if key not in best or best[key].count < d.count:
                best[key] = d
        gidx = self.gseq
        self.gseq += 1
        self.tok_gidx[(stream_name, dma, cnt)] = gidx
        st.ops.append([emit_fn, dma, cnt, list(best.values()), need_drain, sdeps, gidx])
        return Tok(st, dma, cnt)

    def barrier_tokens(self):
        toks = []
        for st in self.streams.values():
            if st.cmp_count:
                toks.append(Tok(st, False, st.cmp_count))
            if st.dma_count:
                toks.append(Tok(st, True, st.dma_count))
        return toks

    def _reorder_T(self):
        ops = self.streams["T"].ops
        n = len(ops)
        if n < 3:
            return
        def max_wait_gidx(op):
            g = -1
            for d in op[3]:
                g = max(g, self.tok_gidx.get((d.stream.name, d.dma, d.count), 1 << 60))
            return g
        mw = [max_wait_gidx(op) for op in ops]
        eff = [op[6] for op in ops]
        for _ in range(24):
            changed = False
            for i in range(n - 1):
                W, X = ops[i], ops[i + 1]
                mwW, mwX = mw[i], mw[i + 1]
                if W[4] or X[4]:
                    continue
                if W[2] in X[5]:
                    continue
                if mwX >= eff[i]:
                    continue
                if mwX >= mwW:
                    continue
                ops[i], ops[i + 1] = X, W
                mw[i], mw[i + 1] = mwX, mwW
                eff[i], eff[i + 1] = eff[i], eff[i]
                changed = True
            if not changed:
                break

    def emit(self, block, sems_cmp, sems_dma):
        self._reorder_T()
        awaited = {name: set() for name in self.streams}
        for st in self.streams.values():
            for _emit_fn, _dma, _cnt, waits, _nd, _sd, _gi in st.ops:
                for d in waits:
                    if not d.dma:
                        awaited[d.stream.name].add(d.count)
        rank = {}
        for name, st in self.streams.items():
            r = 0
            for op in st.ops:
                if not op[1] and op[2] in awaited[name]:
                    r += 1
                    rank[(name, op[2])] = r

        for name, st in self.streams.items():
            if not st.ops:
                continue
            eng_stream_factory = getattr(block, st.engine_attr)

            def make_body(st=st, name=name):
                def body(eng):
                    hw = {}
                    last_drain = 0
                    for emit_fn, dma, cnt, waits, need_drain, _sd, _gi in st.ops:
                        if need_drain > last_drain and not dma:
                            eng.drain()
                            last_drain = cnt - 1
                        for d in waits:
                            if d.dma:
                                k = d.count
                                sem_i = (k - 1) % Sched.NSEM_DMA
                                per = 16 * ((k + Sched.NSEM_DMA - 1) // Sched.NSEM_DMA)
                                key = (d.stream.name, "d", sem_i)
                                if hw.get(key, -1) < per:
                                    eng.wait_ge(sems_dma[d.stream.name][sem_i], per)
                                    hw[key] = per
                            else:
                                val = rank[(d.stream.name, d.count)]
                                key = (d.stream.name, "c", 0)
                                if hw.get(key, -1) < val:
                                    eng.wait_ge(sems_cmp[d.stream.name], val)
                                    hw[key] = val
                        inst = emit_fn(eng)
                        if dma:
                            sem_i = (cnt - 1) % Sched.NSEM_DMA
                            inst.then_inc(sems_dma[st.name][sem_i], 16)
                        elif cnt in awaited[name]:
                            if inst is None:
                                inst = eng.nop()
                            inst.then_inc(sems_cmp[st.name], 1)
                return body
            eng_stream_factory(make_body())



class Buf:
    __slots__ = ("writer", "readers", "name")
    def __init__(self, name=""):
        self.writer = None
        self.readers = []
        self.name = name

    def rd(self):
        return [self.writer] if self.writer else []

    def wr(self):
        deps = list(self.readers)
        if self.writer:
            deps.append(self.writer)
        return deps

    def did_read(self, tok):
        self.readers.append(tok)

    def did_write(self, tok):
        self.writer = tok
        self.readers = []


def deps_rw(reads, writes):
    d = []
    for b in reads:
        d += b.rd()
    for b in writes:
        d += b.wr()
    return d

def commit_rw(tok, reads, writes):
    for b in reads:
        b.did_read(tok)
    for b in writes:
        b.did_write(tok)


RSQ128 = 1.0 / math.sqrt(128.0)

class MK:
    def __init__(self, T_core=2048, npass=NPASS, debug_taps=()):
        self.T = T_core
        self.NP = npass
        self.nc = bass.Bass("TRN2", target_bir_lowering=False, debug=False)
        self.sc = Sched(self.nc)
        self.debug_taps = set(debug_taps)
        self.dram = {}
        self.sbuf = {}
        self.bufs = {}
        self._ctx = []
        self._psum_banks = None

    def dram_t(self, name, shape, dtype=F32, kind="Internal"):
        if kind == "Internal" and name in self.debug_taps:
            kind = "ExternalOutput"
        h = self.nc.dram_tensor(name, shape, dtype, kind=kind)
        self.dram[name] = h
        self.bufs[("dram", name)] = Buf(name)
        return h

    def dbuf(self, name):
        return self.bufs[("dram", name)]

    def sbuf_t(self, name, shape, dtype=F32):
        h = self.nc.alloc_sbuf_tensor(name, shape, dtype)
        self.sbuf[name] = h
        self.bufs[("sb", name)] = Buf(name)
        return h

    def sbbuf(self, name):
        return self.bufs[("sb", name)]

    def _op(self, stream, emit, reads, writes, dma=False):
        tok = self.sc.op(stream, emit, deps=deps_rw(reads, writes), dma=dma)
        commit_rw(tok, reads, writes)
        return tok

    def load(self, dst_ap, src_ap, rbuf, wbuf, q="Y", slow=False):
        return self._op(q, lambda e: e.dma_start(out=dst_ap, in_=src_ap, allow_slow_non_contiguous=slow), [rbuf], [wbuf], dma=True)

    def store(self, dst_ap, src_ap, rbuf, wbuf, q="G", slow=False):
        return self._op(q, lambda e: e.dma_start(out=dst_ap, in_=src_ap, allow_slow_non_contiguous=slow), [rbuf], [wbuf], dma=True)

    def vop(self, emit, reads, writes, stream="V"):
        return self._op(stream, emit, reads, writes)

    def sop(self, emit, reads, writes):
        return self._op("S", emit, list(reads) + [self.kc_buf], writes)

    def gop(self, emit, reads, writes):
        return self._op("G", emit, reads, writes)

    def top(self, emit, reads, writes):
        return self._op("T", emit, reads, writes)

    def split(self, hi_ap, lo_ap, src_ap, rbufs, whi, wlo, stream="V"):
        self.sop(lambda e: e.activation(hi_ap, src_ap, AF.Copy), rbufs, [whi])
        def emit(e):
            return e.tensor_tensor(lo_ap, src_ap, hi_ap.bitcast(F32), ALU.subtract)
        return self._op(stream, emit, list(rbufs) + [whi], [wlo])

    def mm3(self, ps_ap, pairs, rbufs, wbuf, start=True, stop=True, tp=None):
        np_ = self.NP
        def emit(e):
            last = None
            n = len(pairs)
            for i, (lh, ll, rh, rl) in enumerate(pairs):
                first = (i == 0) and start
                plist = [(lh, rh)]
                if np_ >= 2 and rl is not None:
                    plist.append((lh, rl))
                if np_ >= 3 and ll is not None:
                    plist.append((ll, rh))
                for j, (a, b) in enumerate(plist):
                    is_first = first and j == 0
                    is_last = stop and (i == n - 1) and (j == len(plist) - 1)
                    try:
                        last = e.matmul(ps_ap, a, b, start=is_first, stop=is_last,
                                        tile_position=tp)
                    except Exception as ex:
                        raise RuntimeError(f"matmul fail: lhsT={a} rhs={b} out={ps_ap}") from ex
            return last
        return self._op("T", emit, rbufs, [wbuf])

    def mm1(self, ps_ap, pairs, rbufs, wbuf, start=True, stop=True):
        def emit(e):
            last = None
            for i, (a, b) in enumerate(pairs):
                last = e.matmul(ps_ap, a, b,
                                start=(start and i == 0), stop=(stop and i == len(pairs) - 1))
            return last
        return self._op("T", emit, rbufs, [wbuf])

    ARENA_BYTES = 174 * 1024

    def _init_arena(self):
        self.arena = self.nc.alloc_sbuf_tensor("arena", [128, self.ARENA_BYTES], mybir.dt.uint8)
        self.arena_addr = self.nc.lookup_mloc(self.arena).addr
        self.views = {}
        self._stage_tag = 0

    def stage(self, layout, keep=(), auto_base=0):
        toks = self.sc.barrier_tokens()
        bar = self.sc.op("G", lambda e: None, deps=toks)
        old_views = self.views
        self._stage_tag += 1
        self.views = {}
        off = auto_base
        for entry in layout:
            if len(entry) == 4:
                name, shape, dtype, fixed = entry
            else:
                (name, shape, dtype), fixed = entry, None
            per_part = int(np.prod(shape[1:])) * mybir.dt.size(dtype)
            if fixed is not None:
                this_off = fixed
            else:
                off = (off + 63) & ~63
                this_off = off
                off += per_part
            assert this_off + per_part <= self.ARENA_BYTES, f"arena overflow at {name}"
            vname = f"vw{self._stage_tag}_{name}"
            hnd = self.nc.alloc_sbuf_tensor_at(vname, list(shape), dtype,
                                               offset=self.arena_addr + this_off)
            if name in keep and name in old_views:
                b = old_views[name][1]
            else:
                b = Buf(vname)
                b.writer = bar
            self.views[name] = (hnd, b)
        return bar

    def v_ap(self, name):
        return self.views[name][0].ap()

    def v_hnd(self, name):
        return self.views[name][0]

    def v_buf(self, name):
        return self.views[name][1]

    def _init_psum(self):
        self.psum = self.nc.alloc_sbuf_tensor
        self._psum_guard = self.nc.psum_tensor("ps_all", [128, 4096], F32)
        self.psum_h = self._psum_guard.__enter__()
        self.psb = [Buf(f"psum{b}") for b in range(8)]

    def ps_ap(self, bank, cols=512, off=0):
        return self.psum_h[:, bank * 512 + off: bank * 512 + off + cols]

    def ps_ap_p(self, bank, p0, pn, cols, off=0):
        return self.psum_h[p0:p0 + pn, bank * 512 + off: bank * 512 + off + cols]

    FCONSTS = [EPS_LN, EPS_RMS, EPS_GN, EPS_A, 1e-30]

    def _init_fconsts(self):
        nc = self.nc
        vals = np.array(self.FCONSTS, np.float32)
        data = np.repeat(vals[None, :], 128, 0)
        t = nc.inline_tensor(data, name="c_fvals")
        self.k_fc = self.sbuf_t("k_fc", [128, len(self.FCONSTS)], F32)
        self.load(self.k_fc.ap(), t.ap(), Buf("z0"), self.sbbuf("k_fc"))
        for i, v in enumerate(self.FCONSTS):
            nc.const_aps.register(F32, float(v), self.k_fc.ap()[:, i:i+1])
        self.kc_buf = self.sbbuf("k_fc")

    def _init_consts(self):
        nc = self.nc
        T = self.T
        ident = np.eye(128, dtype=np.float32)
        tri = (np.arange(128)[:, None] < np.arange(128)[None, :]).astype(np.float32)
        strict = tri.copy()
        ones = np.ones((128, 128), dtype=np.float32)
        mm_ = np.zeros((128, 4 * 512), dtype=np.float32)
        am_ = np.zeros((128, 4 * 512), dtype=np.float32)
        for sub in range(4):
            keep = (np.arange(512)[None, :] >= (np.arange(128)[:, None] + sub * 128))
            mm_[:, sub * 512:(sub + 1) * 512] = keep
            am_[:, sub * 512:(sub + 1) * 512] = np.where(keep, 0.0, -1e30)
        half = 64
        theta = (10000.0 ** (-np.arange(half, dtype=np.float64) * 2.0 / 128.0))
        pos = np.arange(T, dtype=np.float64)
        ang = pos[None, :] * theta[:, None]
        cos_t = np.cos(ang)
        sin_t = np.sin(ang)
        cos_ext = np.concatenate([cos_t, cos_t], 0).astype(np.float32)
        sin_sgn = np.concatenate([-sin_t, sin_t], 0).astype(np.float32)
        self.c_id = nc.inline_tensor(ident, name="c_id")
        self.c_tri = nc.inline_tensor(tri, name="c_tri")
        self.c_strict = nc.inline_tensor(strict, name="c_strict")
        self.c_strictu = nc.inline_tensor(strict.astype(np.uint8), name="c_strictu")
        self.c_ones = nc.inline_tensor(ones, name="c_ones")
        self.c_mm = nc.inline_tensor(mm_, name="c_mm")
        self.c_am = nc.inline_tensor(am_, name="c_am")
        self.c_cos = nc.inline_tensor(cos_ext, name="c_cos")
        self.c_sin = nc.inline_tensor(sin_sgn, name="c_sin")
        self.k_id = self.sbuf_t("k_id", [128, 128], F32)
        self.k_tri = self.sbuf_t("k_tri", [128, 128], F32R)
        self.k_strict = self.sbuf_t("k_strict", [128, 128], F32)
        self.k_strictu = self.sbuf_t("k_strictu", [128, 128], mybir.dt.uint8)
        self.k_ones = self.sbuf_t("k_ones", [128, 128], F32R)
        self.k_zero = self.sbuf_t("k_zero", [128, 128], F32)
        self.k_mm = self.sbuf_t("k_mm", [128, 4 * 512], F32)
        self.k_am = self.sbuf_t("k_am", [128, 4 * 512], F32)
        zb = Buf("zero")
        self.gop(lambda e: e.memset(self.k_zero.ap(), 0.0), [], [self.sbbuf("k_zero")])
        self.load(self.k_strictu.ap(), self.c_strictu.ap(), zb, self.sbbuf("k_strictu"))
        for dst, src, q in [(self.k_id, self.c_id, "Y"), (self.k_strict, self.c_strict, "Y"),
                            (self.k_mm, self.c_mm, "Y"), (self.k_am, self.c_am, "Y"),
                            (self.k_tri, self.c_tri, "G"), (self.k_ones, self.c_ones, "G")]:
            self.load(dst.ap(), src.ap(), zb, self.sbbuf(dst.name), q=q)

    def _decl_inputs(self):
        nc = self.nc
        T = self.T
        dt_in = [
            ("x", [T, D]),
            ("wq_a", [D, D]), ("wqs_a", [D, D]), ("wk_a", [D, D]), ("wks_a", [D, D]),
            ("wv_a", [D, D]), ("wo_a", [D, D]),
            ("wq_f", [D, D]), ("wk_f", [D, D]), ("wv_f", [D, D]), ("wo_f_nat", [D, D]),
            ("a_f", [D, 16]), ("b_f", [16, F_FFN]),
            ("win", [D, F_FFN]), ("wup", [D, F_FFN]), ("wdn", [F_FFN, D]),
            ("ln1_row", [1, D]), ("ln2_row", [1, D]),
            ("wr_r", [D, D]), ("wk_r", [D, D]), ("wv_r", [D, D]), ("wg_r", [D, D]),
            ("wo_r", [D, D]),
            ("maa_w1", [D, 160]), ("maa_w2", [5, 32, D]),
            ("dec_w1", [D, 64]), ("dec_w2", [64, D]),
            ("maa_cols", [128, 6 * 16]),
            ("decay_row", [1, D]),
            ("u_col", [128, 16]),
            ("lnr_cols", [128, 32]),
        ]
        for name, shape in dt_in:
            self.dram_t(name, shape, F32, kind="ExternalInput")
        self.dram_t("out", [T, D], F32, kind="ExternalOutput")

    def _decl_scratch(self):
        T = self.T
        self.dram_t("R", [T, D])
        self.dram_t("XN", [1 + T, D])
        self.dram_t("XNF_H", [NCH, 128, 1 + T], F32R)
        self.dram_t("XNF_L", [NCH, 128, 1 + T], F32R)
        self.dram_t("XNF_F", [NCH, 128, 1 + T])
        self.dram_t("KF_H", [NCH, 128, T], F32R)
        self.dram_t("KF_L", [NCH, 128, T], F32R)
        self.dram_t("VT_H", [T, D], F32R)
        self.dram_t("VT_L", [T, D], F32R)
        self.dram_t("AOF_H", [NCH, 128, T], F32R)
        self.dram_t("AOF_L", [NCH, 128, T], F32R)
        self.dram_t("BF_H", [NCH, 128, T], F32R)
        self.dram_t("BF_L", [NCH, 128, T], F32R)
        self.dram_t("BF2_H", [NCH, 128, T], F32R)
        self.dram_t("BF2_L", [NCH, 128, T], F32R)
        self.dram_t("RW_R", [NCH, 128, T])
        self.dram_t("RW_K", [T, D])
        self.dram_t("RW_V", [T, D])
        self.dram_t("RW_G", [NCH, 128, T])
        self.dram_t("RW_W", [T, D])
        self.dram_t("ROWTMP", [16, D])
        self.dram_t("ZF_H", [16, T], F32R)
        self.dram_t("ZF_L", [16, T], F32R)

    def transpose_group(self, bank, src_aps, rbufs):
        idap = self.k_id.ap()
        def emit(e):
            last = None
            for i, ap in enumerate(src_aps):
                last = e.transpose(self.ps_ap(bank, 128, i * 128), ap, idap)
            return last
        return self._op("T", emit, rbufs + [self.sbbuf("k_id")], [self.psb[bank]])

    def ln_norm_tile(self, vx, vsq, vst, eps, kind="ln"):
        xap = self.v_ap(vx)
        sqap = self.v_ap(vsq)
        st = self.v_ap(vst)
        bx, bsq, bst = self.v_buf(vx), self.v_buf(vsq), self.v_buf(vst)
        self.sop(lambda e: e.activation(sqap, xap, AF.Square), [bx], [bsq])
        if kind == "ln":
            def red(e):
                e.tensor_reduce(st[:, 0:1], xap, AX.X, ALU.add)
                return e.tensor_reduce(st[:, 1:2], sqap, AX.X, ALU.add)
            self.vop(red, [bx, bsq], [bst])
            def stats(e):
                e.tensor_scalar(st[:, 0:1], st[:, 0:1], 1.0 / D, None, ALU.mult)
                e.tensor_scalar(st[:, 1:2], st[:, 1:2], 1.0 / D, None, ALU.mult)
                e.drain()
                e.tensor_tensor(st[:, 2:3], st[:, 0:1], st[:, 0:1], ALU.mult)
                e.drain()
                return e.tensor_tensor(st[:, 2:3], st[:, 1:2], st[:, 2:3], ALU.subtract)
            self.vop(stats, [bst], [bst])
        else:
            self.vop(lambda e: e.tensor_reduce(st[:, 1:2], sqap, AX.X, ALU.add), [bsq], [bst])
            self.vop(lambda e: e.tensor_scalar(st[:, 2:3], st[:, 1:2], 1.0 / D, None, ALU.mult), [bst], [bst])
        def rs(e):
            e.activation(st[:, 3:4], st[:, 2:3], AF.Ln, bias=float(eps))
            e.drain()
            return e.activation(st[:, 3:4], st[:, 3:4], AF.Exp, scale=-0.5)
        self.sop(rs, [bst], [bst])
        if kind == "ln":
            self.vop(lambda e: e.tensor_scalar(xap, xap, st[:, 0:1], st[:, 3:4],
                                               ALU.subtract, ALU.mult), [bst, bx], [bx])
        else:
            self.vop(lambda e: e.tensor_scalar(xap, xap, st[:, 3:4], None, ALU.mult),
                     [bst, bx], [bx])

    def stage_ln(self, src_name, write_f32=False, eps=EPS_LN):
        T = self.T
        layout = [(f"x{tt}", [128, D], F32) for tt in range(4)]
        layout += [("sq", [128, D], F32), ("st", [128, 8], F32),
                   ("th", [128, 512], F32R), ("tl", [128, 512], F32R),
                   ("tf", [128, 512], F32)]
        self.stage(layout)
        src = self.dram[src_name]
        dsrc = self.dbuf(src_name)
        XN, XNF_H, XNF_L, XNF_F = (self.dram[n] for n in ("XN", "XNF_H", "XNF_L", "XNF_F"))
        bXN, bH, bL, bF = (self.dbuf(n) for n in ("XN", "XNF_H", "XNF_L", "XNF_F"))
        for g in range(T // 512):
            for tt in range(4):
                t0 = g * 512 + tt * 128
                vx = f"x{tt}"
                self.load(self.v_ap(vx), src.ap()[t0:t0 + 128, :], dsrc, self.v_buf(vx))
                self.ln_norm_tile(vx, "sq", "st", eps, kind="ln")
                self.store(XN.ap()[1 + t0: 1 + t0 + 128, :], self.v_ap(vx),
                           self.v_buf(vx), bXN)
            for c in range(NCH):
                bank = c % 2
                srcs = [self.v_ap(f"x{tt}")[:, c * 128:(c + 1) * 128] for tt in range(4)]
                self.transpose_group(bank, srcs, [self.v_buf(f"x{tt}") for tt in range(4)])
                th, tl, tf = self.v_ap("th"), self.v_ap("tl"), self.v_ap("tf")
                bth, btl, btf = self.v_buf("th"), self.v_buf("tl"), self.v_buf("tf")
                self.split(th, tl, self.ps_ap(bank), [self.psb[bank]], bth, btl)
                self.store(XNF_H.ap()[c, :, 1 + g * 512: 1 + g * 512 + 512], th,
                           bth, bH)
                self.store(XNF_L.ap()[c, :, 1 + g * 512: 1 + g * 512 + 512], tl,
                           btl, bL, q="S")
                if write_f32:
                    self.vop(lambda e, tf=tf, bank=bank: e.tensor_copy(tf, self.ps_ap(bank)),
                             [self.psb[bank]], [btf])
                    self.store(XNF_F.ap()[c, :, 1 + g * 512: 1 + g * 512 + 512], tf,
                               btf, bF)

    def stage_zero_pads(self):
        self.stage([("z", [128, D], F32)])
        z = self.v_ap("z")
        bz = self.v_buf("z")
        self.vop(lambda e: e.memset(z, 0.0), [], [bz])
        self.store(self.dram["XN"].ap()[0:1, :], z[0:1, :],
                   bz, self.dbuf("XN"))
        for name in ("XNF_H", "XNF_L", "XNF_F"):
            self.store(self.dram[name].ap()[:, :, 0:1].rearrange("c p o -> p (c o)"),
                       z[:, 0:NCH], bz, self.dbuf(name), q="G", slow=True)

    def stage_copy_out(self, src="R"):
        T = self.T
        self.stage([("t0", [128, D], F32), ("t1", [128, D], F32)])
        for tt in range(T // 128):
            v = f"t{tt % 2}"
            self.load(self.v_ap(v), self.dram[src].ap()[tt * 128:(tt + 1) * 128, :],
                      self.dbuf(src), self.v_buf(v))
            self.store(self.dram["out"].ap()[tt * 128:(tt + 1) * 128, :], self.v_ap(v),
                       self.v_buf(v), self.dbuf("out"))

    def finalize(self):
        nc = self.nc
        toks = self.sc.barrier_tokens()
        self.sc.op("G", lambda e: None, deps=toks)
        self._ctx_sems = []
        sems_cmp = {}
        sems_dma = {}
        with_list = []
        for name in ("T", "V", "S", "G", "Y"):
            g = nc.semaphore(f"sc_{name}")
            h = g.__enter__()
            self._ctx_sems.append(g)
            sems_cmp[name] = h
            lst = []
            for i in range(Sched.NSEM_DMA):
                gg = nc.semaphore(f"sd_{name}{i}")
                hh = gg.__enter__()
                self._ctx_sems.append(gg)
                lst.append(hh)
            sems_dma[name] = lst
        blk = nc.Block()
        block = blk.__enter__()
        self.sc.emit(block, sems_cmp, sems_dma)
        blk.__exit__(None, None, None)
        for g in reversed(self._ctx_sems):
            g.__exit__(None, None, None)
        self._psum_guard.__exit__(None, None, None)
        return nc

    def wcol_fm(self, wname, j0, w, vst, vhi, vlo, kcn=NCH, scale_col=None, scale_const=None):
        W = self.dram[wname]
        bW = self.dbuf(wname)
        nview = self.v_ap(vst).shape[1]
        st_ap = self.v_ap(vst).rearrange("p (k w) -> p k w", k=kcn)[:, :, 0:w]
        src = W.ap().rearrange("(k p) o -> p k o", p=128)[:, 0:kcn, j0:j0 + w]
        bst, bhi, blo = self.v_buf(vst), self.v_buf(vhi), self.v_buf(vlo)
        self.load(st_ap, src, bW, bst)
        extra_r = []
        if scale_col is not None or scale_const is not None:
            sap, sbuf = scale_col if scale_col is not None else (None, None)
            full_ap = self.v_ap(vst)
            def sc(e, full_ap=full_ap):
                last = None
                for kc in range(kcn):
                    sl = st_ap[:, kc, :]
                    if sap is not None:
                        last = e.tensor_scalar(sl, sl, sap[:, kc:kc + 1], None, ALU.mult)
                if scale_const is not None:
                    e.drain()
                    last = e.tensor_scalar(full_ap, full_ap, float(scale_const), None, ALU.mult)
                return last
            rr = [bst] + ([sbuf] if sbuf is not None else [])
            self.vop(sc, rr, [bst])
        self.split(self.v_ap(vhi), self.v_ap(vlo), self.v_ap(vst), [bst], bhi, blo)

    def wtile8(self, wname, kc0, c0, vst, vhi, vlo, q="Y"):
        W = self.dram[wname]
        st = self.v_ap(vst).rearrange("p (k w) -> p k w", k=4)
        src = W.ap().rearrange("(k p) o -> p k o", p=128)[:, kc0:kc0 + 4, c0:c0 + 512]
        self.load(st, src, self.dbuf(wname), self.v_buf(vst), q=q)
        self.split(self.v_ap(vhi), self.v_ap(vlo), self.v_ap(vst),
                   [self.v_buf(vst)], self.v_buf(vhi), self.v_buf(vlo))

    def wtile(self, wname, r0, rn, c0, cn, vst, vhi, vlo, q="Y"):
        W = self.dram[wname]
        st = self.v_ap(vst)[:, 0:cn]
        self.load(st[0:rn, :], W.ap()[r0:r0 + rn, c0:c0 + cn], self.dbuf(wname),
                  self.v_buf(vst), q=q)
        self.split(self.v_ap(vhi)[0:rn, 0:cn], self.v_ap(vlo)[0:rn, 0:cn], st[0:rn, :],
                   [self.v_buf(vst)], self.v_buf(vhi), self.v_buf(vlo))

    def mm3_pairs(self, vwh, vwl, vxh, vxl, kcn, msl=None, nsl=None):
        wh = self.v_ap(vwh).rearrange("p (k w) -> p k w", k=kcn)
        wl = self.v_ap(vwl).rearrange("p (k w) -> p k w", k=kcn)
        xh = self.v_ap(vxh).rearrange("p (k w) -> p k w", k=kcn)
        xl = self.v_ap(vxl).rearrange("p (k w) -> p k w", k=kcn)
        pairs = []
        for kc in range(kcn):
            a, al = wh[:, kc, :], wl[:, kc, :]
            b, bl = xh[:, kc, :], xl[:, kc, :]
            if msl is not None:
                a, al = a[:, msl], al[:, msl]
            if nsl is not None:
                b, bl = b[:, nsl], bl[:, nsl]
            pairs.append((a, al, b, bl))
        return pairs

    def rope_swap(self, bank, t1, b1, t2, b2, t3, b3, cos_ap, sin_ap):
        self.vop(lambda e, t1=t1, bank=bank: e.tensor_copy(t1, self.ps_ap(bank)),
                 [self.psb[bank]], [b1])
        self.store(t2[64:128, :], t1[0:64, :], b1, b2, q="G")
        self.store(t2[0:64, :], t1[64:128, :], b1, b2, q="G")
        def rp(e, t1=t1, t2=t2, t3=t3, cos_ap=cos_ap, sin_ap=sin_ap):
            e.tensor_tensor(t3, t1, cos_ap, ALU.mult)
            e.tensor_tensor(t2, t2, sin_ap, ALU.mult)
            e.drain()
            return e.tensor_tensor(t3, t3, t2, ALU.add)
        self.vop(rp, [b1, b2, self.v_buf("cosb"), self.v_buf("sinb")], [b2, b3])

    def stage_attn_a(self, resid_src="x", resid_dst="R"):
        T = self.T
        NB = T // 512
        self.stage([
            ("xnh", [128, NCH * 512], F32R), ("xnl", [128, NCH * 512], F32R),
            ("wst", [128, 2048], F32), ("wh", [128, 2048], F32R), ("wl", [128, 2048], F32R),
            ("vwst", [128, 4 * 512], F32), ("vwh", [128, 4 * 512], F32R), ("vwl", [128, 4 * 512], F32R),
            ("cosb", [128, 512], F32), ("sinb", [128, 512], F32),
            ("t1", [128, 512], F32), ("t2", [128, 512], F32), ("t3", [128, 512], F32),
            ("sh", [128, 512], F32R), ("sl", [128, 512], F32R),
        ])
        XNH, XNL = self.dram["XNF_H"], self.dram["XNF_L"]
        bXNH, bXNL = self.dbuf("XNF_H"), self.dbuf("XNF_L")
        for kb in range(NB):
            t0 = kb * 512
            xh_ap = self.v_ap("xnh").rearrange("p (k w) -> p k w", k=NCH)
            xl_ap = self.v_ap("xnl").rearrange("p (k w) -> p k w", k=NCH)
            self.load(xh_ap, XNH.ap().rearrange("c p t -> p c t")[:, :, 1 + t0:1 + t0 + 512],
                      bXNH, self.v_buf("xnh"))
            self.load(xl_ap, XNL.ap().rearrange("c p t -> p c t")[:, :, 1 + t0:1 + t0 + 512],
                      bXNL, self.v_buf("xnl"), q="S")
            self.load(self.v_ap("cosb"), self.c_cos.ap()[:, t0:t0 + 512], Buf(""), self.v_buf("cosb"))
            self.load(self.v_ap("sinb"), self.c_sin.ap()[:, t0:t0 + 512], Buf(""), self.v_buf("sinb"))
            for h in range(HEAD128):
                self.wcol_fm("wk_a", h * 128, 128, "wst", "wh", "wl")
                pairs = self.mm3_pairs("wh", "wl", "xnh", "xnl", NCH)
                self.mm3(self.ps_ap(0), pairs,
                         [self.v_buf("wh"), self.v_buf("wl"), self.v_buf("xnh"), self.v_buf("xnl")],
                         self.psb[0])
                t1, t2, t3 = self.v_ap("t1"), self.v_ap("t2"), self.v_ap("t3")
                b1, b2, b3 = self.v_buf("t1"), self.v_buf("t2"), self.v_buf("t3")
                cos_ap, sin_ap = self.v_ap("cosb"), self.v_ap("sinb")
                self.rope_swap(0, t1, b1, t2, b2, t3, b3, cos_ap, sin_ap)
                self.split(self.v_ap("sh"), self.v_ap("sl"), t3, [b3],
                           self.v_buf("sh"), self.v_buf("sl"))
                self.store(self.dram["KF_H"].ap()[h, :, t0:t0 + 512], self.v_ap("sh"),
                           self.v_buf("sh"), self.dbuf("KF_H"))
                self.store(self.dram["KF_L"].ap()[h, :, t0:t0 + 512], self.v_ap("sl"),
                           self.v_buf("sl"), self.dbuf("KF_L"), q="S")
            for of in range(4):
                for kh_ in range(4):
                    self.wtile8("wv_a", kh_ * 4, of * 512, "vwst", "vwh", "vwl")
                    vwh = self.v_ap("vwh").rearrange("p (k w) -> p k w", k=4)
                    vwl = self.v_ap("vwl").rearrange("p (k w) -> p k w", k=4)
                    for mt in range(4):
                        msl = slice(mt * 128, mt * 128 + 128)
                        pairs = [(xh_ap[:, kh_ * 4 + kc, msl], xl_ap[:, kh_ * 4 + kc, msl],
                                  vwh[:, kc, :], vwl[:, kc, :]) for kc in range(4)]
                        self.mm3(self.ps_ap(2 + mt), pairs,
                                 [self.v_buf("vwh"), self.v_buf("vwl"),
                                  self.v_buf("xnh"), self.v_buf("xnl")],
                                 self.psb[2 + mt], start=(kh_ == 0), stop=(kh_ == 3))
                for mt in range(4):
                    self.split(self.v_ap("sh"), self.v_ap("sl"), self.ps_ap(2 + mt),
                               [self.psb[2 + mt]], self.v_buf("sh"), self.v_buf("sl"))
                    r0 = t0 + mt * 128
                    self.store(self.dram["VT_H"].ap()[r0:r0 + 128, of * 512:of * 512 + 512],
                               self.v_ap("sh"), self.v_buf("sh"), self.dbuf("VT_H"))
                    self.store(self.dram["VT_L"].ap()[r0:r0 + 128, of * 512:of * 512 + 512],
                               self.v_ap("sl"), self.v_buf("sl"), self.dbuf("VT_L"), q="S")

        self.stage([
            ("xnh", [128, NCH * 512], F32R), ("xnl", [128, NCH * 512], F32R),
            ("wst", [128, 2048], F32), ("wh", [128, 2048], F32R), ("wl", [128, 2048], F32R),
            ("kh", [128, 2048], F32R), ("kl", [128, 2048], F32R),
            ("vh", [128, 16 * 128], F32R), ("vl", [128, 16 * 128], F32R),
            ("qh", [128, 512], F32R), ("ql", [128, 512], F32R),
            ("cosb", [128, 512], F32), ("sinb", [128, 512], F32),
            ("t1", [128, 512], F32), ("t2", [128, 512], F32), ("t3", [128, 512], F32),
            ("t4", [128, 512], F32), ("drec", [128, 512], F32),
            ("sh", [128, 512], F32R), ("sl", [128, 512], F32R),
            ("oh", [128, 512], F32R), ("ol", [128, 512], F32R),
        ])
        for qb in range(NB):
            q0 = qb * 512
            xh_ap = self.v_ap("xnh").rearrange("p (k w) -> p k w", k=NCH)
            xl_ap = self.v_ap("xnl").rearrange("p (k w) -> p k w", k=NCH)
            self.load(xh_ap, XNH.ap().rearrange("c p t -> p c t")[:, :, 1 + q0:1 + q0 + 512],
                      bXNH, self.v_buf("xnh"))
            self.load(xl_ap, XNL.ap().rearrange("c p t -> p c t")[:, :, 1 + q0:1 + q0 + 512],
                      bXNL, self.v_buf("xnl"), q="S")
            self.load(self.v_ap("cosb"), self.c_cos.ap()[:, q0:q0 + 512], Buf(""), self.v_buf("cosb"))
            self.load(self.v_ap("sinb"), self.c_sin.ap()[:, q0:q0 + 512], Buf(""), self.v_buf("sinb"))
            for h in range(HEAD128):
                self.wcol_fm("wq_a", h * 128, 128, "wst", "wh", "wl")
                pairs = self.mm3_pairs("wh", "wl", "xnh", "xnl", NCH)
                self.mm3(self.ps_ap(0), pairs,
                         [self.v_buf("wh"), self.v_buf("wl"), self.v_buf("xnh"), self.v_buf("xnl")],
                         self.psb[0])
                t1, t2, t3, t4 = (self.v_ap(x) for x in ("t1", "t2", "t3", "t4"))
                b1, b2, b3, b4 = (self.v_buf(x) for x in ("t1", "t2", "t3", "t4"))
                cos_ap, sin_ap = self.v_ap("cosb"), self.v_ap("sinb")
                self.rope_swap(0, t1, b1, t2, b2, t3, b3, cos_ap, sin_ap)
                self.split(self.v_ap("qh"), self.v_ap("ql"), t3, [b3],
                           self.v_buf("qh"), self.v_buf("ql"))
                first_av = True
                nkt = 4 * (qb + 1)
                self.load(self.v_ap("kh")[:, 0:512 * (qb + 1)],
                          self.dram["KF_H"].ap()[h, :, 0:512 * (qb + 1)],
                          self.dbuf("KF_H"), self.v_buf("kh"))
                self.load(self.v_ap("kl")[:, 0:512 * (qb + 1)],
                          self.dram["KF_L"].ap()[h, :, 0:512 * (qb + 1)],
                          self.dbuf("KF_L"), self.v_buf("kl"), q="S")
                vh_ap = self.v_ap("vh").rearrange("p (k w) -> p k w", k=16)
                vl_ap = self.v_ap("vl").rearrange("p (k w) -> p k w", k=16)
                self.load(vh_ap[:, 0:nkt, :], self.dram["VT_H"].ap()
                          .rearrange("(kt p) d -> p kt d", p=128)[:, 0:nkt, h * 128:h * 128 + 128],
                          self.dbuf("VT_H"), self.v_buf("vh"))
                self.load(vl_ap[:, 0:nkt, :], self.dram["VT_L"].ap()
                          .rearrange("(kt p) d -> p kt d", p=128)[:, 0:nkt, h * 128:h * 128 + 128],
                          self.dbuf("VT_L"), self.v_buf("vl"), q="S")
                for kb2 in range(qb + 1):
                    k0 = kb2 * 512
                    for kt in range(4):
                        gkt = 4 * kb2 + kt
                        sb_ = 4 + (kt % 2)
                        pairs = [(self.v_ap("kh")[:, gkt * 128:gkt * 128 + 128],
                                  self.v_ap("kl")[:, gkt * 128:gkt * 128 + 128],
                                  self.v_ap("qh"), self.v_ap("ql"))]
                        self.mm3(self.ps_ap(sb_), pairs,
                                 [self.v_buf("kh"), self.v_buf("kl"), self.v_buf("qh"), self.v_buf("ql")],
                                 self.psb[sb_])
                        glob_kt = 4 * kb2 + kt
                        sub = glob_kt - 4 * qb
                        self.sop(lambda e, sb_=sb_, t1=t1: e.activation(t1, self.ps_ap(sb_), AF.Square),
                                 [self.psb[sb_]], [b1])
                        if 0 <= sub < 4:
                            mask_ap = self.k_mm.ap()[:, sub * 512:sub * 512 + 512]
                            self.vop(lambda e, t1=t1, mask_ap=mask_ap: e.tensor_tensor(t1, t1, mask_ap, ALU.mult),
                                     [b1, self.sbbuf("k_mm")], [b1])
                        self.sop(lambda e, t1=t1, t2=t2: e.activation(t2, t1, AF.Square),
                                 [b1], [b2])
                        self.split(self.v_ap("sh"), self.v_ap("sl"), t2, [b2],
                                   self.v_buf("sh"), self.v_buf("sl"))
                        pairs = [(vh_ap[:, gkt, :], vl_ap[:, gkt, :],
                                  self.v_ap("sh"), self.v_ap("sl"))]
                        last_av = (kb2 == qb and kt == 3)
                        self.mm3(self.ps_ap(2), pairs,
                                 [self.v_buf("vh"), self.v_buf("vl"), self.v_buf("sh"), self.v_buf("sl")],
                                 self.psb[2], start=first_av, stop=last_av)
                        self.mm1(self.ps_ap(3), [(self.k_ones.ap(), self.v_ap("sh")),
                                                 (self.k_ones.ap(), self.v_ap("sl"))],
                                 [self.sbbuf("k_ones"), self.v_buf("sh"), self.v_buf("sl")],
                                 self.psb[3], start=first_av, stop=last_av)
                        first_av = False
                drec, bdr = self.v_ap("drec"), self.v_buf("drec")
                def dn(e, drec=drec):
                    e.tensor_scalar(drec, self.ps_ap(3), float(EPS_A), None, ALU.max)
                    e.drain()
                    return e.reciprocal(drec, drec)
                self.vop(dn, [self.psb[3]], [bdr])
                self.vop(lambda e, t4=t4, drec=drec: e.tensor_tensor(t4, self.ps_ap(2), drec, ALU.mult),
                         [self.psb[2], bdr], [b4])
                self.split(self.v_ap("oh"), self.v_ap("ol"), t4, [b4],
                           self.v_buf("oh"), self.v_buf("ol"))
                self.store(self.dram["AOF_H"].ap()[h, :, q0:q0 + 512], self.v_ap("oh"),
                           self.v_buf("oh"), self.dbuf("AOF_H"))
                self.store(self.dram["AOF_L"].ap()[h, :, q0:q0 + 512], self.v_ap("ol"),
                           self.v_buf("ol"), self.dbuf("AOF_L"), q="S")

        self.pass_out_proj("AOF_H", "AOF_L", "wo_a", resid_src, resid_dst)

    def pass_out_proj(self, aof_h, aof_l, wname, resid_src, resid_dst, extra_tm_add=None):
        T = self.T
        self.stage([
            ("aoh", [128, NCH * 512], F32R), ("aol", [128, NCH * 512], F32R),
            ("vwst", [128, 4 * 512], F32), ("vwh", [128, 4 * 512], F32R), ("vwl", [128, 4 * 512], F32R),
            ("rt", [128, 512], F32), ("rt2", [128, 512], F32),
        ])
        AH, AL = self.dram[aof_h], self.dram[aof_l]
        for sc in range(T // 512):
            s0 = sc * 512
            ah = self.v_ap("aoh").rearrange("p (k w) -> p k w", k=NCH)
            al = self.v_ap("aol").rearrange("p (k w) -> p k w", k=NCH)
            off = 1 if AH.shape[2] == self.T + 1 else 0
            self.load(ah, AH.ap().rearrange("c p t -> p c t")[:, :, off + s0: off + s0 + 512],
                      self.dbuf(aof_h), self.v_buf("aoh"))
            self.load(al, AL.ap().rearrange("c p t -> p c t")[:, :, off + s0: off + s0 + 512],
                      self.dbuf(aof_l), self.v_buf("aol"), q="S")
            for of in range(4):
                for kh_ in range(4):
                    self.wtile8(wname, kh_ * 4, of * 512, "vwst", "vwh", "vwl")
                    vwh = self.v_ap("vwh").rearrange("p (k w) -> p k w", k=4)
                    vwl = self.v_ap("vwl").rearrange("p (k w) -> p k w", k=4)
                    for mt in range(4):
                        msl = slice(mt * 128, mt * 128 + 128)
                        pairs = [(ah[:, kh_ * 4 + kc, msl], al[:, kh_ * 4 + kc, msl],
                                  vwh[:, kc, :], vwl[:, kc, :]) for kc in range(4)]
                        self.mm3(self.ps_ap(2 + mt), pairs,
                                 [self.v_buf("vwh"), self.v_buf("vwl"),
                                  self.v_buf("aoh"), self.v_buf("aol")],
                                 self.psb[2 + mt], start=(kh_ == 0), stop=(kh_ == 3))
                for mt in range(4):
                    r0 = s0 + mt * 128
                    rt, brt = self.v_ap("rt"), self.v_buf("rt")
                    self.load(rt, self.dram[resid_src].ap()[r0:r0 + 128, of * 512:of * 512 + 512],
                              self.dbuf(resid_src), brt)
                    rt2, brt2 = self.v_ap("rt2"), self.v_buf("rt2")
                    if extra_tm_add is not None:
                        exn, exoff = extra_tm_add
                        self.load(rt2, self.dram[exn].ap()[exoff + r0: exoff + r0 + 128,
                                                           of * 512:of * 512 + 512],
                                  self.dbuf(exn), brt2, q="S")
                        def ev(e, mt=mt, rt=rt, rt2=rt2):
                            e.tensor_tensor(rt, rt, rt2, ALU.add)
                            e.drain()
                            return e.tensor_tensor(rt, rt, self.ps_ap(2 + mt), ALU.add)
                        self.vop(ev, [self.psb[2 + mt], brt, brt2], [brt])
                    else:
                        self.vop(lambda e, mt=mt, rt=rt: e.tensor_tensor(rt, rt, self.ps_ap(2 + mt), ALU.add),
                                 [self.psb[2 + mt], brt], [brt])
                    self.store(self.dram[resid_dst].ap()[r0:r0 + 128, of * 512:of * 512 + 512],
                               rt, brt, self.dbuf(resid_dst))

    def stage_woa(self):
        self.k_m = self.sbuf_t("k_m", [128, 256], F32)
        self.k_mh = self.sbuf_t("k_mh", [128, 256], F32R)
        self.k_ml = self.sbuf_t("k_ml", [128, 256], F32R)
        self.stage([
            ("wst", [128, 128], F32), ("wh", [128, 128], F32R), ("wl", [128, 128], F32R),
            ("ast", [128, 16 * 16], F32), ("ah", [128, 256], F32R), ("al", [128, 256], F32R),
        ])
        a3 = self.dram["a_f"].ap().rearrange("(k p) r -> p k r", p=128)
        self.load(self.v_ap("ast").rearrange("p (k r) -> p k r", k=16), a3,
                  self.dbuf("a_f"), self.v_buf("ast"))
        self.split(self.v_ap("ah"), self.v_ap("al"), self.v_ap("ast"),
                   [self.v_buf("ast")], self.v_buf("ah"), self.v_buf("al"))
        ah = self.v_ap("ah").rearrange("p (k r) -> p k r", k=16)
        al = self.v_ap("al").rearrange("p (k r) -> p k r", k=16)
        for ic in range(16):
            for oc in range(16):
                self.wtile("wo_f_nat", oc * 128, 128, ic * 128, 128, "wst", "wh", "wl")
                pairs = [(self.v_ap("wh")[:, 0:128], self.v_ap("wl")[:, 0:128],
                          ah[:, oc, :], al[:, oc, :])]
                self.mm3(self.ps_ap(0, 16, ic * 16), pairs,
                         [self.v_buf("wh"), self.v_buf("wl"), self.v_buf("ah"), self.v_buf("al")],
                         self.psb[0], start=(oc == 0), stop=(oc == 15))
            self.vop(lambda e, ic=ic: e.tensor_copy(self.k_m.ap()[:, ic * 16:ic * 16 + 16],
                                                    self.ps_ap(0, 16, ic * 16)),
                     [self.psb[0]], [self.sbbuf("k_m")])
        self.split(self.k_mh.ap(), self.k_ml.ap(), self.k_m.ap(),
                   [self.sbbuf("k_m")], self.sbbuf("k_mh"), self.sbbuf("k_ml"))

    def stage_rms_base(self):
        T = self.T
        layout = [(f"x{tt}", [128, D], F32) for tt in range(4)]
        layout += [("sq", [128, D], F32), ("st", [128, 8], F32),
                   ("tf", [128, 512], F32),
                   ("th", [128, 512], F32R), ("tl", [128, 512], F32R),
                   ("w1c", [128, 16], F32), ("w2c", [128, 16], F32)]
        self.stage(layout)
        self._row_to_col16("ln1_row", self.v_ap("w1c"), self.v_buf("w1c"))
        self._row_to_col16("ln2_row", self.v_ap("w2c"), self.v_buf("w2c"))
        XN = self.dram["XN"]
        for g in range(T // 512):
            for tt in range(4):
                t0 = g * 512 + tt * 128
                vx = f"x{tt}"
                self.load(self.v_ap(vx), XN.ap()[1 + t0:1 + t0 + 128, :], self.dbuf("XN"),
                          self.v_buf(vx))
                self.ln_norm_tile(vx, "sq", "st", EPS_RMS, kind="rms")
            for c in range(NCH):
                bank = c % 2
                srcs = [self.v_ap(f"x{tt}")[:, c * 128:(c + 1) * 128] for tt in range(4)]
                self.transpose_group(bank, srcs, [self.v_buf(f"x{tt}") for tt in range(4)])
                tf, th, tl = self.v_ap("tf"), self.v_ap("th"), self.v_ap("tl")
                for (wv, dh, dl, qq) in (("w1c", "BF_H", "BF_L", "G"),
                                         ("w2c", "BF2_H", "BF2_L", "S")):
                    wap = self.v_ap(wv)
                    self.vop(lambda e, tf=tf, bank=bank, wap=wap, c=c: e.tensor_scalar(
                        tf, self.ps_ap(bank), wap[:, c:c + 1], None, ALU.mult),
                        [self.psb[bank], self.v_buf(wv)], [self.v_buf("tf")])
                    self.split(th, tl, tf, [self.v_buf("tf")],
                               self.v_buf("th"), self.v_buf("tl"))
                    self.store(self.dram[dh].ap()[c, :, g * 512: g * 512 + 512], th,
                               self.v_buf("th"), self.dbuf(dh), q=qq)
                    self.store(self.dram[dl].ap()[c, :, g * 512: g * 512 + 512], tl,
                               self.v_buf("tl"), self.dbuf(dl), q=qq)

    def _row_to_col16(self, row_name, dst_ap, dst_buf):
        src = self.dram[row_name].ap()[0:1, :].rearrange("o (c p) -> (o p) c", p=128)
        self.load(dst_ap, src, self.dbuf(row_name), dst_buf, slow=True)

    def stage_ffn_attn(self):
        T = self.T
        NB = T // 512
        BFH, BFL = self.dram["BF_H"], self.dram["BF_L"]
        bBFH, bBFL = self.dbuf("BF_H"), self.dbuf("BF_L")
        self.stage([
            ("xnh", [128, NCH * 512], F32R), ("xnl", [128, NCH * 512], F32R),
            ("wst", [128, 2048], F32), ("wh", [128, 2048], F32R), ("wl", [128, 2048], F32R),
            ("vwst", [128, 4 * 512], F32), ("vwh", [128, 4 * 512], F32R), ("vwl", [128, 4 * 512], F32R),
            ("t1", [128, 512], F32), ("t1r", [128, 512], F32R),
            ("sh", [128, 512], F32R), ("sl", [128, 512], F32R),
            ("kssb", [128, 1], F32),
        ])
        self.k_kss = getattr(self, "k_kss", None) or self.sbuf_t("k_kss", [128, 16], F32)
        self.vop(lambda e: e.memset(self.k_kss.ap(), 0.0), [], [self.sbbuf("k_kss")])
        for kb in range(NB):
            t0 = kb * 512
            xh_ap = self.v_ap("xnh").rearrange("p (k w) -> p k w", k=NCH)
            xl_ap = self.v_ap("xnl").rearrange("p (k w) -> p k w", k=NCH)
            self.load(xh_ap, BFH.ap().rearrange("c p t -> p c t")[:, :, t0:t0 + 512],
                      bBFH, self.v_buf("xnh"))
            self.load(xl_ap, BFL.ap().rearrange("c p t -> p c t")[:, :, t0:t0 + 512],
                      bBFL, self.v_buf("xnl"), q="S")
            for h in range(HEAD128):
                self.wcol_fm("wk_f", h * 128, 128, "wst", "wh", "wl")
                pairs = self.mm3_pairs("wh", "wl", "xnh", "xnl", NCH)
                self.mm3(self.ps_ap(0), pairs,
                         [self.v_buf("wh"), self.v_buf("wl"), self.v_buf("xnh"), self.v_buf("xnl")],
                         self.psb[0])
                t1, b1 = self.v_ap("t1"), self.v_buf("t1")
                t1r, b1r = self.v_ap("t1r"), self.v_buf("t1r")
                self.sop(lambda e, t1=t1: e.activation(t1, self.ps_ap(0), AF.Square),
                         [self.psb[0]], [b1])
                self.vop(lambda e, t1=t1, t1r=t1r: e.tensor_copy(t1r, t1), [b1], [b1r])
                self.mm1(self.ps_ap(1), [(self.k_ones.ap(), t1r)],
                         [self.sbbuf("k_ones"), b1r], self.psb[1])
                kssb, bkss = self.v_ap("kssb"), self.v_buf("kssb")
                self.vop(lambda e, kssb=kssb: e.tensor_reduce(kssb, self.ps_ap(1), AX.X, ALU.max),
                         [self.psb[1]], [bkss])
                kap = self.k_kss.ap()[:, h:h + 1]
                self.vop(lambda e, kssb=kssb, kap=kap: e.tensor_tensor(kap, kap, kssb, ALU.max),
                         [bkss, self.sbbuf("k_kss")], [self.sbbuf("k_kss")])
                self.split(self.v_ap("sh"), self.v_ap("sl"), self.ps_ap(0),
                           [self.psb[0]], self.v_buf("sh"), self.v_buf("sl"))
                self.store(self.dram["KF_H"].ap()[h, :, t0:t0 + 512], self.v_ap("sh"),
                           self.v_buf("sh"), self.dbuf("KF_H"))
                self.store(self.dram["KF_L"].ap()[h, :, t0:t0 + 512], self.v_ap("sl"),
                           self.v_buf("sl"), self.dbuf("KF_L"), q="S")
            for of in range(4):
                for kh_ in range(4):
                    self.wtile8("wv_f", kh_ * 4, of * 512, "vwst", "vwh", "vwl")
                    vwh = self.v_ap("vwh").rearrange("p (k w) -> p k w", k=4)
                    vwl = self.v_ap("vwl").rearrange("p (k w) -> p k w", k=4)
                    for mt in range(4):
                        msl = slice(mt * 128, mt * 128 + 128)
                        pairs = [(xh_ap[:, kh_ * 4 + kc, msl], xl_ap[:, kh_ * 4 + kc, msl],
                                  vwh[:, kc, :], vwl[:, kc, :]) for kc in range(4)]
                        self.mm3(self.ps_ap(2 + mt), pairs,
                                 [self.v_buf("vwh"), self.v_buf("vwl"),
                                  self.v_buf("xnh"), self.v_buf("xnl")],
                                 self.psb[2 + mt], start=(kh_ == 0), stop=(kh_ == 3))
                for mt in range(4):
                    self.split(self.v_ap("sh"), self.v_ap("sl"), self.ps_ap(2 + mt),
                               [self.psb[2 + mt]], self.v_buf("sh"), self.v_buf("sl"))
                    r0 = t0 + mt * 128
                    self.store(self.dram["VT_H"].ap()[r0:r0 + 128, of * 512:of * 512 + 512],
                               self.v_ap("sh"), self.v_buf("sh"), self.dbuf("VT_H"))
                    self.store(self.dram["VT_L"].ap()[r0:r0 + 128, of * 512:of * 512 + 512],
                               self.v_ap("sl"), self.v_buf("sl"), self.dbuf("VT_L"), q="S")
        self.stage([
            ("xnh", [128, NCH * 512], F32R), ("xnl", [128, NCH * 512], F32R),
            ("wst", [128, 2048], F32), ("wh", [128, 2048], F32R), ("wl", [128, 2048], F32R),
            ("kh", [128, 2048], F32R), ("kl", [128, 2048], F32R),
            ("vh", [128, 16 * 128], F32R), ("vl", [128, 16 * 128], F32R),
            ("qh", [128, 512], F32R), ("ql", [128, 512], F32R),
            ("t1", [128, 512], F32), ("t2", [128, 512], F32),
            ("t4", [128, 512], F32), ("drec", [128, 512], F32),
            ("sh", [128, 512], F32R), ("sl", [128, 512], F32R),
            ("oh", [128, 512], F32R), ("ol", [128, 512], F32R),
            ("t1r", [128, 512], F32R), ("bnd", [128, 8], F32),
        ])
        for qb in range(NB):
            q0 = qb * 512
            xh_ap = self.v_ap("xnh").rearrange("p (k w) -> p k w", k=NCH)
            xl_ap = self.v_ap("xnl").rearrange("p (k w) -> p k w", k=NCH)
            self.load(xh_ap, BFH.ap().rearrange("c p t -> p c t")[:, :, q0:q0 + 512],
                      bBFH, self.v_buf("xnh"))
            self.load(xl_ap, BFL.ap().rearrange("c p t -> p c t")[:, :, q0:q0 + 512],
                      bBFL, self.v_buf("xnl"), q="S")
            for h in range(HEAD128):
                self.wcol_fm("wq_f", h * 128, 128, "wst", "wh", "wl", scale_const=RSQ128)
                pairs = self.mm3_pairs("wh", "wl", "xnh", "xnl", NCH)
                self.mm3(self.ps_ap(0), pairs,
                         [self.v_buf("wh"), self.v_buf("wl"), self.v_buf("xnh"), self.v_buf("xnl")],
                         self.psb[0])
                t1, b1 = self.v_ap("t1"), self.v_buf("t1")
                t2, b2 = self.v_ap("t2"), self.v_buf("t2")
                t4, b4 = self.v_ap("t4"), self.v_buf("t4")
                t1r, b1r = self.v_ap("t1r"), self.v_buf("t1r")
                bnd, bbnd = self.v_ap("bnd"), self.v_buf("bnd")
                self.sop(lambda e, t1=t1: e.activation(t1, self.ps_ap(0), AF.Square),
                         [self.psb[0]], [b1])
                self.vop(lambda e, t1=t1, t1r=t1r: e.tensor_copy(t1r, t1), [b1], [b1r])
                self.mm1(self.ps_ap(1), [(self.k_ones.ap(), t1r)],
                         [self.sbbuf("k_ones"), b1r], self.psb[1])
                kap = self.k_kss.ap()[:, h:h + 1]
                def mkbound(e, bnd=bnd, kap=kap):
                    e.tensor_reduce(bnd[:, 0:1], self.ps_ap(1), AX.X, ALU.max)
                    e.drain()
                    e.tensor_tensor(bnd[:, 1:2], bnd[:, 0:1], kap, ALU.mult)
                    return None
                self.vop(mkbound, [self.psb[1], self.sbbuf("k_kss")], [bbnd])
                def mkbound2(e, bnd=bnd):
                    e.activation(bnd[:, 2:3], bnd[:, 1:2], AF.Ln, bias=1e-30)
                    e.drain()
                    e.activation(bnd[:, 3:4], bnd[:, 2:3], AF.Exp, scale=0.5)
                    e.drain()
                    return None
                self.sop(mkbound2, [bbnd], [bbnd])
                self.vop(lambda e, bnd=bnd: e.tensor_scalar(bnd[:, 4:5], bnd[:, 3:4],
                                                            -1.0001, -0.01, ALU.mult, ALU.add),
                         [bbnd], [bbnd])
                self.split(self.v_ap("qh"), self.v_ap("ql"), self.ps_ap(0),
                           [self.psb[0]], self.v_buf("qh"), self.v_buf("ql"))
                first_av = True
                nkt = 4 * (qb + 1)
                self.load(self.v_ap("kh")[:, 0:512 * (qb + 1)],
                          self.dram["KF_H"].ap()[h, :, 0:512 * (qb + 1)],
                          self.dbuf("KF_H"), self.v_buf("kh"))
                self.load(self.v_ap("kl")[:, 0:512 * (qb + 1)],
                          self.dram["KF_L"].ap()[h, :, 0:512 * (qb + 1)],
                          self.dbuf("KF_L"), self.v_buf("kl"), q="S")
                vh_ap = self.v_ap("vh").rearrange("p (k w) -> p k w", k=16)
                vl_ap = self.v_ap("vl").rearrange("p (k w) -> p k w", k=16)
                self.load(vh_ap[:, 0:nkt, :], self.dram["VT_H"].ap()
                          .rearrange("(kt p) d -> p kt d", p=128)[:, 0:nkt, h * 128:h * 128 + 128],
                          self.dbuf("VT_H"), self.v_buf("vh"))
                self.load(vl_ap[:, 0:nkt, :], self.dram["VT_L"].ap()
                          .rearrange("(kt p) d -> p kt d", p=128)[:, 0:nkt, h * 128:h * 128 + 128],
                          self.dbuf("VT_L"), self.v_buf("vl"), q="S")
                for kb2 in range(qb + 1):
                    k0 = kb2 * 512
                    for kt in range(4):
                        gkt = 4 * kb2 + kt
                        sb_ = 4 + (kt % 2)
                        pairs = [(self.v_ap("kh")[:, gkt * 128:gkt * 128 + 128],
                                  self.v_ap("kl")[:, gkt * 128:gkt * 128 + 128],
                                  self.v_ap("qh"), self.v_ap("ql"))]
                        self.mm3(self.ps_ap(sb_), pairs,
                                 [self.v_buf("kh"), self.v_buf("kl"), self.v_buf("qh"), self.v_buf("ql")],
                                 self.psb[sb_])
                        glob_kt = 4 * kb2 + kt
                        sub = glob_kt - 4 * qb
                        nb_ap = bnd[:, 4:5]
                        if 0 <= sub < 4:
                            mask_ap = self.k_am.ap()[:, sub * 512:sub * 512 + 512]
                            self.vop(lambda e, sb_=sb_, t1=t1, mask_ap=mask_ap:
                                     e.tensor_tensor(t1, self.ps_ap(sb_), mask_ap, ALU.add),
                                     [self.psb[sb_], self.sbbuf("k_am")], [b1])
                            self.sop(lambda e, t1=t1, t2=t2, nb_ap=nb_ap:
                                     e.activation(t2, t1, AF.Exp, bias=nb_ap),
                                     [b1, bbnd], [b2])
                        else:
                            self.sop(lambda e, sb_=sb_, t2=t2, nb_ap=nb_ap:
                                     e.activation(t2, self.ps_ap(sb_), AF.Exp, bias=nb_ap),
                                     [self.psb[sb_], bbnd], [b2])
                        self.split(self.v_ap("sh"), self.v_ap("sl"), t2, [b2],
                                   self.v_buf("sh"), self.v_buf("sl"))
                        last_av = (kb2 == qb and kt == 3)
                        pairs = [(vh_ap[:, gkt, :], vl_ap[:, gkt, :],
                                  self.v_ap("sh"), self.v_ap("sl"))]
                        self.mm3(self.ps_ap(2), pairs,
                                 [self.v_buf("vh"), self.v_buf("vl"), self.v_buf("sh"), self.v_buf("sl")],
                                 self.psb[2], start=first_av, stop=last_av)
                        self.mm1(self.ps_ap(3), [(self.k_ones.ap(), self.v_ap("sh")),
                                                 (self.k_ones.ap(), self.v_ap("sl"))],
                                 [self.sbbuf("k_ones"), self.v_buf("sh"), self.v_buf("sl")],
                                 self.psb[3], start=first_av, stop=last_av)
                        first_av = False
                drec, bdr = self.v_ap("drec"), self.v_buf("drec")
                def dn(e, drec=drec):
                    e.tensor_scalar(drec, self.ps_ap(3), float(EPS_A), None, ALU.max)
                    e.drain()
                    return e.reciprocal(drec, drec)
                self.vop(dn, [self.psb[3]], [bdr])
                self.vop(lambda e, t4=t4, drec=drec: e.tensor_tensor(t4, self.ps_ap(2), drec, ALU.mult),
                         [self.psb[2], bdr], [b4])
                self.split(self.v_ap("oh"), self.v_ap("ol"), t4, [b4],
                           self.v_buf("oh"), self.v_buf("ol"))
                self.store(self.dram["AOF_H"].ap()[h, :, q0:q0 + 512], self.v_ap("oh"),
                           self.v_buf("oh"), self.dbuf("AOF_H"))
                self.store(self.dram["AOF_L"].ap()[h, :, q0:q0 + 512], self.v_ap("ol"),
                           self.v_buf("ol"), self.dbuf("AOF_L"), q="S")

    def stage_modz(self):
        T = self.T
        self.stage([
            ("aoh", [128, NCH * 512], F32R), ("aol", [128, NCH * 512], F32R),
            ("bfh", [128, NCH * 512], F32R), ("bfl", [128, NCH * 512], F32R),
            ("ast", [128, 16 * 16], F32), ("ah", [128, 256], F32R), ("al", [128, 256], F32R),
            ("tz", [16, 512], F32), ("tm", [16, 512], F32),
            ("zh", [16, 512], F32R), ("zl", [16, 512], F32R),
        ])
        a3 = self.dram["a_f"].ap().rearrange("(k p) r -> p k r", p=128)
        ast = self.v_ap("ast").rearrange("p (k r) -> p k r", k=16)
        self.load(ast, a3, self.dbuf("a_f"), self.v_buf("ast"))
        self.split(self.v_ap("ah"), self.v_ap("al"), self.v_ap("ast"),
                   [self.v_buf("ast")], self.v_buf("ah"), self.v_buf("al"))
        ah = self.v_ap("ah").rearrange("p (k r) -> p k r", k=16)
        al = self.v_ap("al").rearrange("p (k r) -> p k r", k=16)
        mh = self.k_mh.ap().rearrange("p (k r) -> p k r", k=16)
        ml = self.k_ml.ap().rearrange("p (k r) -> p k r", k=16)
        AOH, AOL = self.dram["AOF_H"], self.dram["AOF_L"]
        for qb in range(T // 512):
            q0 = qb * 512
            aoh = self.v_ap("aoh").rearrange("p (k w) -> p k w", k=NCH)
            aol = self.v_ap("aol").rearrange("p (k w) -> p k w", k=NCH)
            bfh = self.v_ap("bfh").rearrange("p (k w) -> p k w", k=NCH)
            bfl = self.v_ap("bfl").rearrange("p (k w) -> p k w", k=NCH)
            self.load(aoh, AOH.ap().rearrange("c p t -> p c t")[:, :, q0:q0 + 512],
                      self.dbuf("AOF_H"), self.v_buf("aoh"))
            self.load(aol, AOL.ap().rearrange("c p t -> p c t")[:, :, q0:q0 + 512],
                      self.dbuf("AOF_L"), self.v_buf("aol"), q="S")
            self.load(bfh, self.dram["BF2_H"].ap().rearrange("c p t -> p c t")[:, :, q0:q0 + 512],
                      self.dbuf("BF2_H"), self.v_buf("bfh"))
            self.load(bfl, self.dram["BF2_L"].ap().rearrange("c p t -> p c t")[:, :, q0:q0 + 512],
                      self.dbuf("BF2_L"), self.v_buf("bfl"), q="S")
            pairs = [(mh[:, kc, :], ml[:, kc, :], aoh[:, kc, :], aol[:, kc, :])
                     for kc in range(NCH)]
            self.mm3(self.ps_ap_p(0, 0, 16, 512), pairs,
                     [self.sbbuf("k_mh"), self.sbbuf("k_ml"), self.v_buf("aoh"), self.v_buf("aol")],
                     self.psb[0])
            pairs = [(ah[:, kc, :], al[:, kc, :], bfh[:, kc, :], bfl[:, kc, :])
                     for kc in range(NCH)]
            self.mm3(self.ps_ap_p(1, 0, 16, 512), pairs,
                     [self.v_buf("ah"), self.v_buf("al"), self.v_buf("bfh"), self.v_buf("bfl")],
                     self.psb[1])
            tz, btz = self.v_ap("tz"), self.v_buf("tz")
            tm_, btm = self.v_ap("tm"), self.v_buf("tm")
            self.sop(lambda e, tm_=tm_: e.activation(tm_, self.ps_ap_p(0, 0, 16, 512), AF.Tanh),
                     [self.psb[0]], [btm])
            self.vop(lambda e, tz=tz, tm_=tm_: e.tensor_tensor(tz, tm_, self.ps_ap_p(1, 0, 16, 512), ALU.mult),
                     [btm, self.psb[1]], [btz])
            self.split(self.v_ap("zh"), self.v_ap("zl"), tz, [btz],
                       self.v_buf("zh"), self.v_buf("zl"))
            self.store(self.dram["ZF_H"].ap()[:, q0:q0 + 512], self.v_ap("zh"),
                       self.v_buf("zh"), self.dbuf("ZF_H"))
            self.store(self.dram["ZF_L"].ap()[:, q0:q0 + 512], self.v_ap("zl"),
                       self.v_buf("zl"), self.dbuf("ZF_L"), q="S")

    def stage_mlp(self, resid_src, resid_dst):
        T = self.T
        BF_OFF, GU_OFF, ACC_OFF = 0, 64 * 1024, 96 * 1024
        for blk in range(T // 512):
            t0 = blk * 512
            for q in range(4):
                f0 = q * 16
                self.stage([
                    ("bfh", [128, NCH * 512], F32R, BF_OFF),
                    ("bfl", [128, NCH * 512], F32R, BF_OFF + 32 * 1024),
                    ("gu", [128, 16 * 512], F32, GU_OFF),
                    ("w1st", [128, 1024], F32), ("w1h", [128, 1024], F32R), ("w1l", [128, 1024], F32R),
                    ("w2h", [128, 1024], F32R), ("w2l", [128, 1024], F32R),
                    ("bst", [16, 128], F32), ("bh", [16, 128], F32R), ("bl", [16, 128], F32R),
                    ("zh", [16, 512], F32R), ("zl", [16, 512], F32R),
                    ("gax", [128, 512], F32),
                ], keep=(("bfh", "bfl") if q > 0 else ()), auto_base=ACC_OFF + 32 * 1024)
                bfh = self.v_ap("bfh").rearrange("p (k w) -> p k w", k=NCH)
                bfl = self.v_ap("bfl").rearrange("p (k w) -> p k w", k=NCH)
                if q == 0:
                    self.load(bfh, self.dram["BF2_H"].ap().rearrange("c p t -> p c t")[:, :, t0:t0 + 512],
                              self.dbuf("BF2_H"), self.v_buf("bfh"))
                    self.load(bfl, self.dram["BF2_L"].ap().rearrange("c p t -> p c t")[:, :, t0:t0 + 512],
                              self.dbuf("BF2_L"), self.v_buf("bfl"), q="S")
                self.load(self.v_ap("zh"), self.dram["ZF_H"].ap()[:, t0:t0 + 512],
                          self.dbuf("ZF_H"), self.v_buf("zh"))
                self.load(self.v_ap("zl"), self.dram["ZF_L"].ap()[:, t0:t0 + 512],
                          self.dbuf("ZF_L"), self.v_buf("zl"), q="S")
                gu = self.v_ap("gu").rearrange("p (f w) -> p f w", f=16)
                bgu = self.v_buf("gu")
                for fi in range(16):
                    fc = f0 + fi
                    bank_g, bank_u = fi % 2, 2 + fi % 2
                    wbufs = [self.v_buf("w1h"), self.v_buf("w1l"),
                             self.v_buf("bfh"), self.v_buf("bfl")]
                    def hpairs(hname, lname, kc0):
                        wh = self.v_ap(hname).rearrange("p (k w) -> p k w", k=8)
                        wl = self.v_ap(lname).rearrange("p (k w) -> p k w", k=8)
                        return [(wh[:, kc, :], wl[:, kc, :],
                                 bfh[:, kc0 + kc, :], bfl[:, kc0 + kc, :]) for kc in range(8)]
                    self.wcol_fm("win", fc * 128, 128, "w1st", "w1h", "w1l", kcn=8)
                    self.mm3(self.ps_ap(bank_g), hpairs("w1h", "w1l", 0), wbufs,
                             self.psb[bank_g], start=True, stop=False)
                    self.wcol_fm2("win", fc * 128, 128, "w1st", "w1h", "w1l", kc0=8)
                    self.wtile_b("b_f", fc, "bst", "bh", "bl")
                    pairs = hpairs("w1h", "w1l", 8)
                    pairs.append((self.v_ap("bh")[0:16, 0:128], self.v_ap("bl")[0:16, 0:128],
                                  self.v_ap("zh"), self.v_ap("zl")))
                    self.mm3(self.ps_ap(bank_g), pairs,
                             wbufs + [self.v_buf("bh"), self.v_buf("bl"),
                                      self.v_buf("zh"), self.v_buf("zl")],
                             self.psb[bank_g], start=False, stop=True)
                    self.wcol_fm("wup", fc * 128, 128, "w1st", "w2h", "w2l", kcn=8)
                    self.mm3(self.ps_ap(bank_u),
                             [(self.v_ap("w2h").rearrange("p (k w) -> p k w", k=8)[:, kc, :],
                               self.v_ap("w2l").rearrange("p (k w) -> p k w", k=8)[:, kc, :],
                               bfh[:, kc, :], bfl[:, kc, :]) for kc in range(8)],
                             [self.v_buf("w2h"), self.v_buf("w2l"), self.v_buf("bfh"), self.v_buf("bfl")],
                             self.psb[bank_u], start=True, stop=False)
                    self.wcol_fm2("wup", fc * 128, 128, "w1st", "w2h", "w2l", kc0=8)
                    self.mm3(self.ps_ap(bank_u),
                             [(self.v_ap("w2h").rearrange("p (k w) -> p k w", k=8)[:, kc, :],
                               self.v_ap("w2l").rearrange("p (k w) -> p k w", k=8)[:, kc, :],
                               bfh[:, 8 + kc, :], bfl[:, 8 + kc, :]) for kc in range(8)],
                             [self.v_buf("w2h"), self.v_buf("w2l"), self.v_buf("bfh"), self.v_buf("bfl")],
                             self.psb[bank_u], start=False, stop=True)
                    gax, bgax = self.v_ap("gax"), self.v_buf("gax")
                    self.sop(lambda e, gax=gax, bank_g=bank_g: e.activation(gax, self.ps_ap(bank_g), AF.Silu),
                             [self.psb[bank_g]], [bgax])
                    self.vop(lambda e, fi=fi, gax=gax, bank_u=bank_u, gu=gu:
                             e.tensor_tensor(gu[:, fi, :], self.ps_ap(bank_u), gax, ALU.mult),
                             [self.psb[bank_u], bgax], [bgu])
                self.stage([
                    ("bfh", [128, NCH * 512], F32R, BF_OFF),
                    ("bfl", [128, NCH * 512], F32R, BF_OFF + 32 * 1024),
                    ("gu", [128, 16 * 512], F32, GU_OFF),
                    ("acc", [128, 4 * 2048], F32, ACC_OFF),
                    ("dwst", [128, 1024], F32), ("dwh", [128, 1024], F32R), ("dwl", [128, 1024], F32R),
                    ("guh", [128, 512], F32R), ("gul", [128, 512], F32R),
                    ("rt", [128, 512], F32), ("rt2", [128, 512], F32),
                ], keep=("gu", "bfh", "bfl") + (("acc",) if q > 0 else ()),
                    auto_base=ACC_OFF + 32 * 1024)
                gu = self.v_ap("gu").rearrange("p (f w) -> p f w", f=16)
                bgu = self.v_buf("gu")
                acc = self.v_ap("acc").rearrange("p (m w) -> p m w", m=4)
                bacc = self.v_buf("acc")
                for ofh in range(2):
                    o0 = ofh * 1024
                    dwh = self.v_ap("dwh").rearrange("p (n w) -> p n w", n=2)
                    dwl = self.v_ap("dwl").rearrange("p (n w) -> p n w", n=2)
                    for fi in range(16):
                        fc = f0 + fi
                        self.wtile("wdn", fc * 128, 128, o0, 1024, "dwst", "dwh", "dwl")
                        self.split(self.v_ap("guh"), self.v_ap("gul"), gu[:, fi, :],
                                   [bgu], self.v_buf("guh"), self.v_buf("gul"))
                        for mt in range(4):
                            for nf in range(2):
                                bank = mt * 2 + nf
                                pairs = [(self.v_ap("guh")[:, mt * 128:mt * 128 + 128],
                                          self.v_ap("gul")[:, mt * 128:mt * 128 + 128],
                                          dwh[:, nf, :], dwl[:, nf, :])]
                                self.mm3(self.ps_ap(bank), pairs,
                                         [self.v_buf("guh"), self.v_buf("gul"),
                                          self.v_buf("dwh"), self.v_buf("dwl")],
                                         self.psb[bank], start=(fi == 0), stop=(fi == 15))
                    for mt in range(4):
                        for nf in range(2):
                            bank = mt * 2 + nf
                            of0 = o0 + nf * 512
                            if q == 0:
                                self.vop(lambda e, mt=mt, of0=of0, bank=bank, acc=acc:
                                         e.tensor_copy(acc[:, mt, of0:of0 + 512], self.ps_ap(bank)),
                                         [self.psb[bank]], [bacc])
                            elif q < 3:
                                self.vop(lambda e, mt=mt, of0=of0, bank=bank, acc=acc:
                                         e.tensor_tensor(acc[:, mt, of0:of0 + 512],
                                                         acc[:, mt, of0:of0 + 512],
                                                         self.ps_ap(bank), ALU.add),
                                         [self.psb[bank], bacc], [bacc])
                            else:
                                r0 = t0 + mt * 128
                                rt, brt = self.v_ap("rt"), self.v_buf("rt")
                                rt2, brt2 = self.v_ap("rt2"), self.v_buf("rt2")
                                self.load(rt, self.dram[resid_src].ap()[r0:r0 + 128, of0:of0 + 512],
                                          self.dbuf(resid_src), brt)
                                self.load(rt2, self.dram["XN"].ap()[1 + r0:1 + r0 + 128, of0:of0 + 512],
                                          self.dbuf("XN"), brt2, q="S")
                                def ev(e, bank=bank, rt=rt, rt2=rt2, mt=mt, of0=of0, acc=acc):
                                    e.tensor_tensor(rt, rt, rt2, ALU.add)
                                    e.drain()
                                    e.tensor_tensor(rt, rt, acc[:, mt, of0:of0 + 512], ALU.add)
                                    e.drain()
                                    return e.tensor_tensor(rt, rt, self.ps_ap(bank), ALU.add)
                                self.vop(ev, [self.psb[bank], brt, brt2, bacc], [brt])
                                self.store(self.dram[resid_dst].ap()[r0:r0 + 128, of0:of0 + 512],
                                           rt, brt, self.dbuf(resid_dst))

    def wcol_fm2(self, wname, j0, w, vst, vhi, vlo, kc0, scale_col=None):
        W = self.dram[wname]
        st_ap = self.v_ap(vst).rearrange("p (k w) -> p k w", k=8)
        src = W.ap().rearrange("(k p) o -> p k o", p=128)[:, kc0:kc0 + 8, j0:j0 + w]
        self.load(st_ap, src, self.dbuf(wname), self.v_buf(vst))
        if scale_col is not None:
            sap, sbuf = scale_col
            def sc(e):
                last = None
                for kc in range(8):
                    last = e.tensor_scalar(st_ap[:, kc, :], st_ap[:, kc, :],
                                           sap[:, kc:kc + 1], None, ALU.mult)
                return last
            self.vop(sc, [self.v_buf(vst), sbuf], [self.v_buf(vst)])
        self.split(self.v_ap(vhi), self.v_ap(vlo), self.v_ap(vst),
                   [self.v_buf(vst)], self.v_buf(vhi), self.v_buf(vlo))

    def wtile_b(self, wname, fc, vst, vhi, vlo):
        W = self.dram[wname]
        st = self.v_ap(vst)
        self.load(st[0:16, :], W.ap()[0:16, fc * 128:fc * 128 + 128], self.dbuf(wname),
                  self.v_buf(vst))
        self.vop(lambda e, st=st: e.tensor_scalar(st[0:16, :], st[0:16, :], 0.1, None, ALU.mult),
                 [self.v_buf(vst)], [self.v_buf(vst)])
        self.split(self.v_ap(vhi)[0:16, :], self.v_ap(vlo)[0:16, :], st[0:16, :],
                   [self.v_buf(vst)], self.v_buf(vhi), self.v_buf(vlo))

    def stage_ffn(self, resid_src, resid_dst):
        self.stage_rms_base()
        self.stage_ffn_attn()
        self.stage_modz()
        self.stage_mlp(resid_src, resid_dst)

    def stage_rwkv(self, resid_src="R", resid_dst="R"):
        self.sc.t_drain = True
        T = self.T
        NCHK = T // 128
        dbg = "DBG1" in self.debug_taps
        if dbg:
            for i in (1, 2, 3, 4):
                self.dram_t(f"DBG{i}", [T, D])
        self.k_S = getattr(self, "k_S", None) or self.sbuf_t("k_S", [128, 1024], F32)
        self.k_Sh = getattr(self, "k_Sh", None) or self.sbuf_t("k_Sh", [128, 1024], F32R)
        self.k_Sl = getattr(self, "k_Sl", None) or self.sbuf_t("k_Sl", [128, 1024], F32R)
        bS, bSh, bSl = self.sbbuf("k_S"), self.sbbuf("k_Sh"), self.sbbuf("k_Sl")
        self.vop(lambda e: e.memset(self.k_S.ap(), 0.0), [], [bS])

        for g in range(T // 256):
            t0 = g * 256
            self.stage([
                ("xf", [128, NCH * 257], F32),
                ("xx", [128, NCH * 256], F32),
                ("xs", [128, NCH * 256], F32),
                ("xsh", [128, NCH * 256], F32R), ("xsl", [128, NCH * 256], F32R),
                ("mc", [128, 96], F32),
                ("w1st", [128, 16 * 160], F32), ("w1h", [128, 16 * 160], F32R), ("w1l", [128, 16 * 160], F32R),
                ("aa", [128, 512], F32), ("aah", [128, 512], F32R), ("aal", [128, 512], F32R),
                ("wb2st", [128, 2048], F32), ("wb2h", [128, 2048], F32R), ("wb2l", [128, 2048], F32R),
                ("vwst", [128, 4 * 512], F32), ("vwh", [128, 4 * 512], F32R), ("vwl", [128, 4 * 512], F32R),
                ("d1", [64, 256], F32), ("d1h", [64, 256], F32R), ("d1l", [64, 256], F32R),
                                ("dkbs", [128, 512], F32),
                ("t1", [128, 512], F32),
            ])
            bXF = self.dbuf("XNF_F")
            xf = self.v_ap("xf").rearrange("p (c w) -> p c w", c=NCH)
            self.load(xf, self.dram["XNF_F"].ap().rearrange("c p t -> p c t")[:, :, t0:t0 + 257],
                      bXF, self.v_buf("xf"))
            self.load(self.v_ap("mc"), self.dram["maa_cols"].ap(), self.dbuf("maa_cols"),
                      self.v_buf("mc"))
            xn = xf[:, :, 1:257]
            xsh_v = xf[:, :, 0:256]
            xx = self.v_ap("xx").rearrange("p (c w) -> p c w", c=NCH)
            self.vop(lambda e, xx=xx, xn=xn, xsh_v=xsh_v: e.tensor_tensor(xx, xsh_v, xn, ALU.subtract),
                     [self.v_buf("xf")], [self.v_buf("xx")])
            xs = self.v_ap("xs").rearrange("p (c w) -> p c w", c=NCH)
            mc = self.v_ap("mc")
            def lerp1(e, mcol, xs=xs, xx=xx, mc=mc):
                last = None
                for c in range(NCH):
                    last = e.tensor_scalar(xs[:, c, :], xx[:, c, :],
                                           mc[:, mcol * 16 + c:mcol * 16 + c + 1], None, ALU.mult)
                return last
            self.vop(lambda e: lerp1(e, 0), [self.v_buf("xx"), self.v_buf("mc")], [self.v_buf("xs")])
            self.vop(lambda e, xs=xs, xn=xn: e.tensor_tensor(xs, xs, xn, ALU.add),
                     [self.v_buf("xf"), self.v_buf("xs")], [self.v_buf("xs")])
            self.split(self.v_ap("xsh"), self.v_ap("xsl"), self.v_ap("xs"),
                       [self.v_buf("xs")], self.v_buf("xsh"), self.v_buf("xsl"))
            self.wcol_fm("maa_w1", 0, 160, "w1st", "w1h", "w1l", kcn=NCH)
            w1h = self.v_ap("w1h").rearrange("p (k w) -> p k w", k=NCH)
            w1l = self.v_ap("w1l").rearrange("p (k w) -> p k w", k=NCH)
            xshr = self.v_ap("xsh").rearrange("p (c w) -> p c w", c=NCH)
            xslr = self.v_ap("xsl").rearrange("p (c w) -> p c w", c=NCH)
            pairsA = [(w1h[:, kc, 0:128], w1l[:, kc, 0:128], xshr[:, kc, :], xslr[:, kc, :])
                      for kc in range(NCH)]
            pairsB = [(w1h[:, kc, 128:160], w1l[:, kc, 128:160], xshr[:, kc, :], xslr[:, kc, :])
                      for kc in range(NCH)]
            wb = [self.v_buf("w1h"), self.v_buf("w1l"), self.v_buf("xsh"), self.v_buf("xsl")]
            self.mm3(self.ps_ap(0, 256), pairsA, wb, self.psb[0])
            self.mm3(self.ps_ap_p(1, 0, 32, 256), pairsB, wb, self.psb[1])
            aa, baa = self.v_ap("aa"), self.v_buf("aa")
            self.sop(lambda e, aa=aa: e.activation(aa[:, 0:256], self.ps_ap(0, 256), AF.Tanh),
                     [self.psb[0]], [baa])
            self.sop(lambda e, aa=aa: e.activation(aa[0:32, 256:512], self.ps_ap_p(1, 0, 32, 256), AF.Tanh),
                     [self.psb[1]], [baa])
            self.split(self.v_ap("aah"), self.v_ap("aal"), aa, [baa],
                       self.v_buf("aah"), self.v_buf("aal"))
            XSTAR = ("xs",)
            for f in range(5):
                pb = f * 32 if f < 4 else 0
                tpf = (96, 0) if f == 3 else None
                w2s = self.v_ap("wb2st")[pb:pb + 32, :]
                self.load(w2s, self.dram["maa_w2"].ap()[f, :, :], self.dbuf("maa_w2"),
                          self.v_buf("wb2st"))
                self.split(self.v_ap("wb2h")[pb:pb + 32, :], self.v_ap("wb2l")[pb:pb + 32, :],
                           w2s, [self.v_buf("wb2st")],
                           self.v_buf("wb2h"), self.v_buf("wb2l"))
                if f < 4:
                    ash = self.v_ap("aah")[f * 32:f * 32 + 32, 0:256]
                    asl = self.v_ap("aal")[f * 32:f * 32 + 32, 0:256]
                else:
                    ash = self.v_ap("aah")[0:32, 256:512]
                    asl = self.v_ap("aal")[0:32, 256:512]
                for mcch in range(NCH):
                    bank = 2 + (mcch % 2)
                    pairs = [(self.v_ap("wb2h")[pb:pb + 32, mcch * 128:mcch * 128 + 128],
                              self.v_ap("wb2l")[pb:pb + 32, mcch * 128:mcch * 128 + 128], ash, asl)]
                    self.mm3(self.ps_ap(bank, 256), pairs,
                             [self.v_buf("wb2h"), self.v_buf("wb2l"), baa,
                              self.v_buf("aah"), self.v_buf("aal")],
                             self.psb[bank], tp=tpf)
                    t1, b1 = self.v_ap("t1"), self.v_buf("t1")
                    def fuse(e, bank=bank, f=f, mcch=mcch, t1=t1, xs=xs, xx=xx, xn=xn, mc=mc):
                        e.tensor_scalar(t1[:, 0:256], self.ps_ap(bank, 256),
                                        mc[:, (f + 1) * 16 + mcch:(f + 1) * 16 + mcch + 1],
                                        None, ALU.add)
                        e.drain()
                        e.tensor_tensor(t1[:, 0:256], xx[:, mcch, :], t1[:, 0:256], ALU.mult)
                        e.drain()
                        return e.tensor_tensor(xs[:, mcch, :], xn[:, mcch, :], t1[:, 0:256], ALU.add)
                    self.vop(fuse, [self.psb[bank], self.v_buf("mc"), self.v_buf("xx"),
                                    self.v_buf("xf")], [b1, self.v_buf("xs")])
                self.split(self.v_ap("xsh"), self.v_ap("xsl"), self.v_ap("xs"),
                           [self.v_buf("xs")], self.v_buf("xsh"), self.v_buf("xsl"))
                if f == 0:
                    self.wcol_fm("dec_w1", 0, 64, "w1st", "w1h", "w1l", kcn=NCH)
                    w1h2 = self.v_ap("w1h").rearrange("p (k w) -> p k w", k=NCH)
                    w1l2 = self.v_ap("w1l").rearrange("p (k w) -> p k w", k=NCH)
                    pairs = [(w1h2[:, kc, 0:64], w1l2[:, kc, 0:64], xshr[:, kc, :], xslr[:, kc, :])
                             for kc in range(NCH)]
                    self.mm3(self.ps_ap_p(4, 0, 64, 256), pairs, wb, self.psb[4])
                    d1, bd1 = self.v_ap("d1"), self.v_buf("d1")
                    self.sop(lambda e, d1=d1: e.activation(d1, self.ps_ap_p(4, 0, 64, 256), AF.Tanh),
                             [self.psb[4]], [bd1])
                    self.split(self.v_ap("d1h"), self.v_ap("d1l"), d1, [bd1],
                               self.v_buf("d1h"), self.v_buf("d1l"))
                    self.load(self.v_ap("wb2st")[0:64, :], self.dram["dec_w2"].ap(), self.dbuf("dec_w2"),
                              self.v_buf("wb2st"))
                    self.split(self.v_ap("wb2h")[0:64, :], self.v_ap("wb2l")[0:64, :],
                               self.v_ap("wb2st")[0:64, :],
                               [self.v_buf("wb2st")], self.v_buf("wb2h"), self.v_buf("wb2l"))
                    for mt in range(2):
                        for nf in range(4):
                            bank = 4 + (mt * 4 + nf) % 2
                            pairs = [(self.v_ap("d1h")[:, mt * 128:mt * 128 + 128],
                                      self.v_ap("d1l")[:, mt * 128:mt * 128 + 128],
                                      self.v_ap("wb2h")[0:64, nf * 512:nf * 512 + 512],
                                      self.v_ap("wb2l")[0:64, nf * 512:nf * 512 + 512])]
                            self.mm3(self.ps_ap(bank), pairs,
                                     [self.v_buf("d1h"), self.v_buf("d1l"),
                                      self.v_buf("wb2h"), self.v_buf("wb2l")],
                                     self.psb[bank])
                            t1, b1 = self.v_ap("t1"), self.v_buf("t1")
                            dkbs, bdkbs = self.v_ap("dkbs"), self.v_buf("dkbs")
                            self.load(dkbs, self.dram["decay_row"].ap()
                                      [0:1, nf * 512:nf * 512 + 512].broadcast_to([128, 512]),
                                      self.dbuf("decay_row"), bdkbs)
                            self.vop(lambda e, bank=bank, t1=t1, dkbs=dkbs:
                                     e.tensor_tensor(t1, self.ps_ap(bank), dkbs, ALU.add),
                                     [self.psb[bank], bdkbs], [b1])
                            r0 = t0 + mt * 128
                            self.store(self.dram["RW_W"].ap()[r0:r0 + 128, nf * 512:nf * 512 + 512],
                                       t1, b1, self.dbuf("RW_W"))
                elif f in (1, 2):
                    wname = "wk_r" if f == 1 else "wv_r"
                    dstn = "RW_K" if f == 1 else "RW_V"
                    for of in range(4):
                        for kh_ in range(4):
                            self.wtile8(wname, kh_ * 4, of * 512, "vwst", "vwh", "vwl")
                            vwh = self.v_ap("vwh").rearrange("p (k w) -> p k w", k=4)
                            vwl = self.v_ap("vwl").rearrange("p (k w) -> p k w", k=4)
                            for mt in range(2):
                                msl = slice(mt * 128, mt * 128 + 128)
                                pairs = [(xshr[:, kh_ * 4 + kc, msl], xslr[:, kh_ * 4 + kc, msl],
                                          vwh[:, kc, :], vwl[:, kc, :]) for kc in range(4)]
                                self.mm3(self.ps_ap(4 + mt), pairs,
                                         [self.v_buf("vwh"), self.v_buf("vwl"),
                                          self.v_buf("xsh"), self.v_buf("xsl")],
                                         self.psb[4 + mt], start=(kh_ == 0), stop=(kh_ == 3))
                        for mt in range(2):
                            t1, b1 = self.v_ap("t1"), self.v_buf("t1")
                            self.vop(lambda e, mt=mt, t1=t1: e.tensor_copy(t1, self.ps_ap(4 + mt)),
                                     [self.psb[4 + mt]], [b1])
                            r0 = t0 + mt * 128
                            self.store(self.dram[dstn].ap()[r0:r0 + 128, of * 512:of * 512 + 512],
                                       t1, b1, self.dbuf(dstn))
                else:
                    wname = "wr_r" if f == 3 else "wg_r"
                    dstn = "RW_R" if f == 3 else "RW_G"
                    for h in range(NCH):
                        self.wcol_fm(wname, h * 128, 128, "w1st", "w1h", "w1l", kcn=NCH)
                        w1h3 = self.v_ap("w1h").rearrange("p (k w) -> p k w", k=NCH)
                        w1l3 = self.v_ap("w1l").rearrange("p (k w) -> p k w", k=NCH)
                        pairs = [(w1h3[:, kc, 0:128], w1l3[:, kc, 0:128],
                                  xshr[:, kc, :], xslr[:, kc, :]) for kc in range(NCH)]
                        bank = 4 + h % 2
                        self.mm3(self.ps_ap(bank, 256), pairs, wb, self.psb[bank])
                        t1, b1 = self.v_ap("t1"), self.v_buf("t1")
                        if f == 4:
                            self.sop(lambda e, bank=bank, t1=t1: e.activation(t1[:, 0:256],
                                                                              self.ps_ap(bank, 256), AF.Silu),
                                     [self.psb[bank]], [b1])
                        else:
                            self.vop(lambda e, bank=bank, t1=t1: e.tensor_copy(t1[:, 0:256],
                                                                               self.ps_ap(bank, 256)),
                                     [self.psb[bank]], [b1])
                        self.store(self.dram[dstn].ap()[h, :, t0:t0 + 256], t1[:, 0:256],
                                   b1, self.dbuf(dstn))

        H64 = 32
        for cc in range(NCHK):
            c0 = cc * 128
            self.stage(
                [(f"F{i}", [128, 2048], F32) for i in range(10)] +
                [(f"R{i}", [128, 2048], F32R) for i in range(8)] +
                [("sm", [128, 128], F32), ("smh", [128, 128], F32R), ("sml", [128, 128], F32R),
                 ("dv", [128, 32], F32), ("gst", [128, 128], F32),
                 ("pcol", [128, 16], F32), ("pcol2", [128, 16], F32),
                 ("ucol", [128, 16], F32), ("lng", [128, 32], F32)])
            A = self.v_ap
            B = self.v_buf
            self.load(A("F0").rearrange("p (c w) -> p c w", c=NCH),
                      self.dram["RW_R"].ap().rearrange("c p t -> p c t")[:, :, c0:c0 + 128],
                      self.dbuf("RW_R"), B("F0"))
            self.load(A("F1"), self.dram["RW_K"].ap()[c0:c0 + 128, :], self.dbuf("RW_K"), B("F1"))
            self.load(A("F2"), self.dram["RW_V"].ap()[c0:c0 + 128, :], self.dbuf("RW_V"), B("F2"), q="S")
            self.load(A("F3"), self.dram["RW_W"].ap()[c0:c0 + 128, :], self.dbuf("RW_W"), B("F3"), q="S")
            self.load(A("ucol"), self.dram["u_col"].ap(), self.dbuf("u_col"), B("ucol"))
            self.load(A("lng"), self.dram["lnr_cols"].ap(), self.dbuf("lnr_cols"), B("lng"))
            self.sop(lambda e, a=A("F3"): e.activation(a, a, AF.Exp), [B("F3")], [B("F3")])
            self.split(A("R0"), A("R1"), A("F3"), [B("F3")], B("R0"), B("R1"))
            for nf in range(4):
                pairs = [(self.k_tri.ap(), A("R0")[:, nf * 512:nf * 512 + 512]),
                         (self.k_tri.ap(), A("R1")[:, nf * 512:nf * 512 + 512])]
                self.mm1(self.ps_ap(nf), pairs,
                         [self.sbbuf("k_tri"), B("R0"), B("R1")], self.psb[nf])
            for nf in range(4):
                pairs = [(self.k_ones.ap(), A("R0")[:, nf * 512:nf * 512 + 512]),
                         (self.k_ones.ap(), A("R1")[:, nf * 512:nf * 512 + 512])]
                self.mm1(self.ps_ap(4 + nf), pairs,
                         [self.sbbuf("k_ones"), B("R0"), B("R1")], self.psb[4 + nf])
            def totcp(e, t=A("F4")):
                last = None
                for nf in range(4):
                    last = e.tensor_copy(t[:, nf * 512:nf * 512 + 512], self.ps_ap(4 + nf))
                return last
            self.vop(totcp, [self.psb[4], self.psb[5], self.psb[6], self.psb[7]], [B("F4")])
            self.sop(lambda e, a=A("F5"), t=A("F4"): e.activation(a, t, AF.Exp, scale=-1.0),
                     [B("F4")], [B("F5")])
            self.store(self.dram["ROWTMP"].ap()[1:2, :], A("F5")[0:1, :], B("F5"), self.dbuf("ROWTMP"))
            self.load(A("pcol"), self.dram["ROWTMP"].ap()[1:2, :].rearrange("o (c p) -> (o p) c", p=128),
                      self.dbuf("ROWTMP"), B("pcol"), slow=True)
            self.vop(lambda e, t=A("F5"), s=A("F4"): e.tensor_scalar(t, s, 0.5, None, ALU.mult),
                     [B("F4")], [B("F5")])
            def aarg(e, a=A("F6"), toth=A("F5")):
                last = None
                for nf in range(4):
                    last = e.tensor_tensor(a[:, nf * 512:nf * 512 + 512],
                                           toth[:, nf * 512:nf * 512 + 512],
                                           self.ps_ap(nf), ALU.subtract)
                return last
            self.vop(aarg, [self.psb[0], self.psb[1], self.psb[2], self.psb[3], B("F5")], [B("F6")])
            self.sop(lambda e, a=A("F6"): e.activation(a, a, AF.Exp), [B("F6")], [B("F6")])
            def iarg(e, a=A("F7"), wc=A("F3"), toth=A("F5")):
                for nf in range(4):
                    sl = slice(nf * 512, nf * 512 + 512)
                    e.tensor_tensor(a[:, sl], self.ps_ap(nf), wc[:, sl], ALU.add)
                e.drain()
                return e.tensor_tensor(a, a, toth, ALU.subtract)
            self.vop(iarg, [self.psb[0], self.psb[1], self.psb[2], self.psb[3], B("F3"), B("F5")], [B("F7")])
            self.sop(lambda e, a=A("F7"): e.activation(a, a, AF.Exp), [B("F7")], [B("F7")])
            if dbg:
                self.store(self.dram["DBG1"].ap()[c0:c0 + 128, :], A("F6"), B("F6"), self.dbuf("DBG1"))
                self.store(self.dram["DBG2"].ap()[c0:c0 + 128, :], A("F7"), B("F7"), self.dbuf("DBG2"))
            self.sop(lambda e, a=A("F9"), t=A("F5"): e.activation(a, t, AF.Exp, scale=-1.0),
                     [B("F5")], [B("F9")])
            self.store(self.dram["ROWTMP"].ap()[2:3, :], A("F9")[0:1, :], B("F9"), self.dbuf("ROWTMP"))
            self.load(A("pcol2"), self.dram["ROWTMP"].ap()[2:3, :].rearrange("o (c p) -> (o p) c", p=128),
                      self.dbuf("ROWTMP"), B("pcol2"), slow=True)
            self.vop(lambda e, o=A("F8"), k=A("F1"), iv=A("F7"): e.tensor_tensor(o, k, iv, ALU.mult),
                     [B("F1"), B("F7")], [B("F8")])
            self.vop(lambda e, o=A("F9"), kh=A("F8"): e.tensor_tensor(o, kh, o, ALU.mult),
                     [B("F8"), B("F9")], [B("F9")])
            for (srcv, dstv) in [("F6", "F7"), ("F8", "F8"), ("F1", "F3")]:
                sap, dap = A(srcv), A(dstv)
                sb_, db_ = B(srcv), B(dstv)
                for grp in range(4):
                    bank = 4 + grp % 2
                    srcs = [sap[:, (grp * 4 + i) * 128:(grp * 4 + i) * 128 + 128] for i in range(4)]
                    self.transpose_group(bank, srcs, [sb_])
                    self.vop(lambda e, dap=dap, grp=grp, bank=bank:
                             e.tensor_copy(dap[:, grp * 512:grp * 512 + 512], self.ps_ap(bank)),
                             [self.psb[bank]], [db_])
            def mkrku(e, o=A("F1"), r=A("F0"), kfm=A("F3"), ucol=A("ucol")):
                e.tensor_tensor(o, r, kfm, ALU.mult)
                e.drain()
                last = None
                for c in range(NCH):
                    last = e.tensor_scalar(o[:, c * 128:c * 128 + 128], o[:, c * 128:c * 128 + 128],
                                           ucol[:, c:c + 1], None, ALU.mult)
                return last
            self.vop(mkrku, [B("F0"), B("F3"), B("ucol")], [B("F1")])
            self.split(A("R2"), A("R3"), A("F1"), [B("F1")], B("R2"), B("R3"))
            for h in range(H64):
                par = (h % 2) * 64
                hp = h // 2
                pairs = [(A("R2")[par:par + 64, hp * 128:hp * 128 + 128],
                          self.k_ones.ap()[par:par + 64, 0:2]),
                         (A("R3")[par:par + 64, hp * 128:hp * 128 + 128],
                          self.k_ones.ap()[par:par + 64, 0:2])]
                self.mm1(self.ps_ap(5, 2, 2 * h), pairs,
                         [B("R2"), B("R3"), self.sbbuf("k_ones")], self.psb[5])
            self.vop(lambda e, dv=A("dv"): e.tensor_copy(
                dv, self.ps_ap(5, 64, 0).rearrange("p (c two) -> p c two", two=2)[:, :, 0]),
                [self.psb[5]], [B("dv")])
            self.split(A("R0"), A("R1"), A("F8"), [B("F8")], B("R0"), B("R1"))
            self.split(A("R2"), A("R3"), A("F9"), [B("F9")], B("R2"), B("R3"))
            self.split(A("R4"), A("R5"), A("F2"), [B("F2")], B("R4"), B("R5"))
            self.vop(lambda e, r=A("F0"), afm=A("F7"): e.tensor_tensor(r, r, afm, ALU.mult),
                     [B("F7"), B("F0")], [B("F0")])
            self.split(A("R6"), A("R7"), A("F0"), [B("F0")], B("R6"), B("R7"))
            p2b = A("pcol2")[:, :, None].broadcast_to([128, 16, 64])
            self.vop(lambda e, o=A("F5"), p2b=p2b: e.tensor_tensor(
                o[:, 0:1024].rearrange("p (m i) -> p m i", m=16),
                self.k_S.ap().rearrange("p (m i) -> p m i", m=16), p2b, ALU.mult),
                [bS, B("pcol2")], [B("F5")])
            self.split(self.k_Sh.ap(), self.k_Sl.ap(), A("F5")[:, 0:1024], [B("F5")], bSh, bSl)
            for h in range(H64):
                par = (h % 2) * 64
                hp = h // 2
                sb_sc = 6 + h % 2
                pairs = [(A("R0")[par:par + 64, hp * 128:hp * 128 + 128],
                          A("R1")[par:par + 64, hp * 128:hp * 128 + 128],
                          A("R6")[par:par + 64, hp * 128:hp * 128 + 128],
                          A("R7")[par:par + 64, hp * 128:hp * 128 + 128])]
                self.mm3(self.ps_ap(sb_sc, 128), pairs,
                         [B("R0"), B("R1"), B("R6"), B("R7")], self.psb[sb_sc])
                self.vop(lambda e, sm=A("sm"), sb_sc=sb_sc: e.select(
                    sm, self.k_strictu.ap(), self.ps_ap(sb_sc, 128), self.k_zero.ap()),
                    [self.psb[sb_sc], self.sbbuf("k_strictu"), self.sbbuf("k_zero")], [B("sm")])
                self.split(A("smh"), A("sml"), A("sm"), [B("sm")], B("smh"), B("sml"))
                ybank = h // 8
                ycol = (h % 8) * 64
                pairs = [(A("smh"), A("sml"),
                          A("R4")[:, h * 64:h * 64 + 64], A("R5")[:, h * 64:h * 64 + 64])]
                self.mm3(self.ps_ap(ybank, 64, ycol), pairs,
                         [B("smh"), B("sml"), B("R4"), B("R5")],
                         self.psb[ybank], start=True, stop=False)
                pairs = [(A("R6")[par:par + 64, hp * 128:hp * 128 + 128],
                          A("R7")[par:par + 64, hp * 128:hp * 128 + 128],
                          self.k_Sh.ap()[par:par + 64, hp * 64:hp * 64 + 64],
                          self.k_Sl.ap()[par:par + 64, hp * 64:hp * 64 + 64])]
                self.mm3(self.ps_ap(ybank, 64, ycol), pairs,
                         [B("R6"), B("R7"), bSh, bSl],
                         self.psb[ybank], start=False, stop=True)
            for rnd in range(2):
                for mp in range(rnd * 8, rnd * 8 + 8):
                    k = mp % 8
                    bank = 4 + k // 4
                    dcol = (k % 4) * 128
                    pairs = [(A("R2")[:, mp * 128:mp * 128 + 128],
                              A("R3")[:, mp * 128:mp * 128 + 128],
                              A("R4")[:, mp * 128:mp * 128 + 128],
                              A("R5")[:, mp * 128:mp * 128 + 128])]
                    self.mm3(self.ps_ap(bank, 128, dcol), pairs,
                             [B("R2"), B("R3"), B("R4"), B("R5")], self.psb[bank])
                scol = rnd * 512
                Sap = self.k_S.ap()[:, scol:scol + 512]
                pcol_b = (A("pcol")[:, rnd * 8:rnd * 8 + 8, None].broadcast_to([128, 8, 64]))
                def supd(e, Sap=Sap, pcol_b=pcol_b, rnd=rnd):
                    e.tensor_tensor(Sap.rearrange("p (m i) -> p m i", m=8),
                                    Sap.rearrange("p (m i) -> p m i", m=8), pcol_b, ALU.mult)
                    e.drain()
                    ps45 = self.psum_h[:, 4 * 512: 6 * 512]
                    even = ps45[0:64, :].rearrange("p (m i) -> p m i", i=128)[:, :, 0:64]
                    odd = ps45[64:128, :].rearrange("p (m i) -> p m i", i=128)[:, :, 64:128]
                    e.tensor_tensor(Sap[0:64, :].rearrange("p (m i) -> p m i", m=8),
                                    Sap[0:64, :].rearrange("p (m i) -> p m i", m=8), even, ALU.add)
                    e.drain()
                    return e.tensor_tensor(Sap[64:128, :].rearrange("p (m i) -> p m i", m=8),
                                           Sap[64:128, :].rearrange("p (m i) -> p m i", m=8), odd, ALU.add)
                self.vop(supd, [self.psb[4], self.psb[5], B("pcol"), bS], [bS])
            dvb = A("dv")[:, :, None].broadcast_to([128, 32, 64])
            self.vop(lambda e, yt=A("F3"), dvb=dvb, vc_=A("F2"):
                     e.tensor_tensor(yt.rearrange("p (c i) -> p c i", c=32),
                                     vc_.rearrange("p (c i) -> p c i", c=32), dvb, ALU.mult),
                     [B("dv"), B("F2")], [B("F3")])
            def yev(e, yf=A("F6"), yt=A("F3")):
                last = None
                for b in range(4):
                    last = e.tensor_tensor(yf[:, b * 512:b * 512 + 512], self.ps_ap(b),
                                           yt[:, b * 512:b * 512 + 512], ALU.add)
                return last
            self.vop(yev, [B("F3"), self.psb[0], self.psb[1], self.psb[2], self.psb[3]], [B("F6")])
            if dbg:
                self.store(self.dram["DBG3"].ap()[c0:c0 + 128, :], A("F6"), B("F6"), self.dbuf("DBG3"))
            yf, byf = A("F6"), B("F6")
            yt, byt = A("F3"), B("F3")
            gst, bgst = A("gst"), B("gst")
            yf3 = yf.rearrange("p (c i) -> p c i", c=32)
            self.sop(lambda e, yt=yt, yf=yf: e.activation(yt, yf, AF.Square), [byf], [byt])
            yt3 = yt.rearrange("p (c i) -> p c i", c=32)
            def gred(e, gst=gst, yf3=yf3, yt3=yt3):
                e.tensor_reduce(gst[:, 0:32], yf3, AX.X, ALU.add)
                e.tensor_reduce(gst[:, 32:64], yt3, AX.X, ALU.add)
                e.drain()
                e.tensor_scalar(gst[:, 0:32], gst[:, 0:32], 1.0 / 64, None, ALU.mult)
                e.tensor_scalar(gst[:, 32:64], gst[:, 32:64], 1.0 / 64, None, ALU.mult)
                e.drain()
                e.tensor_tensor(gst[:, 64:96], gst[:, 0:32], gst[:, 0:32], ALU.mult)
                e.drain()
                return e.tensor_tensor(gst[:, 64:96], gst[:, 32:64], gst[:, 64:96], ALU.subtract)
            self.vop(gred, [byf, byt], [bgst])
            def grs(e, gst=gst):
                e.activation(gst[:, 96:128], gst[:, 64:96], AF.Ln, bias=float(EPS_GN))
                e.drain()
                return e.activation(gst[:, 96:128], gst[:, 96:128], AF.Exp, scale=-0.5)
            self.sop(grs, [bgst], [bgst])
            def ynorm(e, yf3=yf3, gst=gst):
                e.tensor_tensor(yf3, yf3,
                                gst[:, 0:32][:, :, None].broadcast_to([128, 32, 64]),
                                ALU.subtract)
                e.drain()
                return e.tensor_tensor(yf3, yf3,
                                       gst[:, 96:128][:, :, None].broadcast_to([128, 32, 64]),
                                       ALU.mult)
            self.vop(ynorm, [bgst, byf], [byf])
            self.load(A("F4").rearrange("p (c w) -> p c w", c=NCH),
                      self.dram["RW_G"].ap().rearrange("c p t -> p c t")[:, :, c0:c0 + 128],
                      self.dbuf("RW_G"), B("F4"))
            lng = A("lng")
            for c in range(NCH):
                bank = 4 + c % 2
                self.transpose_group(bank, [yf[:, c * 128:c * 128 + 128]], [byf])
                def ynev(e, bank=bank, c=c, t1=A("sm"), lng=lng, gc_=A("F4")):
                    e.tensor_scalar(t1, self.ps_ap(bank, 128), lng[:, c:c + 1],
                                    lng[:, 16 + c:16 + c + 1], ALU.mult, ALU.add)
                    e.drain()
                    return e.tensor_tensor(t1, t1, gc_[:, c * 128:c * 128 + 128], ALU.mult)
                self.vop(ynev, [self.psb[bank], B("lng"), B("F4")], [B("sm")])
                self.split(A("smh"), A("sml"), A("sm"), [B("sm")], B("smh"), B("sml"))
                self.store(self.dram["AOF_H"].ap()[c, :, c0:c0 + 128], A("smh"),
                           B("smh"), self.dbuf("AOF_H"))
                self.store(self.dram["AOF_L"].ap()[c, :, c0:c0 + 128], A("sml"),
                           B("sml"), self.dbuf("AOF_L"), q="S")
        self.sc.t_drain = False
        self.pass_out_proj("AOF_H", "AOF_L", "wo_r", resid_src, resid_dst)


def build_full(T_core=2048, npass=NPASS, debug_taps=()):
    m = MK(T_core=T_core, npass=npass, debug_taps=debug_taps)
    m._init_psum()
    m._init_arena()
    m._init_fconsts()
    m._decl_inputs()
    m._decl_scratch()
    m._init_consts()
    m.stage_zero_pads()
    m.stage_woa()
    m.stage_ln("x")
    m.stage_attn_a(resid_src="x", resid_dst="R")
    m.stage_ln("R")
    m.stage_ffn(resid_src="R", resid_dst="R")
    m.stage_ln("R", write_f32=True)
    m.stage_rwkv(resid_src="R", resid_dst="R")
    m.stage_ln("R")
    m.stage_ffn(resid_src="R", resid_dst="out")
    nc = m.finalize()
    return nc, m


def make_inmap(m, x_seq, w):
    perm = np.concatenate([np.arange(64, 128), np.arange(64)])
    permD = (np.arange(D).reshape(NCH, 128)[:, perm]).reshape(-1)
    def fmcol(v):
        return np.ascontiguousarray(v.reshape(NCH, 128).T)
    T_ = lambda a: np.ascontiguousarray(np.asarray(a).T)
    C_ = lambda a: np.ascontiguousarray(np.asarray(a))
    maa = np.concatenate([fmcol(np.asarray(w[f"time_maa_{k}"]).reshape(-1))
                          for k in ("x", "w", "k", "v", "r", "g")], 1)
    return {
        "x": np.ascontiguousarray(x_seq),
        "wq_a": T_(w["a_Wq"]), "wqs_a": T_(np.asarray(w["a_Wq"])[permD]),
        "wk_a": T_(w["a_Wk"]), "wks_a": T_(np.asarray(w["a_Wk"])[permD]),
        "wv_a": T_(w["a_Wv"]), "wo_a": T_(w["a_Wo"]),
        "wq_f": T_(w["f_Wq"]), "wk_f": T_(w["f_Wk"]), "wv_f": T_(w["f_Wv"]),
        "wo_f_nat": C_(w["f_Wo"]), "a_f": C_(w["f_A"]), "b_f": C_(w["f_B"]),
        "win": T_(w["f_Win"]), "wup": T_(w["f_Wup"]), "wdn": T_(w["f_Wdn"]),
        "ln1_row": np.asarray(w["f_ln1"]).reshape(1, D).astype(np.float32),
        "ln2_row": np.asarray(w["f_ln2"]).reshape(1, D).astype(np.float32),
        "wr_r": T_(w["r_Wr"]), "wk_r": T_(w["r_Wk"]), "wv_r": T_(w["r_Wv"]),
        "wg_r": T_(w["r_Wg"]), "wo_r": T_(w["r_Wo"]),
        "maa_w1": C_(w["time_maa_w1"]), "maa_w2": C_(w["time_maa_w2"]),
        "dec_w1": C_(w["time_decay_w1"]), "dec_w2": C_(w["time_decay_w2"]),
        "maa_cols": maa,
        "decay_row": np.asarray(w["time_decay"]).reshape(1, D).astype(np.float32),
        "u_col": fmcol(np.asarray(w["time_faaaa"]).reshape(-1)),
        "lnr_cols": np.concatenate([fmcol(np.asarray(w["r_ln_gamma"]).reshape(-1)),
                                    fmcol(np.asarray(w["r_ln_beta"]).reshape(-1))], 1),
    }

_CACHE = {}

def kernel(**inputs):
    if "nc" not in _CACHE:
        _CACHE["nc"], _CACHE["m"] = build_full(T_core=2048)
    nc, m = _CACHE["nc"], _CACHE["m"]
    w = {
        "a_Wq": inputs["L0_attn_a_Wq"], "a_Wk": inputs["L0_attn_a_Wk"],
        "a_Wv": inputs["L0_attn_a_Wv"], "a_Wo": inputs["L0_attn_a_Wo"],
        "f_Wq": inputs["L0_ffn_Wq"], "f_Wk": inputs["L0_ffn_Wk"], "f_Wv": inputs["L0_ffn_Wv"],
        "f_Wo": inputs["L0_ffn_Wo"], "f_A": inputs["L0_ffn_A"], "f_B": inputs["L0_ffn_B"],
        "f_Win": inputs["L0_ffn_W_in_base"], "f_Wup": inputs["L0_ffn_W_up_base"],
        "f_Wdn": inputs["L0_ffn_W_down"], "f_ln1": inputs["L0_ffn_ln1_w"],
        "f_ln2": inputs["L0_ffn_ln2_w"],
        "r_Wr": inputs["L0_attn_b_W_r"], "r_Wk": inputs["L0_attn_b_W_k"],
        "r_Wv": inputs["L0_attn_b_W_v"], "r_Wg": inputs["L0_attn_b_W_g"],
        "r_Wo": inputs["L0_attn_b_W_o"],
        "r_ln_gamma": inputs["L0_attn_b_ln_gamma"], "r_ln_beta": inputs["L0_attn_b_ln_beta"],
        "time_maa_x": inputs["L0_attn_b_time_maa_x"], "time_maa_w": inputs["L0_attn_b_time_maa_w"],
        "time_maa_k": inputs["L0_attn_b_time_maa_k"], "time_maa_v": inputs["L0_attn_b_time_maa_v"],
        "time_maa_r": inputs["L0_attn_b_time_maa_r"], "time_maa_g": inputs["L0_attn_b_time_maa_g"],
        "time_maa_w1": inputs["L0_attn_b_time_maa_w1"], "time_maa_w2": inputs["L0_attn_b_time_maa_w2"],
        "time_decay": inputs["L0_attn_b_time_decay"], "time_decay_w1": inputs["L0_attn_b_time_decay_w1"],
        "time_decay_w2": inputs["L0_attn_b_time_decay_w2"], "time_faaaa": inputs["L0_attn_b_time_faaaa"],
    }
    x = np.asarray(inputs["L0_attn_a_x"])
    base = make_inmap(m, x[0], w)
    in_maps = []
    for c in range(8):
        s = c // 2
        im = dict(base)
        im["x"] = np.ascontiguousarray(x[s])
        in_maps.append(im)
    res = run_bass_kernel_spmd(nc, in_maps, core_ids=list(range(8)))
    return np.stack([np.asarray(res.results[2 * s]["out"]) for s in range(4)], 0)


# version 09 (final)
# speedup vs baseline: 1.1803x; 1.0925x over previous
import math
import os
os.environ["GAUGE_NTFF_IGNORE_DMA"] = "1"
import numpy as np
import concourse.bass as bass
import concourse.mybir as mybir
from concourse.bass_utils import run_bass_kernel_spmd

F32 = mybir.dt.float32
F32R = mybir.dt.float32r
AF = mybir.ActivationFunctionType
ALU = mybir.AluOpType
AX = mybir.AxisListType

D = 2048
NCH = 16
HEAD128 = 16
NPASS = 3
F_FFN = 8192
NFC = F_FFN // 128
EPS_A = 1e-6
EPS_LN = 1e-5
EPS_RMS = 1e-6
EPS_GN = 1e-5 * 64.0

class Stream:
    def __init__(self, name, engine_attr):
        self.name = name
        self.engine_attr = engine_attr
        self.cmp_count = 0
        self.dma_count = 0
        self.ops = []

class Tok:
    __slots__ = ("stream", "dma", "count")
    def __init__(self, stream, dma, count):
        self.stream = stream
        self.dma = dma
        self.count = count

class Sched:
    NSEM_DMA = 6

    def __init__(self, nc):
        self.t_drain = False
        self.nc = nc
        self.gseq = 0
        self.tok_gidx = {}
        self.streams = {}
        for name, attr in [("T", "tensor"), ("V", "vector"), ("S", "scalar"),
                           ("G", "gpsimd"), ("Y", "sync")]:
            self.streams[name] = Stream(name, attr)

    def op(self, stream_name, emit_fn, deps=(), dma=False):
        st = self.streams[stream_name]
        if dma:
            st.dma_count += 1
            cnt = st.dma_count
        else:
            st.cmp_count += 1
            cnt = st.cmp_count
        waits = []
        need_drain = 0
        sdeps = set()
        for d in deps:
            if d is None:
                continue
            if d.stream is st and not d.dma:
                sdeps.add(d.count)
                if stream_name != "T" or self.t_drain:
                    need_drain = max(need_drain, d.count)
                continue
            waits.append(d)
        best = {}
        for d in waits:
            key = (d.stream.name, d.dma)
            if key not in best or best[key].count < d.count:
                best[key] = d
        gidx = self.gseq
        self.gseq += 1
        self.tok_gidx[(stream_name, dma, cnt)] = gidx
        st.ops.append([emit_fn, dma, cnt, list(best.values()), need_drain, sdeps, gidx])
        return Tok(st, dma, cnt)

    def barrier_tokens(self):
        toks = []
        for st in self.streams.values():
            if st.cmp_count:
                toks.append(Tok(st, False, st.cmp_count))
            if st.dma_count:
                toks.append(Tok(st, True, st.dma_count))
        return toks

    def _reorder_T(self):
        ops = self.streams["T"].ops
        n = len(ops)
        if n < 3:
            return
        def max_wait_gidx(op):
            g = -1
            for d in op[3]:
                g = max(g, self.tok_gidx.get((d.stream.name, d.dma, d.count), 1 << 60))
            return g
        mw = [max_wait_gidx(op) for op in ops]
        eff = [op[6] for op in ops]
        for _ in range(24):
            changed = False
            for i in range(n - 1):
                W, X = ops[i], ops[i + 1]
                mwW, mwX = mw[i], mw[i + 1]
                if W[4] or X[4]:
                    continue
                if W[2] in X[5]:
                    continue
                if mwX >= eff[i]:
                    continue
                if mwX >= mwW:
                    continue
                ops[i], ops[i + 1] = X, W
                mw[i], mw[i + 1] = mwX, mwW
                eff[i], eff[i + 1] = eff[i], eff[i]
                changed = True
            if not changed:
                break

    def emit(self, block, sems_cmp, sems_dma):
        self._reorder_T()
        awaited = {name: set() for name in self.streams}
        for st in self.streams.values():
            for _emit_fn, _dma, _cnt, waits, _nd, _sd, _gi in st.ops:
                for d in waits:
                    if not d.dma:
                        awaited[d.stream.name].add(d.count)
        rank = {}
        for name, st in self.streams.items():
            r = 0
            for op in st.ops:
                if not op[1] and op[2] in awaited[name]:
                    r += 1
                    rank[(name, op[2])] = r

        for name, st in self.streams.items():
            if not st.ops:
                continue
            eng_stream_factory = getattr(block, st.engine_attr)

            def make_body(st=st, name=name):
                def body(eng):
                    hw = {}
                    last_drain = 0
                    for emit_fn, dma, cnt, waits, need_drain, _sd, _gi in st.ops:
                        if need_drain > last_drain and not dma:
                            eng.drain()
                            last_drain = cnt - 1
                        for d in waits:
                            if d.dma:
                                k = d.count
                                sem_i = (k - 1) % Sched.NSEM_DMA
                                per = 16 * ((k + Sched.NSEM_DMA - 1) // Sched.NSEM_DMA)
                                key = (d.stream.name, "d", sem_i)
                                if hw.get(key, -1) < per:
                                    eng.wait_ge(sems_dma[d.stream.name][sem_i], per)
                                    hw[key] = per
                            else:
                                val = rank[(d.stream.name, d.count)]
                                key = (d.stream.name, "c", 0)
                                if hw.get(key, -1) < val:
                                    eng.wait_ge(sems_cmp[d.stream.name], val)
                                    hw[key] = val
                        inst = emit_fn(eng)
                        if dma:
                            sem_i = (cnt - 1) % Sched.NSEM_DMA
                            inst.then_inc(sems_dma[st.name][sem_i], 16)
                        elif cnt in awaited[name]:
                            if inst is None:
                                inst = eng.nop()
                            inst.then_inc(sems_cmp[st.name], 1)
                return body
            eng_stream_factory(make_body())



class Buf:
    __slots__ = ("writer", "readers", "name")
    def __init__(self, name=""):
        self.writer = None
        self.readers = []
        self.name = name

    def rd(self):
        return [self.writer] if self.writer else []

    def wr(self):
        deps = list(self.readers)
        if self.writer:
            deps.append(self.writer)
        return deps

    def did_read(self, tok):
        self.readers.append(tok)

    def did_write(self, tok):
        self.writer = tok
        self.readers = []


def deps_rw(reads, writes):
    d = []
    for b in reads:
        d += b.rd()
    for b in writes:
        d += b.wr()
    return d

def commit_rw(tok, reads, writes):
    for b in reads:
        b.did_read(tok)
    for b in writes:
        b.did_write(tok)


RSQ128 = 1.0 / math.sqrt(128.0)

class MK:
    def __init__(self, T_core=2048, npass=NPASS, debug_taps=()):
        self.T = T_core
        self.NP = npass
        self.nc = bass.Bass("TRN2", target_bir_lowering=False, debug=False)
        self.sc = Sched(self.nc)
        self.debug_taps = set(debug_taps)
        self.dram = {}
        self.sbuf = {}
        self.bufs = {}
        self._ctx = []
        self._psum_banks = None

    def dram_t(self, name, shape, dtype=F32, kind="Internal"):
        if kind == "Internal" and name in self.debug_taps:
            kind = "ExternalOutput"
        h = self.nc.dram_tensor(name, shape, dtype, kind=kind)
        self.dram[name] = h
        self.bufs[("dram", name)] = Buf(name)
        return h

    def dbuf(self, name):
        return self.bufs[("dram", name)]

    def sbuf_t(self, name, shape, dtype=F32):
        h = self.nc.alloc_sbuf_tensor(name, shape, dtype)
        self.sbuf[name] = h
        self.bufs[("sb", name)] = Buf(name)
        return h

    def sbbuf(self, name):
        return self.bufs[("sb", name)]

    def _op(self, stream, emit, reads, writes, dma=False):
        tok = self.sc.op(stream, emit, deps=deps_rw(reads, writes), dma=dma)
        commit_rw(tok, reads, writes)
        return tok

    def load(self, dst_ap, src_ap, rbuf, wbuf, q="Y", slow=False):
        return self._op(q, lambda e: e.dma_start(out=dst_ap, in_=src_ap, allow_slow_non_contiguous=slow), [rbuf], [wbuf], dma=True)

    def store(self, dst_ap, src_ap, rbuf, wbuf, q="G", slow=False):
        return self._op(q, lambda e: e.dma_start(out=dst_ap, in_=src_ap, allow_slow_non_contiguous=slow), [rbuf], [wbuf], dma=True)

    def vop(self, emit, reads, writes, stream="V"):
        return self._op(stream, emit, reads, writes)

    def sop(self, emit, reads, writes):
        return self._op("S", emit, list(reads) + [self.kc_buf], writes)

    def gop(self, emit, reads, writes):
        return self._op("G", emit, reads, writes)

    def top(self, emit, reads, writes):
        return self._op("T", emit, reads, writes)

    def split(self, hi_ap, lo_ap, src_ap, rbufs, whi, wlo, stream="V"):
        self.sop(lambda e: e.activation(hi_ap, src_ap, AF.Copy), rbufs, [whi])
        def emit(e):
            return e.tensor_tensor(lo_ap, src_ap, hi_ap.bitcast(F32), ALU.subtract)
        return self._op(stream, emit, list(rbufs) + [whi], [wlo])

    def mm3(self, ps_ap, pairs, rbufs, wbuf, start=True, stop=True, tp=None):
        np_ = self.NP
        def emit(e):
            last = None
            n = len(pairs)
            for i, (lh, ll, rh, rl) in enumerate(pairs):
                first = (i == 0) and start
                plist = [(lh, rh)]
                if np_ >= 2 and rl is not None:
                    plist.append((lh, rl))
                if np_ >= 3 and ll is not None:
                    plist.append((ll, rh))
                for j, (a, b) in enumerate(plist):
                    is_first = first and j == 0
                    is_last = stop and (i == n - 1) and (j == len(plist) - 1)
                    try:
                        last = e.matmul(ps_ap, a, b, start=is_first, stop=is_last,
                                        tile_position=tp)
                    except Exception as ex:
                        raise RuntimeError(f"matmul fail: lhsT={a} rhs={b} out={ps_ap}") from ex
            return last
        return self._op("T", emit, rbufs, [wbuf])

    def mm1(self, ps_ap, pairs, rbufs, wbuf, start=True, stop=True):
        def emit(e):
            last = None
            for i, (a, b) in enumerate(pairs):
                last = e.matmul(ps_ap, a, b,
                                start=(start and i == 0), stop=(stop and i == len(pairs) - 1))
            return last
        return self._op("T", emit, rbufs, [wbuf])

    ARENA_BYTES = 174 * 1024

    def _init_arena(self):
        self.arena = self.nc.alloc_sbuf_tensor("arena", [128, self.ARENA_BYTES], mybir.dt.uint8)
        self.arena_addr = self.nc.lookup_mloc(self.arena).addr
        self.views = {}
        self._stage_tag = 0

    def stage(self, layout, keep=(), auto_base=0):
        toks = self.sc.barrier_tokens()
        bar = self.sc.op("G", lambda e: None, deps=toks)
        old_views = self.views
        self._stage_tag += 1
        self.views = {}
        off = auto_base
        for entry in layout:
            if len(entry) == 4:
                name, shape, dtype, fixed = entry
            else:
                (name, shape, dtype), fixed = entry, None
            per_part = int(np.prod(shape[1:])) * mybir.dt.size(dtype)
            if fixed is not None:
                this_off = fixed
            else:
                off = (off + 63) & ~63
                this_off = off
                off += per_part
            assert this_off + per_part <= self.ARENA_BYTES, f"arena overflow at {name}"
            vname = f"vw{self._stage_tag}_{name}"
            hnd = self.nc.alloc_sbuf_tensor_at(vname, list(shape), dtype,
                                               offset=self.arena_addr + this_off)
            if name in keep and name in old_views:
                b = old_views[name][1]
            else:
                b = Buf(vname)
                b.writer = bar
            self.views[name] = (hnd, b)
        return bar

    def v_ap(self, name):
        return self.views[name][0].ap()

    def v_hnd(self, name):
        return self.views[name][0]

    def v_buf(self, name):
        return self.views[name][1]

    def _init_psum(self):
        self.psum = self.nc.alloc_sbuf_tensor
        self._psum_guard = self.nc.psum_tensor("ps_all", [128, 4096], F32)
        self.psum_h = self._psum_guard.__enter__()
        self.psb = [Buf(f"psum{b}") for b in range(8)]

    def ps_ap(self, bank, cols=512, off=0):
        return self.psum_h[:, bank * 512 + off: bank * 512 + off + cols]

    def ps_ap_p(self, bank, p0, pn, cols, off=0):
        return self.psum_h[p0:p0 + pn, bank * 512 + off: bank * 512 + off + cols]

    FCONSTS = [EPS_LN, EPS_RMS, EPS_GN, EPS_A, 1e-30]

    def _init_fconsts(self):
        nc = self.nc
        vals = np.array(self.FCONSTS, np.float32)
        data = np.repeat(vals[None, :], 128, 0)
        t = nc.inline_tensor(data, name="c_fvals")
        self.k_fc = self.sbuf_t("k_fc", [128, len(self.FCONSTS)], F32)
        self.load(self.k_fc.ap(), t.ap(), Buf("z0"), self.sbbuf("k_fc"))
        for i, v in enumerate(self.FCONSTS):
            nc.const_aps.register(F32, float(v), self.k_fc.ap()[:, i:i+1])
        self.kc_buf = self.sbbuf("k_fc")

    def _init_consts(self):
        nc = self.nc
        T = self.T
        ident = np.eye(128, dtype=np.float32)
        tri = (np.arange(128)[:, None] < np.arange(128)[None, :]).astype(np.float32)
        strict = tri.copy()
        ones = np.ones((128, 128), dtype=np.float32)
        mm_ = np.zeros((128, 4 * 512), dtype=np.float32)
        am_ = np.zeros((128, 4 * 512), dtype=np.float32)
        for sub in range(4):
            keep = (np.arange(512)[None, :] >= (np.arange(128)[:, None] + sub * 128))
            mm_[:, sub * 512:(sub + 1) * 512] = keep
            am_[:, sub * 512:(sub + 1) * 512] = np.where(keep, 0.0, -1e30)
        half = 64
        theta = (10000.0 ** (-np.arange(half, dtype=np.float64) * 2.0 / 128.0))
        pos = np.arange(T, dtype=np.float64)
        ang = pos[None, :] * theta[:, None]
        cos_t = np.cos(ang)
        sin_t = np.sin(ang)
        cos_ext = np.concatenate([cos_t, cos_t], 0).astype(np.float32)
        sin_sgn = np.concatenate([-sin_t, sin_t], 0).astype(np.float32)
        self.c_id = nc.inline_tensor(ident, name="c_id")
        self.c_tri = nc.inline_tensor(tri, name="c_tri")
        self.c_strict = nc.inline_tensor(strict, name="c_strict")
        self.c_strictu = nc.inline_tensor(strict.astype(np.uint8), name="c_strictu")
        self.c_ones = nc.inline_tensor(ones, name="c_ones")
        self.c_mm = nc.inline_tensor(mm_, name="c_mm")
        self.c_am = nc.inline_tensor(am_, name="c_am")
        self.c_cos = nc.inline_tensor(cos_ext, name="c_cos")
        self.c_sin = nc.inline_tensor(sin_sgn, name="c_sin")
        self.k_id = self.sbuf_t("k_id", [128, 128], F32)
        self.k_tri = self.sbuf_t("k_tri", [128, 128], F32R)
        self.k_strict = self.sbuf_t("k_strict", [128, 128], F32)
        self.k_strictu = self.sbuf_t("k_strictu", [128, 128], mybir.dt.uint8)
        self.k_ones = self.sbuf_t("k_ones", [128, 128], F32R)
        self.k_zero = self.sbuf_t("k_zero", [128, 128], F32)
        self.k_mm = self.sbuf_t("k_mm", [128, 4 * 512], F32)
        self.k_am = self.sbuf_t("k_am", [128, 4 * 512], F32)
        zb = Buf("zero")
        self.gop(lambda e: e.memset(self.k_zero.ap(), 0.0), [], [self.sbbuf("k_zero")])
        self.load(self.k_strictu.ap(), self.c_strictu.ap(), zb, self.sbbuf("k_strictu"))
        for dst, src, q in [(self.k_id, self.c_id, "Y"), (self.k_strict, self.c_strict, "Y"),
                            (self.k_mm, self.c_mm, "Y"), (self.k_am, self.c_am, "Y"),
                            (self.k_tri, self.c_tri, "G"), (self.k_ones, self.c_ones, "G")]:
            self.load(dst.ap(), src.ap(), zb, self.sbbuf(dst.name), q=q)

    def _decl_inputs(self):
        nc = self.nc
        T = self.T
        dt_in = [
            ("x", [T, D]),
            ("wq_a", [D, D]), ("wqs_a", [D, D]), ("wk_a", [D, D]), ("wks_a", [D, D]),
            ("wv_a", [D, D]), ("wo_a", [D, D]),
            ("wq_f", [D, D]), ("wk_f", [D, D]), ("wv_f", [D, D]), ("wo_f_nat", [D, D]),
            ("a_f", [D, 16]), ("b_f", [16, F_FFN]),
            ("win", [D, F_FFN]), ("wup", [D, F_FFN]), ("wdn", [F_FFN, D]),
            ("ln1_row", [1, D]), ("ln2_row", [1, D]),
            ("wr_r", [D, D]), ("wk_r", [D, D]), ("wv_r", [D, D]), ("wg_r", [D, D]),
            ("wo_r", [D, D]),
            ("maa_w1", [D, 160]), ("maa_w2", [5, 32, D]),
            ("dec_w1", [D, 64]), ("dec_w2", [64, D]),
            ("maa_cols", [128, 6 * 16]),
            ("decay_row", [1, D]),
            ("u_col", [128, 16]),
            ("lnr_cols", [128, 32]),
        ]
        for name, shape in dt_in:
            self.dram_t(name, shape, F32, kind="ExternalInput")
        self.dram_t("out", [T, D], F32, kind="ExternalOutput")

    def _decl_scratch(self):
        T = self.T
        self.dram_t("R", [T, D])
        self.dram_t("XN", [1 + T, D])
        self.dram_t("XNF_H", [NCH, 128, 1 + T], F32R)
        self.dram_t("XNF_L", [NCH, 128, 1 + T], F32R)
        self.dram_t("XNF_F", [NCH, 128, 1 + T])
        self.dram_t("KF_H", [NCH, 128, T], F32R)
        self.dram_t("KF_L", [NCH, 128, T], F32R)
        self.dram_t("VT_H", [T, D], F32R)
        self.dram_t("VT_L", [T, D], F32R)
        self.dram_t("AOF_H", [NCH, 128, T], F32R)
        self.dram_t("AOF_L", [NCH, 128, T], F32R)
        self.dram_t("BF_H", [NCH, 128, T], F32R)
        self.dram_t("BF_L", [NCH, 128, T], F32R)
        self.dram_t("BF2_H", [NCH, 128, T], F32R)
        self.dram_t("BF2_L", [NCH, 128, T], F32R)
        self.dram_t("RW_R", [NCH, 128, T])
        self.dram_t("RW_K", [T, D])
        self.dram_t("RW_V", [T, D])
        self.dram_t("RW_G", [NCH, 128, T])
        self.dram_t("RW_W", [T, D])
        self.dram_t("ROWTMP", [16, D])
        self.dram_t("ZF_H", [16, T], F32R)
        self.dram_t("ZF_L", [16, T], F32R)

    def transpose_group(self, bank, src_aps, rbufs):
        idap = self.k_id.ap()
        def emit(e):
            last = None
            for i, ap in enumerate(src_aps):
                last = e.transpose(self.ps_ap(bank, 128, i * 128), ap, idap)
            return last
        return self._op("T", emit, rbufs + [self.sbbuf("k_id")], [self.psb[bank]])

    def ln_norm_tile(self, vx, vsq, vst, eps, kind="ln"):
        xap = self.v_ap(vx)
        sqap = self.v_ap(vsq)
        st = self.v_ap(vst)
        bx, bsq, bst = self.v_buf(vx), self.v_buf(vsq), self.v_buf(vst)
        self.sop(lambda e: e.activation(sqap, xap, AF.Square), [bx], [bsq])
        if kind == "ln":
            def red(e):
                e.tensor_reduce(st[:, 0:1], xap, AX.X, ALU.add)
                return e.tensor_reduce(st[:, 1:2], sqap, AX.X, ALU.add)
            self.vop(red, [bx, bsq], [bst])
            def stats(e):
                e.tensor_scalar(st[:, 0:1], st[:, 0:1], 1.0 / D, None, ALU.mult)
                e.tensor_scalar(st[:, 1:2], st[:, 1:2], 1.0 / D, None, ALU.mult)
                e.drain()
                e.tensor_tensor(st[:, 2:3], st[:, 0:1], st[:, 0:1], ALU.mult)
                e.drain()
                return e.tensor_tensor(st[:, 2:3], st[:, 1:2], st[:, 2:3], ALU.subtract)
            self.vop(stats, [bst], [bst])
        else:
            self.vop(lambda e: e.tensor_reduce(st[:, 1:2], sqap, AX.X, ALU.add), [bsq], [bst])
            self.vop(lambda e: e.tensor_scalar(st[:, 2:3], st[:, 1:2], 1.0 / D, None, ALU.mult), [bst], [bst])
        def rs(e):
            e.activation(st[:, 3:4], st[:, 2:3], AF.Ln, bias=float(eps))
            e.drain()
            return e.activation(st[:, 3:4], st[:, 3:4], AF.Exp, scale=-0.5)
        self.sop(rs, [bst], [bst])
        if kind == "ln":
            self.vop(lambda e: e.tensor_scalar(xap, xap, st[:, 0:1], st[:, 3:4],
                                               ALU.subtract, ALU.mult), [bst, bx], [bx])
        else:
            self.vop(lambda e: e.tensor_scalar(xap, xap, st[:, 3:4], None, ALU.mult),
                     [bst, bx], [bx])

    def stage_ln(self, src_name, write_f32=False, eps=EPS_LN):
        T = self.T
        layout = [(f"x{tt}", [128, D], F32) for tt in range(4)]
        layout += [("sq", [128, D], F32), ("st", [128, 8], F32),
                   ("th", [128, 512], F32R), ("tl", [128, 512], F32R),
                   ("tf", [128, 512], F32)]
        self.stage(layout)
        src = self.dram[src_name]
        dsrc = self.dbuf(src_name)
        XN, XNF_H, XNF_L, XNF_F = (self.dram[n] for n in ("XN", "XNF_H", "XNF_L", "XNF_F"))
        bXN, bH, bL, bF = (self.dbuf(n) for n in ("XN", "XNF_H", "XNF_L", "XNF_F"))
        for g in range(T // 512):
            for tt in range(4):
                t0 = g * 512 + tt * 128
                vx = f"x{tt}"
                self.load(self.v_ap(vx), src.ap()[t0:t0 + 128, :], dsrc, self.v_buf(vx))
                self.ln_norm_tile(vx, "sq", "st", eps, kind="ln")
                self.store(XN.ap()[1 + t0: 1 + t0 + 128, :], self.v_ap(vx),
                           self.v_buf(vx), bXN)
            for c in range(NCH):
                bank = c % 2
                srcs = [self.v_ap(f"x{tt}")[:, c * 128:(c + 1) * 128] for tt in range(4)]
                self.transpose_group(bank, srcs, [self.v_buf(f"x{tt}") for tt in range(4)])
                th, tl, tf = self.v_ap("th"), self.v_ap("tl"), self.v_ap("tf")
                bth, btl, btf = self.v_buf("th"), self.v_buf("tl"), self.v_buf("tf")
                self.split(th, tl, self.ps_ap(bank), [self.psb[bank]], bth, btl)
                self.store(XNF_H.ap()[c, :, 1 + g * 512: 1 + g * 512 + 512], th,
                           bth, bH)
                self.store(XNF_L.ap()[c, :, 1 + g * 512: 1 + g * 512 + 512], tl,
                           btl, bL, q="S")
                if write_f32:
                    self.vop(lambda e, tf=tf, bank=bank: e.tensor_copy(tf, self.ps_ap(bank)),
                             [self.psb[bank]], [btf])
                    self.store(XNF_F.ap()[c, :, 1 + g * 512: 1 + g * 512 + 512], tf,
                               btf, bF)

    def stage_zero_pads(self):
        self.stage([("z", [128, D], F32)])
        z = self.v_ap("z")
        bz = self.v_buf("z")
        self.vop(lambda e: e.memset(z, 0.0), [], [bz])
        self.store(self.dram["XN"].ap()[0:1, :], z[0:1, :],
                   bz, self.dbuf("XN"))
        for name in ("XNF_H", "XNF_L", "XNF_F"):
            self.store(self.dram[name].ap()[:, :, 0:1].rearrange("c p o -> p (c o)"),
                       z[:, 0:NCH], bz, self.dbuf(name), q="G", slow=True)

    def stage_copy_out(self, src="R"):
        T = self.T
        self.stage([("t0", [128, D], F32), ("t1", [128, D], F32)])
        for tt in range(T // 128):
            v = f"t{tt % 2}"
            self.load(self.v_ap(v), self.dram[src].ap()[tt * 128:(tt + 1) * 128, :],
                      self.dbuf(src), self.v_buf(v))
            self.store(self.dram["out"].ap()[tt * 128:(tt + 1) * 128, :], self.v_ap(v),
                       self.v_buf(v), self.dbuf("out"))

    def finalize(self):
        nc = self.nc
        toks = self.sc.barrier_tokens()
        self.sc.op("G", lambda e: None, deps=toks)
        self._ctx_sems = []
        sems_cmp = {}
        sems_dma = {}
        with_list = []
        for name in ("T", "V", "S", "G", "Y"):
            g = nc.semaphore(f"sc_{name}")
            h = g.__enter__()
            self._ctx_sems.append(g)
            sems_cmp[name] = h
            lst = []
            for i in range(Sched.NSEM_DMA):
                gg = nc.semaphore(f"sd_{name}{i}")
                hh = gg.__enter__()
                self._ctx_sems.append(gg)
                lst.append(hh)
            sems_dma[name] = lst
        blk = nc.Block()
        block = blk.__enter__()
        self.sc.emit(block, sems_cmp, sems_dma)
        blk.__exit__(None, None, None)
        for g in reversed(self._ctx_sems):
            g.__exit__(None, None, None)
        self._psum_guard.__exit__(None, None, None)
        return nc

    def wcol_fm(self, wname, j0, w, vst, vhi, vlo, kcn=NCH, scale_col=None, scale_const=None):
        W = self.dram[wname]
        bW = self.dbuf(wname)
        nview = self.v_ap(vst).shape[1]
        st_ap = self.v_ap(vst).rearrange("p (k w) -> p k w", k=kcn)[:, :, 0:w]
        src = W.ap().rearrange("(k p) o -> p k o", p=128)[:, 0:kcn, j0:j0 + w]
        bst, bhi, blo = self.v_buf(vst), self.v_buf(vhi), self.v_buf(vlo)
        self.load(st_ap, src, bW, bst)
        extra_r = []
        if scale_col is not None or scale_const is not None:
            sap, sbuf = scale_col if scale_col is not None else (None, None)
            full_ap = self.v_ap(vst)
            def sc(e, full_ap=full_ap):
                last = None
                for kc in range(kcn):
                    sl = st_ap[:, kc, :]
                    if sap is not None:
                        last = e.tensor_scalar(sl, sl, sap[:, kc:kc + 1], None, ALU.mult)
                if scale_const is not None:
                    e.drain()
                    last = e.tensor_scalar(full_ap, full_ap, float(scale_const), None, ALU.mult)
                return last
            rr = [bst] + ([sbuf] if sbuf is not None else [])
            self.vop(sc, rr, [bst])
        self.split(self.v_ap(vhi), self.v_ap(vlo), self.v_ap(vst), [bst], bhi, blo)

    def wtile8(self, wname, kc0, c0, vst, vhi, vlo, q="Y"):
        W = self.dram[wname]
        st = self.v_ap(vst).rearrange("p (k w) -> p k w", k=4)
        src = W.ap().rearrange("(k p) o -> p k o", p=128)[:, kc0:kc0 + 4, c0:c0 + 512]
        self.load(st, src, self.dbuf(wname), self.v_buf(vst), q=q)
        self.split(self.v_ap(vhi), self.v_ap(vlo), self.v_ap(vst),
                   [self.v_buf(vst)], self.v_buf(vhi), self.v_buf(vlo))

    def wtile(self, wname, r0, rn, c0, cn, vst, vhi, vlo, q="Y"):
        W = self.dram[wname]
        st = self.v_ap(vst)[:, 0:cn]
        self.load(st[0:rn, :], W.ap()[r0:r0 + rn, c0:c0 + cn], self.dbuf(wname),
                  self.v_buf(vst), q=q)
        self.split(self.v_ap(vhi)[0:rn, 0:cn], self.v_ap(vlo)[0:rn, 0:cn], st[0:rn, :],
                   [self.v_buf(vst)], self.v_buf(vhi), self.v_buf(vlo))

    def mm3_pairs(self, vwh, vwl, vxh, vxl, kcn, msl=None, nsl=None):
        wh = self.v_ap(vwh).rearrange("p (k w) -> p k w", k=kcn)
        wl = self.v_ap(vwl).rearrange("p (k w) -> p k w", k=kcn)
        xh = self.v_ap(vxh).rearrange("p (k w) -> p k w", k=kcn)
        xl = self.v_ap(vxl).rearrange("p (k w) -> p k w", k=kcn)
        pairs = []
        for kc in range(kcn):
            a, al = wh[:, kc, :], wl[:, kc, :]
            b, bl = xh[:, kc, :], xl[:, kc, :]
            if msl is not None:
                a, al = a[:, msl], al[:, msl]
            if nsl is not None:
                b, bl = b[:, nsl], bl[:, nsl]
            pairs.append((a, al, b, bl))
        return pairs

    def rope_swap(self, bank, t1, b1, t2, b2, t3, b3, cos_ap, sin_ap):
        self.vop(lambda e, t1=t1, bank=bank: e.tensor_copy(t1, self.ps_ap(bank)),
                 [self.psb[bank]], [b1])
        self.store(t2[64:128, :], t1[0:64, :], b1, b2, q="G")
        self.store(t2[0:64, :], t1[64:128, :], b1, b2, q="G")
        def rp(e, t1=t1, t2=t2, t3=t3, cos_ap=cos_ap, sin_ap=sin_ap):
            e.tensor_tensor(t3, t1, cos_ap, ALU.mult)
            e.tensor_tensor(t2, t2, sin_ap, ALU.mult)
            e.drain()
            return e.tensor_tensor(t3, t3, t2, ALU.add)
        self.vop(rp, [b1, b2, self.v_buf("cosb"), self.v_buf("sinb")], [b2, b3])

    def stage_attn_a(self, resid_src="x", resid_dst="R"):
        T = self.T
        NB = T // 512
        self.stage([
            ("xnh", [128, NCH * 512], F32R), ("xnl", [128, NCH * 512], F32R),
            ("wst", [128, 2048], F32), ("wh", [128, 2048], F32R), ("wl", [128, 2048], F32R),
            ("wst2", [128, 2048], F32), ("wh2", [128, 2048], F32R), ("wl2", [128, 2048], F32R),
            ("vwst", [128, 4 * 512], F32), ("vwh", [128, 4 * 512], F32R), ("vwl", [128, 4 * 512], F32R),
            ("vwst2", [128, 4 * 512], F32), ("vwh2", [128, 4 * 512], F32R), ("vwl2", [128, 4 * 512], F32R),
            ("cosb", [128, 512], F32), ("sinb", [128, 512], F32),
            ("t1", [128, 512], F32), ("t2", [128, 512], F32), ("t3", [128, 512], F32),
            ("sh", [128, 512], F32R), ("sl", [128, 512], F32R),
        ])
        XNH, XNL = self.dram["XNF_H"], self.dram["XNF_L"]
        bXNH, bXNL = self.dbuf("XNF_H"), self.dbuf("XNF_L")
        for kb in range(NB):
            t0 = kb * 512
            xh_ap = self.v_ap("xnh").rearrange("p (k w) -> p k w", k=NCH)
            xl_ap = self.v_ap("xnl").rearrange("p (k w) -> p k w", k=NCH)
            self.load(xh_ap, XNH.ap().rearrange("c p t -> p c t")[:, :, 1 + t0:1 + t0 + 512],
                      bXNH, self.v_buf("xnh"))
            self.load(xl_ap, XNL.ap().rearrange("c p t -> p c t")[:, :, 1 + t0:1 + t0 + 512],
                      bXNL, self.v_buf("xnl"), q="S")
            self.load(self.v_ap("cosb"), self.c_cos.ap()[:, t0:t0 + 512], Buf(""), self.v_buf("cosb"))
            self.load(self.v_ap("sinb"), self.c_sin.ap()[:, t0:t0 + 512], Buf(""), self.v_buf("sinb"))
            for h in range(HEAD128):
                sf = "2" if h % 2 else ""
                self.wcol_fm("wk_a", h * 128, 128, "wst" + sf, "wh" + sf, "wl" + sf)
                pairs = self.mm3_pairs("wh" + sf, "wl" + sf, "xnh", "xnl", NCH)
                self.mm3(self.ps_ap(0), pairs,
                         [self.v_buf("wh" + sf), self.v_buf("wl" + sf),
                          self.v_buf("xnh"), self.v_buf("xnl")],
                         self.psb[0])
                t1, t2, t3 = self.v_ap("t1"), self.v_ap("t2"), self.v_ap("t3")
                b1, b2, b3 = self.v_buf("t1"), self.v_buf("t2"), self.v_buf("t3")
                cos_ap, sin_ap = self.v_ap("cosb"), self.v_ap("sinb")
                self.rope_swap(0, t1, b1, t2, b2, t3, b3, cos_ap, sin_ap)
                self.split(self.v_ap("sh"), self.v_ap("sl"), t3, [b3],
                           self.v_buf("sh"), self.v_buf("sl"))
                self.store(self.dram["KF_H"].ap()[h, :, t0:t0 + 512], self.v_ap("sh"),
                           self.v_buf("sh"), self.dbuf("KF_H"))
                self.store(self.dram["KF_L"].ap()[h, :, t0:t0 + 512], self.v_ap("sl"),
                           self.v_buf("sl"), self.dbuf("KF_L"), q="S")
            for of in range(4):
                for kh_ in range(4):
                    vsf = "2" if kh_ % 2 else ""
                    self.wtile8("wv_a", kh_ * 4, of * 512, "vwst" + vsf, "vwh" + vsf, "vwl" + vsf)
                    vwh = self.v_ap("vwh" + vsf).rearrange("p (k w) -> p k w", k=4)
                    vwl = self.v_ap("vwl" + vsf).rearrange("p (k w) -> p k w", k=4)
                    for mt in range(4):
                        msl = slice(mt * 128, mt * 128 + 128)
                        pairs = [(xh_ap[:, kh_ * 4 + kc, msl], xl_ap[:, kh_ * 4 + kc, msl],
                                  vwh[:, kc, :], vwl[:, kc, :]) for kc in range(4)]
                        self.mm3(self.ps_ap(2 + mt), pairs,
                                 [self.v_buf("vwh" + vsf), self.v_buf("vwl" + vsf),
                                  self.v_buf("xnh"), self.v_buf("xnl")],
                                 self.psb[2 + mt], start=(kh_ == 0), stop=(kh_ == 3))
                for mt in range(4):
                    self.split(self.v_ap("sh"), self.v_ap("sl"), self.ps_ap(2 + mt),
                               [self.psb[2 + mt]], self.v_buf("sh"), self.v_buf("sl"))
                    r0 = t0 + mt * 128
                    self.store(self.dram["VT_H"].ap()[r0:r0 + 128, of * 512:of * 512 + 512],
                               self.v_ap("sh"), self.v_buf("sh"), self.dbuf("VT_H"))
                    self.store(self.dram["VT_L"].ap()[r0:r0 + 128, of * 512:of * 512 + 512],
                               self.v_ap("sl"), self.v_buf("sl"), self.dbuf("VT_L"), q="S")

        self.stage([
            ("xnh", [128, NCH * 512], F32R), ("xnl", [128, NCH * 512], F32R),
            ("wst", [128, 2048], F32), ("wh", [128, 2048], F32R), ("wl", [128, 2048], F32R),
            ("wst2", [128, 2048], F32), ("wh2", [128, 2048], F32R), ("wl2", [128, 2048], F32R),
            ("kh", [128, 2048], F32R), ("kl", [128, 2048], F32R),
            ("vh", [128, 16 * 128], F32R), ("vl", [128, 16 * 128], F32R),
            ("qh", [128, 512], F32R), ("ql", [128, 512], F32R),
            ("cosb", [128, 512], F32), ("sinb", [128, 512], F32),
            ("t1", [128, 512], F32), ("t2", [128, 512], F32), ("t3", [128, 512], F32),
            ("t4", [128, 512], F32), ("drec", [128, 512], F32),
            ("sh", [128, 512], F32R), ("sl", [128, 512], F32R),
            ("oh", [128, 512], F32R), ("ol", [128, 512], F32R),
        ])
        for qb in range(NB):
            q0 = qb * 512
            xh_ap = self.v_ap("xnh").rearrange("p (k w) -> p k w", k=NCH)
            xl_ap = self.v_ap("xnl").rearrange("p (k w) -> p k w", k=NCH)
            self.load(xh_ap, XNH.ap().rearrange("c p t -> p c t")[:, :, 1 + q0:1 + q0 + 512],
                      bXNH, self.v_buf("xnh"))
            self.load(xl_ap, XNL.ap().rearrange("c p t -> p c t")[:, :, 1 + q0:1 + q0 + 512],
                      bXNL, self.v_buf("xnl"), q="S")
            self.load(self.v_ap("cosb"), self.c_cos.ap()[:, q0:q0 + 512], Buf(""), self.v_buf("cosb"))
            self.load(self.v_ap("sinb"), self.c_sin.ap()[:, q0:q0 + 512], Buf(""), self.v_buf("sinb"))
            for h in range(HEAD128):
                sf = "2" if h % 2 else ""
                self.wcol_fm("wq_a", h * 128, 128, "wst" + sf, "wh" + sf, "wl" + sf)
                pairs = self.mm3_pairs("wh" + sf, "wl" + sf, "xnh", "xnl", NCH)
                self.mm3(self.ps_ap(0), pairs,
                         [self.v_buf("wh" + sf), self.v_buf("wl" + sf),
                          self.v_buf("xnh"), self.v_buf("xnl")],
                         self.psb[0])
                t1, t2, t3, t4 = (self.v_ap(x) for x in ("t1", "t2", "t3", "t4"))
                b1, b2, b3, b4 = (self.v_buf(x) for x in ("t1", "t2", "t3", "t4"))
                cos_ap, sin_ap = self.v_ap("cosb"), self.v_ap("sinb")
                self.rope_swap(0, t1, b1, t2, b2, t3, b3, cos_ap, sin_ap)
                self.split(self.v_ap("qh"), self.v_ap("ql"), t3, [b3],
                           self.v_buf("qh"), self.v_buf("ql"))
                first_av = True
                nkt = 4 * (qb + 1)
                self.load(self.v_ap("kh")[:, 0:512 * (qb + 1)],
                          self.dram["KF_H"].ap()[h, :, 0:512 * (qb + 1)],
                          self.dbuf("KF_H"), self.v_buf("kh"))
                self.load(self.v_ap("kl")[:, 0:512 * (qb + 1)],
                          self.dram["KF_L"].ap()[h, :, 0:512 * (qb + 1)],
                          self.dbuf("KF_L"), self.v_buf("kl"), q="S")
                vh_ap = self.v_ap("vh").rearrange("p (k w) -> p k w", k=16)
                vl_ap = self.v_ap("vl").rearrange("p (k w) -> p k w", k=16)
                self.load(vh_ap[:, 0:nkt, :], self.dram["VT_H"].ap()
                          .rearrange("(kt p) d -> p kt d", p=128)[:, 0:nkt, h * 128:h * 128 + 128],
                          self.dbuf("VT_H"), self.v_buf("vh"))
                self.load(vl_ap[:, 0:nkt, :], self.dram["VT_L"].ap()
                          .rearrange("(kt p) d -> p kt d", p=128)[:, 0:nkt, h * 128:h * 128 + 128],
                          self.dbuf("VT_L"), self.v_buf("vl"), q="S")
                for kb2 in range(qb + 1):
                    k0 = kb2 * 512
                    for kt in range(4):
                        gkt = 4 * kb2 + kt
                        sb_ = 4 + (kt % 2)
                        pairs = [(self.v_ap("kh")[:, gkt * 128:gkt * 128 + 128],
                                  self.v_ap("kl")[:, gkt * 128:gkt * 128 + 128],
                                  self.v_ap("qh"), self.v_ap("ql"))]
                        self.mm3(self.ps_ap(sb_), pairs,
                                 [self.v_buf("kh"), self.v_buf("kl"), self.v_buf("qh"), self.v_buf("ql")],
                                 self.psb[sb_])
                        glob_kt = 4 * kb2 + kt
                        sub = glob_kt - 4 * qb
                        self.sop(lambda e, sb_=sb_, t1=t1: e.activation(t1, self.ps_ap(sb_), AF.Square),
                                 [self.psb[sb_]], [b1])
                        if 0 <= sub < 4:
                            mask_ap = self.k_mm.ap()[:, sub * 512:sub * 512 + 512]
                            self.vop(lambda e, t1=t1, mask_ap=mask_ap: e.tensor_tensor(t1, t1, mask_ap, ALU.mult),
                                     [b1, self.sbbuf("k_mm")], [b1])
                        self.sop(lambda e, t1=t1, t2=t2: e.activation(t2, t1, AF.Square),
                                 [b1], [b2])
                        self.split(self.v_ap("sh"), self.v_ap("sl"), t2, [b2],
                                   self.v_buf("sh"), self.v_buf("sl"))
                        pairs = [(vh_ap[:, gkt, :], vl_ap[:, gkt, :],
                                  self.v_ap("sh"), self.v_ap("sl"))]
                        last_av = (kb2 == qb and kt == 3)
                        self.mm3(self.ps_ap(2), pairs,
                                 [self.v_buf("vh"), self.v_buf("vl"), self.v_buf("sh"), self.v_buf("sl")],
                                 self.psb[2], start=first_av, stop=last_av)
                        self.mm1(self.ps_ap(3), [(self.k_ones.ap(), self.v_ap("sh")),
                                                 (self.k_ones.ap(), self.v_ap("sl"))],
                                 [self.sbbuf("k_ones"), self.v_buf("sh"), self.v_buf("sl")],
                                 self.psb[3], start=first_av, stop=last_av)
                        first_av = False
                drec, bdr = self.v_ap("drec"), self.v_buf("drec")
                def dn(e, drec=drec):
                    e.tensor_scalar(drec, self.ps_ap(3), float(EPS_A), None, ALU.max)
                    e.drain()
                    return e.reciprocal(drec, drec)
                self.vop(dn, [self.psb[3]], [bdr])
                self.vop(lambda e, t4=t4, drec=drec: e.tensor_tensor(t4, self.ps_ap(2), drec, ALU.mult),
                         [self.psb[2], bdr], [b4])
                self.split(self.v_ap("oh"), self.v_ap("ol"), t4, [b4],
                           self.v_buf("oh"), self.v_buf("ol"))
                self.store(self.dram["AOF_H"].ap()[h, :, q0:q0 + 512], self.v_ap("oh"),
                           self.v_buf("oh"), self.dbuf("AOF_H"))
                self.store(self.dram["AOF_L"].ap()[h, :, q0:q0 + 512], self.v_ap("ol"),
                           self.v_buf("ol"), self.dbuf("AOF_L"), q="S")

        self.pass_out_proj("AOF_H", "AOF_L", "wo_a", resid_src, resid_dst)

    def pass_out_proj(self, aof_h, aof_l, wname, resid_src, resid_dst, extra_tm_add=None):
        T = self.T
        self.stage([
            ("aoh", [128, NCH * 512], F32R), ("aol", [128, NCH * 512], F32R),
            ("vwst", [128, 4 * 512], F32), ("vwh", [128, 4 * 512], F32R), ("vwl", [128, 4 * 512], F32R),
            ("vwst2", [128, 4 * 512], F32), ("vwh2", [128, 4 * 512], F32R), ("vwl2", [128, 4 * 512], F32R),
            ("rt", [128, 512], F32), ("rt2", [128, 512], F32),
        ])
        AH, AL = self.dram[aof_h], self.dram[aof_l]
        for sc in range(T // 512):
            s0 = sc * 512
            ah = self.v_ap("aoh").rearrange("p (k w) -> p k w", k=NCH)
            al = self.v_ap("aol").rearrange("p (k w) -> p k w", k=NCH)
            off = 1 if AH.shape[2] == self.T + 1 else 0
            self.load(ah, AH.ap().rearrange("c p t -> p c t")[:, :, off + s0: off + s0 + 512],
                      self.dbuf(aof_h), self.v_buf("aoh"))
            self.load(al, AL.ap().rearrange("c p t -> p c t")[:, :, off + s0: off + s0 + 512],
                      self.dbuf(aof_l), self.v_buf("aol"), q="S")
            for of in range(4):
                for kh_ in range(4):
                    vsf = "2" if kh_ % 2 else ""
                    self.wtile8(wname, kh_ * 4, of * 512, "vwst" + vsf, "vwh" + vsf, "vwl" + vsf)
                    vwh = self.v_ap("vwh" + vsf).rearrange("p (k w) -> p k w", k=4)
                    vwl = self.v_ap("vwl" + vsf).rearrange("p (k w) -> p k w", k=4)
                    for mt in range(4):
                        msl = slice(mt * 128, mt * 128 + 128)
                        pairs = [(ah[:, kh_ * 4 + kc, msl], al[:, kh_ * 4 + kc, msl],
                                  vwh[:, kc, :], vwl[:, kc, :]) for kc in range(4)]
                        self.mm3(self.ps_ap(2 + mt), pairs,
                                 [self.v_buf("vwh" + vsf), self.v_buf("vwl" + vsf),
                                  self.v_buf("aoh"), self.v_buf("aol")],
                                 self.psb[2 + mt], start=(kh_ == 0), stop=(kh_ == 3))
                for mt in range(4):
                    r0 = s0 + mt * 128
                    rt, brt = self.v_ap("rt"), self.v_buf("rt")
                    self.load(rt, self.dram[resid_src].ap()[r0:r0 + 128, of * 512:of * 512 + 512],
                              self.dbuf(resid_src), brt)
                    rt2, brt2 = self.v_ap("rt2"), self.v_buf("rt2")
                    if extra_tm_add is not None:
                        exn, exoff = extra_tm_add
                        self.load(rt2, self.dram[exn].ap()[exoff + r0: exoff + r0 + 128,
                                                           of * 512:of * 512 + 512],
                                  self.dbuf(exn), brt2, q="S")
                        def ev(e, mt=mt, rt=rt, rt2=rt2):
                            e.tensor_tensor(rt, rt, rt2, ALU.add)
                            e.drain()
                            return e.tensor_tensor(rt, rt, self.ps_ap(2 + mt), ALU.add)
                        self.vop(ev, [self.psb[2 + mt], brt, brt2], [brt])
                    else:
                        self.vop(lambda e, mt=mt, rt=rt: e.tensor_tensor(rt, rt, self.ps_ap(2 + mt), ALU.add),
                                 [self.psb[2 + mt], brt], [brt])
                    self.store(self.dram[resid_dst].ap()[r0:r0 + 128, of * 512:of * 512 + 512],
                               rt, brt, self.dbuf(resid_dst))

    def stage_woa(self):
        self.k_m = self.sbuf_t("k_m", [128, 256], F32)
        self.k_mh = self.sbuf_t("k_mh", [128, 256], F32R)
        self.k_ml = self.sbuf_t("k_ml", [128, 256], F32R)
        self.stage([
            ("wst", [128, 128], F32), ("wh", [128, 128], F32R), ("wl", [128, 128], F32R),
            ("ast", [128, 16 * 16], F32), ("ah", [128, 256], F32R), ("al", [128, 256], F32R),
        ])
        a3 = self.dram["a_f"].ap().rearrange("(k p) r -> p k r", p=128)
        self.load(self.v_ap("ast").rearrange("p (k r) -> p k r", k=16), a3,
                  self.dbuf("a_f"), self.v_buf("ast"))
        self.split(self.v_ap("ah"), self.v_ap("al"), self.v_ap("ast"),
                   [self.v_buf("ast")], self.v_buf("ah"), self.v_buf("al"))
        ah = self.v_ap("ah").rearrange("p (k r) -> p k r", k=16)
        al = self.v_ap("al").rearrange("p (k r) -> p k r", k=16)
        for ic in range(16):
            for oc in range(16):
                self.wtile("wo_f_nat", oc * 128, 128, ic * 128, 128, "wst", "wh", "wl")
                pairs = [(self.v_ap("wh")[:, 0:128], self.v_ap("wl")[:, 0:128],
                          ah[:, oc, :], al[:, oc, :])]
                self.mm3(self.ps_ap(0, 16, ic * 16), pairs,
                         [self.v_buf("wh"), self.v_buf("wl"), self.v_buf("ah"), self.v_buf("al")],
                         self.psb[0], start=(oc == 0), stop=(oc == 15))
            self.vop(lambda e, ic=ic: e.tensor_copy(self.k_m.ap()[:, ic * 16:ic * 16 + 16],
                                                    self.ps_ap(0, 16, ic * 16)),
                     [self.psb[0]], [self.sbbuf("k_m")])
        self.split(self.k_mh.ap(), self.k_ml.ap(), self.k_m.ap(),
                   [self.sbbuf("k_m")], self.sbbuf("k_mh"), self.sbbuf("k_ml"))

    def stage_rms_base(self):
        T = self.T
        layout = [(f"x{tt}", [128, D], F32) for tt in range(4)]
        layout += [("sq", [128, D], F32), ("st", [128, 8], F32),
                   ("tf", [128, 512], F32),
                   ("th", [128, 512], F32R), ("tl", [128, 512], F32R),
                   ("w1c", [128, 16], F32), ("w2c", [128, 16], F32)]
        self.stage(layout)
        self._row_to_col16("ln1_row", self.v_ap("w1c"), self.v_buf("w1c"))
        self._row_to_col16("ln2_row", self.v_ap("w2c"), self.v_buf("w2c"))
        XN = self.dram["XN"]
        for g in range(T // 512):
            for tt in range(4):
                t0 = g * 512 + tt * 128
                vx = f"x{tt}"
                self.load(self.v_ap(vx), XN.ap()[1 + t0:1 + t0 + 128, :], self.dbuf("XN"),
                          self.v_buf(vx))
                self.ln_norm_tile(vx, "sq", "st", EPS_RMS, kind="rms")
            for c in range(NCH):
                bank = c % 2
                srcs = [self.v_ap(f"x{tt}")[:, c * 128:(c + 1) * 128] for tt in range(4)]
                self.transpose_group(bank, srcs, [self.v_buf(f"x{tt}") for tt in range(4)])
                tf, th, tl = self.v_ap("tf"), self.v_ap("th"), self.v_ap("tl")
                for (wv, dh, dl, qq) in (("w1c", "BF_H", "BF_L", "G"),
                                         ("w2c", "BF2_H", "BF2_L", "S")):
                    wap = self.v_ap(wv)
                    self.vop(lambda e, tf=tf, bank=bank, wap=wap, c=c: e.tensor_scalar(
                        tf, self.ps_ap(bank), wap[:, c:c + 1], None, ALU.mult),
                        [self.psb[bank], self.v_buf(wv)], [self.v_buf("tf")])
                    self.split(th, tl, tf, [self.v_buf("tf")],
                               self.v_buf("th"), self.v_buf("tl"))
                    self.store(self.dram[dh].ap()[c, :, g * 512: g * 512 + 512], th,
                               self.v_buf("th"), self.dbuf(dh), q=qq)
                    self.store(self.dram[dl].ap()[c, :, g * 512: g * 512 + 512], tl,
                               self.v_buf("tl"), self.dbuf(dl), q=qq)

    def _row_to_col16(self, row_name, dst_ap, dst_buf):
        src = self.dram[row_name].ap()[0:1, :].rearrange("o (c p) -> (o p) c", p=128)
        self.load(dst_ap, src, self.dbuf(row_name), dst_buf, slow=True)

    def stage_ffn_attn(self):
        T = self.T
        NB = T // 512
        BFH, BFL = self.dram["BF_H"], self.dram["BF_L"]
        bBFH, bBFL = self.dbuf("BF_H"), self.dbuf("BF_L")
        self.stage([
            ("xnh", [128, NCH * 512], F32R), ("xnl", [128, NCH * 512], F32R),
            ("wst", [128, 2048], F32), ("wh", [128, 2048], F32R), ("wl", [128, 2048], F32R),
            ("wst2", [128, 2048], F32), ("wh2", [128, 2048], F32R), ("wl2", [128, 2048], F32R),
            ("vwst", [128, 4 * 512], F32), ("vwh", [128, 4 * 512], F32R), ("vwl", [128, 4 * 512], F32R),
            ("vwst2", [128, 4 * 512], F32), ("vwh2", [128, 4 * 512], F32R), ("vwl2", [128, 4 * 512], F32R),
            ("t1", [128, 512], F32), ("t1r", [128, 512], F32R),
            ("sh", [128, 512], F32R), ("sl", [128, 512], F32R),
            ("kssb", [128, 1], F32),
        ])
        self.k_kss = getattr(self, "k_kss", None) or self.sbuf_t("k_kss", [128, 16], F32)
        self.vop(lambda e: e.memset(self.k_kss.ap(), 0.0), [], [self.sbbuf("k_kss")])
        for kb in range(NB):
            t0 = kb * 512
            xh_ap = self.v_ap("xnh").rearrange("p (k w) -> p k w", k=NCH)
            xl_ap = self.v_ap("xnl").rearrange("p (k w) -> p k w", k=NCH)
            self.load(xh_ap, BFH.ap().rearrange("c p t -> p c t")[:, :, t0:t0 + 512],
                      bBFH, self.v_buf("xnh"))
            self.load(xl_ap, BFL.ap().rearrange("c p t -> p c t")[:, :, t0:t0 + 512],
                      bBFL, self.v_buf("xnl"), q="S")
            for h in range(HEAD128):
                sf = "2" if h % 2 else ""
                self.wcol_fm("wk_f", h * 128, 128, "wst" + sf, "wh" + sf, "wl" + sf)
                pairs = self.mm3_pairs("wh" + sf, "wl" + sf, "xnh", "xnl", NCH)
                self.mm3(self.ps_ap(0), pairs,
                         [self.v_buf("wh" + sf), self.v_buf("wl" + sf),
                          self.v_buf("xnh"), self.v_buf("xnl")],
                         self.psb[0])
                t1, b1 = self.v_ap("t1"), self.v_buf("t1")
                t1r, b1r = self.v_ap("t1r"), self.v_buf("t1r")
                self.sop(lambda e, t1=t1: e.activation(t1, self.ps_ap(0), AF.Square),
                         [self.psb[0]], [b1])
                self.vop(lambda e, t1=t1, t1r=t1r: e.tensor_copy(t1r, t1), [b1], [b1r])
                self.mm1(self.ps_ap(1), [(self.k_ones.ap(), t1r)],
                         [self.sbbuf("k_ones"), b1r], self.psb[1])
                kssb, bkss = self.v_ap("kssb"), self.v_buf("kssb")
                self.vop(lambda e, kssb=kssb: e.tensor_reduce(kssb, self.ps_ap(1), AX.X, ALU.max),
                         [self.psb[1]], [bkss])
                kap = self.k_kss.ap()[:, h:h + 1]
                self.vop(lambda e, kssb=kssb, kap=kap: e.tensor_tensor(kap, kap, kssb, ALU.max),
                         [bkss, self.sbbuf("k_kss")], [self.sbbuf("k_kss")])
                self.split(self.v_ap("sh"), self.v_ap("sl"), self.ps_ap(0),
                           [self.psb[0]], self.v_buf("sh"), self.v_buf("sl"))
                self.store(self.dram["KF_H"].ap()[h, :, t0:t0 + 512], self.v_ap("sh"),
                           self.v_buf("sh"), self.dbuf("KF_H"))
                self.store(self.dram["KF_L"].ap()[h, :, t0:t0 + 512], self.v_ap("sl"),
                           self.v_buf("sl"), self.dbuf("KF_L"), q="S")
            for of in range(4):
                for kh_ in range(4):
                    vsf = "2" if kh_ % 2 else ""
                    self.wtile8("wv_f", kh_ * 4, of * 512, "vwst" + vsf, "vwh" + vsf, "vwl" + vsf)
                    vwh = self.v_ap("vwh" + vsf).rearrange("p (k w) -> p k w", k=4)
                    vwl = self.v_ap("vwl" + vsf).rearrange("p (k w) -> p k w", k=4)
                    for mt in range(4):
                        msl = slice(mt * 128, mt * 128 + 128)
                        pairs = [(xh_ap[:, kh_ * 4 + kc, msl], xl_ap[:, kh_ * 4 + kc, msl],
                                  vwh[:, kc, :], vwl[:, kc, :]) for kc in range(4)]
                        self.mm3(self.ps_ap(2 + mt), pairs,
                                 [self.v_buf("vwh" + vsf), self.v_buf("vwl" + vsf),
                                  self.v_buf("xnh"), self.v_buf("xnl")],
                                 self.psb[2 + mt], start=(kh_ == 0), stop=(kh_ == 3))
                for mt in range(4):
                    self.split(self.v_ap("sh"), self.v_ap("sl"), self.ps_ap(2 + mt),
                               [self.psb[2 + mt]], self.v_buf("sh"), self.v_buf("sl"))
                    r0 = t0 + mt * 128
                    self.store(self.dram["VT_H"].ap()[r0:r0 + 128, of * 512:of * 512 + 512],
                               self.v_ap("sh"), self.v_buf("sh"), self.dbuf("VT_H"))
                    self.store(self.dram["VT_L"].ap()[r0:r0 + 128, of * 512:of * 512 + 512],
                               self.v_ap("sl"), self.v_buf("sl"), self.dbuf("VT_L"), q="S")
        self.stage([
            ("xnh", [128, NCH * 512], F32R), ("xnl", [128, NCH * 512], F32R),
            ("wst", [128, 2048], F32), ("wh", [128, 2048], F32R), ("wl", [128, 2048], F32R),
            ("wst2", [128, 2048], F32), ("wh2", [128, 2048], F32R), ("wl2", [128, 2048], F32R),
            ("kh", [128, 2048], F32R), ("kl", [128, 2048], F32R),
            ("vh", [128, 16 * 128], F32R), ("vl", [128, 16 * 128], F32R),
            ("qh", [128, 512], F32R), ("ql", [128, 512], F32R),
            ("t1", [128, 512], F32), ("t2", [128, 512], F32),
            ("t4", [128, 512], F32), ("drec", [128, 512], F32),
            ("sh", [128, 512], F32R), ("sl", [128, 512], F32R),
            ("oh", [128, 512], F32R), ("ol", [128, 512], F32R),
            ("t1r", [128, 512], F32R), ("bnd", [128, 8], F32),
        ])
        for qb in range(NB):
            q0 = qb * 512
            xh_ap = self.v_ap("xnh").rearrange("p (k w) -> p k w", k=NCH)
            xl_ap = self.v_ap("xnl").rearrange("p (k w) -> p k w", k=NCH)
            self.load(xh_ap, BFH.ap().rearrange("c p t -> p c t")[:, :, q0:q0 + 512],
                      bBFH, self.v_buf("xnh"))
            self.load(xl_ap, BFL.ap().rearrange("c p t -> p c t")[:, :, q0:q0 + 512],
                      bBFL, self.v_buf("xnl"), q="S")
            for h in range(HEAD128):
                sf = "2" if h % 2 else ""
                self.wcol_fm("wq_f", h * 128, 128, "wst" + sf, "wh" + sf, "wl" + sf,
                             scale_const=RSQ128)
                pairs = self.mm3_pairs("wh" + sf, "wl" + sf, "xnh", "xnl", NCH)
                self.mm3(self.ps_ap(0), pairs,
                         [self.v_buf("wh" + sf), self.v_buf("wl" + sf),
                          self.v_buf("xnh"), self.v_buf("xnl")],
                         self.psb[0])
                t1, b1 = self.v_ap("t1"), self.v_buf("t1")
                t2, b2 = self.v_ap("t2"), self.v_buf("t2")
                t4, b4 = self.v_ap("t4"), self.v_buf("t4")
                t1r, b1r = self.v_ap("t1r"), self.v_buf("t1r")
                bnd, bbnd = self.v_ap("bnd"), self.v_buf("bnd")
                self.sop(lambda e, t1=t1: e.activation(t1, self.ps_ap(0), AF.Square),
                         [self.psb[0]], [b1])
                self.vop(lambda e, t1=t1, t1r=t1r: e.tensor_copy(t1r, t1), [b1], [b1r])
                self.mm1(self.ps_ap(1), [(self.k_ones.ap(), t1r)],
                         [self.sbbuf("k_ones"), b1r], self.psb[1])
                kap = self.k_kss.ap()[:, h:h + 1]
                def mkbound(e, bnd=bnd, kap=kap):
                    e.tensor_reduce(bnd[:, 0:1], self.ps_ap(1), AX.X, ALU.max)
                    e.drain()
                    e.tensor_tensor(bnd[:, 1:2], bnd[:, 0:1], kap, ALU.mult)
                    return None
                self.vop(mkbound, [self.psb[1], self.sbbuf("k_kss")], [bbnd])
                def mkbound2(e, bnd=bnd):
                    e.activation(bnd[:, 2:3], bnd[:, 1:2], AF.Ln, bias=1e-30)
                    e.drain()
                    e.activation(bnd[:, 3:4], bnd[:, 2:3], AF.Exp, scale=0.5)
                    e.drain()
                    return None
                self.sop(mkbound2, [bbnd], [bbnd])
                self.vop(lambda e, bnd=bnd: e.tensor_scalar(bnd[:, 4:5], bnd[:, 3:4],
                                                            -1.0001, -0.01, ALU.mult, ALU.add),
                         [bbnd], [bbnd])
                self.split(self.v_ap("qh"), self.v_ap("ql"), self.ps_ap(0),
                           [self.psb[0]], self.v_buf("qh"), self.v_buf("ql"))
                first_av = True
                nkt = 4 * (qb + 1)
                self.load(self.v_ap("kh")[:, 0:512 * (qb + 1)],
                          self.dram["KF_H"].ap()[h, :, 0:512 * (qb + 1)],
                          self.dbuf("KF_H"), self.v_buf("kh"))
                self.load(self.v_ap("kl")[:, 0:512 * (qb + 1)],
                          self.dram["KF_L"].ap()[h, :, 0:512 * (qb + 1)],
                          self.dbuf("KF_L"), self.v_buf("kl"), q="S")
                vh_ap = self.v_ap("vh").rearrange("p (k w) -> p k w", k=16)
                vl_ap = self.v_ap("vl").rearrange("p (k w) -> p k w", k=16)
                self.load(vh_ap[:, 0:nkt, :], self.dram["VT_H"].ap()
                          .rearrange("(kt p) d -> p kt d", p=128)[:, 0:nkt, h * 128:h * 128 + 128],
                          self.dbuf("VT_H"), self.v_buf("vh"))
                self.load(vl_ap[:, 0:nkt, :], self.dram["VT_L"].ap()
                          .rearrange("(kt p) d -> p kt d", p=128)[:, 0:nkt, h * 128:h * 128 + 128],
                          self.dbuf("VT_L"), self.v_buf("vl"), q="S")
                for kb2 in range(qb + 1):
                    k0 = kb2 * 512
                    for kt in range(4):
                        gkt = 4 * kb2 + kt
                        sb_ = 4 + (kt % 2)
                        pairs = [(self.v_ap("kh")[:, gkt * 128:gkt * 128 + 128],
                                  self.v_ap("kl")[:, gkt * 128:gkt * 128 + 128],
                                  self.v_ap("qh"), self.v_ap("ql"))]
                        self.mm3(self.ps_ap(sb_), pairs,
                                 [self.v_buf("kh"), self.v_buf("kl"), self.v_buf("qh"), self.v_buf("ql")],
                                 self.psb[sb_])
                        glob_kt = 4 * kb2 + kt
                        sub = glob_kt - 4 * qb
                        nb_ap = bnd[:, 4:5]
                        if 0 <= sub < 4:
                            mask_ap = self.k_am.ap()[:, sub * 512:sub * 512 + 512]
                            self.vop(lambda e, sb_=sb_, t1=t1, mask_ap=mask_ap:
                                     e.tensor_tensor(t1, self.ps_ap(sb_), mask_ap, ALU.add),
                                     [self.psb[sb_], self.sbbuf("k_am")], [b1])
                            self.sop(lambda e, t1=t1, t2=t2, nb_ap=nb_ap:
                                     e.activation(t2, t1, AF.Exp, bias=nb_ap),
                                     [b1, bbnd], [b2])
                        else:
                            self.sop(lambda e, sb_=sb_, t2=t2, nb_ap=nb_ap:
                                     e.activation(t2, self.ps_ap(sb_), AF.Exp, bias=nb_ap),
                                     [self.psb[sb_], bbnd], [b2])
                        self.split(self.v_ap("sh"), self.v_ap("sl"), t2, [b2],
                                   self.v_buf("sh"), self.v_buf("sl"))
                        last_av = (kb2 == qb and kt == 3)
                        pairs = [(vh_ap[:, gkt, :], vl_ap[:, gkt, :],
                                  self.v_ap("sh"), self.v_ap("sl"))]
                        self.mm3(self.ps_ap(2), pairs,
                                 [self.v_buf("vh"), self.v_buf("vl"), self.v_buf("sh"), self.v_buf("sl")],
                                 self.psb[2], start=first_av, stop=last_av)
                        self.mm1(self.ps_ap(3), [(self.k_ones.ap(), self.v_ap("sh")),
                                                 (self.k_ones.ap(), self.v_ap("sl"))],
                                 [self.sbbuf("k_ones"), self.v_buf("sh"), self.v_buf("sl")],
                                 self.psb[3], start=first_av, stop=last_av)
                        first_av = False
                drec, bdr = self.v_ap("drec"), self.v_buf("drec")
                def dn(e, drec=drec):
                    e.tensor_scalar(drec, self.ps_ap(3), float(EPS_A), None, ALU.max)
                    e.drain()
                    return e.reciprocal(drec, drec)
                self.vop(dn, [self.psb[3]], [bdr])
                self.vop(lambda e, t4=t4, drec=drec: e.tensor_tensor(t4, self.ps_ap(2), drec, ALU.mult),
                         [self.psb[2], bdr], [b4])
                self.split(self.v_ap("oh"), self.v_ap("ol"), t4, [b4],
                           self.v_buf("oh"), self.v_buf("ol"))
                self.store(self.dram["AOF_H"].ap()[h, :, q0:q0 + 512], self.v_ap("oh"),
                           self.v_buf("oh"), self.dbuf("AOF_H"))
                self.store(self.dram["AOF_L"].ap()[h, :, q0:q0 + 512], self.v_ap("ol"),
                           self.v_buf("ol"), self.dbuf("AOF_L"), q="S")

    def stage_modz(self):
        T = self.T
        self.stage([
            ("aoh", [128, NCH * 512], F32R), ("aol", [128, NCH * 512], F32R),
            ("bfh", [128, NCH * 512], F32R), ("bfl", [128, NCH * 512], F32R),
            ("ast", [128, 16 * 16], F32), ("ah", [128, 256], F32R), ("al", [128, 256], F32R),
            ("tz", [16, 512], F32), ("tm", [16, 512], F32),
            ("zh", [16, 512], F32R), ("zl", [16, 512], F32R),
        ])
        a3 = self.dram["a_f"].ap().rearrange("(k p) r -> p k r", p=128)
        ast = self.v_ap("ast").rearrange("p (k r) -> p k r", k=16)
        self.load(ast, a3, self.dbuf("a_f"), self.v_buf("ast"))
        self.split(self.v_ap("ah"), self.v_ap("al"), self.v_ap("ast"),
                   [self.v_buf("ast")], self.v_buf("ah"), self.v_buf("al"))
        ah = self.v_ap("ah").rearrange("p (k r) -> p k r", k=16)
        al = self.v_ap("al").rearrange("p (k r) -> p k r", k=16)
        mh = self.k_mh.ap().rearrange("p (k r) -> p k r", k=16)
        ml = self.k_ml.ap().rearrange("p (k r) -> p k r", k=16)
        AOH, AOL = self.dram["AOF_H"], self.dram["AOF_L"]
        for qb in range(T // 512):
            q0 = qb * 512
            aoh = self.v_ap("aoh").rearrange("p (k w) -> p k w", k=NCH)
            aol = self.v_ap("aol").rearrange("p (k w) -> p k w", k=NCH)
            bfh = self.v_ap("bfh").rearrange("p (k w) -> p k w", k=NCH)
            bfl = self.v_ap("bfl").rearrange("p (k w) -> p k w", k=NCH)
            self.load(aoh, AOH.ap().rearrange("c p t -> p c t")[:, :, q0:q0 + 512],
                      self.dbuf("AOF_H"), self.v_buf("aoh"))
            self.load(aol, AOL.ap().rearrange("c p t -> p c t")[:, :, q0:q0 + 512],
                      self.dbuf("AOF_L"), self.v_buf("aol"), q="S")
            self.load(bfh, self.dram["BF2_H"].ap().rearrange("c p t -> p c t")[:, :, q0:q0 + 512],
                      self.dbuf("BF2_H"), self.v_buf("bfh"))
            self.load(bfl, self.dram["BF2_L"].ap().rearrange("c p t -> p c t")[:, :, q0:q0 + 512],
                      self.dbuf("BF2_L"), self.v_buf("bfl"), q="S")
            pairs = [(mh[:, kc, :], ml[:, kc, :], aoh[:, kc, :], aol[:, kc, :])
                     for kc in range(NCH)]
            self.mm3(self.ps_ap_p(0, 0, 16, 512), pairs,
                     [self.sbbuf("k_mh"), self.sbbuf("k_ml"), self.v_buf("aoh"), self.v_buf("aol")],
                     self.psb[0])
            pairs = [(ah[:, kc, :], al[:, kc, :], bfh[:, kc, :], bfl[:, kc, :])
                     for kc in range(NCH)]
            self.mm3(self.ps_ap_p(1, 0, 16, 512), pairs,
                     [self.v_buf("ah"), self.v_buf("al"), self.v_buf("bfh"), self.v_buf("bfl")],
                     self.psb[1])
            tz, btz = self.v_ap("tz"), self.v_buf("tz")
            tm_, btm = self.v_ap("tm"), self.v_buf("tm")
            self.sop(lambda e, tm_=tm_: e.activation(tm_, self.ps_ap_p(0, 0, 16, 512), AF.Tanh),
                     [self.psb[0]], [btm])
            self.vop(lambda e, tz=tz, tm_=tm_: e.tensor_tensor(tz, tm_, self.ps_ap_p(1, 0, 16, 512), ALU.mult),
                     [btm, self.psb[1]], [btz])
            self.split(self.v_ap("zh"), self.v_ap("zl"), tz, [btz],
                       self.v_buf("zh"), self.v_buf("zl"))
            self.store(self.dram["ZF_H"].ap()[:, q0:q0 + 512], self.v_ap("zh"),
                       self.v_buf("zh"), self.dbuf("ZF_H"))
            self.store(self.dram["ZF_L"].ap()[:, q0:q0 + 512], self.v_ap("zl"),
                       self.v_buf("zl"), self.dbuf("ZF_L"), q="S")

    def stage_mlp(self, resid_src, resid_dst):
        T = self.T
        BF_OFF, GU_OFF, ACC_OFF = 0, 64 * 1024, 96 * 1024
        for blk in range(T // 512):
            t0 = blk * 512
            for q in range(4):
                f0 = q * 16
                self.stage([
                    ("bfh", [128, NCH * 512], F32R, BF_OFF),
                    ("bfl", [128, NCH * 512], F32R, BF_OFF + 32 * 1024),
                    ("gu", [128, 16 * 512], F32, GU_OFF),
                    ("w1st", [128, 1024], F32), ("w1h", [128, 1024], F32R), ("w1l", [128, 1024], F32R),
                    ("w2h", [128, 1024], F32R), ("w2l", [128, 1024], F32R),
                    ("w1st2", [128, 1024], F32), ("w1h2", [128, 1024], F32R), ("w1l2", [128, 1024], F32R),
                    ("bst", [16, 128], F32), ("bh", [16, 128], F32R), ("bl", [16, 128], F32R),
                    ("zh", [16, 512], F32R), ("zl", [16, 512], F32R),
                    ("gax", [128, 512], F32),
                ], keep=(("bfh", "bfl") if q > 0 else ()), auto_base=ACC_OFF + 32 * 1024)
                bfh = self.v_ap("bfh").rearrange("p (k w) -> p k w", k=NCH)
                bfl = self.v_ap("bfl").rearrange("p (k w) -> p k w", k=NCH)
                if q == 0:
                    self.load(bfh, self.dram["BF2_H"].ap().rearrange("c p t -> p c t")[:, :, t0:t0 + 512],
                              self.dbuf("BF2_H"), self.v_buf("bfh"))
                    self.load(bfl, self.dram["BF2_L"].ap().rearrange("c p t -> p c t")[:, :, t0:t0 + 512],
                              self.dbuf("BF2_L"), self.v_buf("bfl"), q="S")
                self.load(self.v_ap("zh"), self.dram["ZF_H"].ap()[:, t0:t0 + 512],
                          self.dbuf("ZF_H"), self.v_buf("zh"))
                self.load(self.v_ap("zl"), self.dram["ZF_L"].ap()[:, t0:t0 + 512],
                          self.dbuf("ZF_L"), self.v_buf("zl"), q="S")
                gu = self.v_ap("gu").rearrange("p (f w) -> p f w", f=16)
                bgu = self.v_buf("gu")
                for fi in range(16):
                    fc = f0 + fi
                    bank_g, bank_u = fi % 2, 2 + fi % 2
                    sf = "" if fi % 2 == 0 else "2"
                    w1st, w1h, w1l = "w1st" + sf, "w1h" + sf, "w1l" + sf
                    w2h, w2l = "w2h", "w2l"
                    wbufs = [self.v_buf(w1h), self.v_buf(w1l),
                             self.v_buf("bfh"), self.v_buf("bfl")]
                    def hpairs(hname, lname, kc0):
                        wh = self.v_ap(hname).rearrange("p (k w) -> p k w", k=8)
                        wl = self.v_ap(lname).rearrange("p (k w) -> p k w", k=8)
                        return [(wh[:, kc, :], wl[:, kc, :],
                                 bfh[:, kc0 + kc, :], bfl[:, kc0 + kc, :]) for kc in range(8)]
                    self.wcol_fm("win", fc * 128, 128, w1st, w1h, w1l, kcn=8)
                    self.mm3(self.ps_ap(bank_g), hpairs(w1h, w1l, 0), wbufs,
                             self.psb[bank_g], start=True, stop=False)
                    self.wcol_fm2("win", fc * 128, 128, w1st, w1h, w1l, kc0=8)
                    self.wtile_b("b_f", fc, "bst", "bh", "bl")
                    pairs = hpairs(w1h, w1l, 8)
                    pairs.append((self.v_ap("bh")[0:16, 0:128], self.v_ap("bl")[0:16, 0:128],
                                  self.v_ap("zh"), self.v_ap("zl")))
                    self.mm3(self.ps_ap(bank_g), pairs,
                             wbufs + [self.v_buf("bh"), self.v_buf("bl"),
                                      self.v_buf("zh"), self.v_buf("zl")],
                             self.psb[bank_g], start=False, stop=True)
                    self.wcol_fm("wup", fc * 128, 128, w1st, w2h, w2l, kcn=8)
                    self.mm3(self.ps_ap(bank_u),
                             [(self.v_ap(w2h).rearrange("p (k w) -> p k w", k=8)[:, kc, :],
                               self.v_ap(w2l).rearrange("p (k w) -> p k w", k=8)[:, kc, :],
                               bfh[:, kc, :], bfl[:, kc, :]) for kc in range(8)],
                             [self.v_buf(w2h), self.v_buf(w2l), self.v_buf("bfh"), self.v_buf("bfl")],
                             self.psb[bank_u], start=True, stop=False)
                    self.wcol_fm2("wup", fc * 128, 128, w1st, w2h, w2l, kc0=8)
                    self.mm3(self.ps_ap(bank_u),
                             [(self.v_ap(w2h).rearrange("p (k w) -> p k w", k=8)[:, kc, :],
                               self.v_ap(w2l).rearrange("p (k w) -> p k w", k=8)[:, kc, :],
                               bfh[:, 8 + kc, :], bfl[:, 8 + kc, :]) for kc in range(8)],
                             [self.v_buf(w2h), self.v_buf(w2l), self.v_buf("bfh"), self.v_buf("bfl")],
                             self.psb[bank_u], start=False, stop=True)
                    gax, bgax = self.v_ap("gax"), self.v_buf("gax")
                    self.sop(lambda e, gax=gax, bank_g=bank_g: e.activation(gax, self.ps_ap(bank_g), AF.Silu),
                             [self.psb[bank_g]], [bgax])
                    self.vop(lambda e, fi=fi, gax=gax, bank_u=bank_u, gu=gu:
                             e.tensor_tensor(gu[:, fi, :], self.ps_ap(bank_u), gax, ALU.mult),
                             [self.psb[bank_u], bgax], [bgu])
                self.stage([
                    ("bfh", [128, NCH * 512], F32R, BF_OFF),
                    ("bfl", [128, NCH * 512], F32R, BF_OFF + 32 * 1024),
                    ("gu", [128, 16 * 512], F32, GU_OFF),
                    ("acc", [128, 4 * 2048], F32, ACC_OFF),
                    ("dwst", [128, 1024], F32), ("dwh", [128, 1024], F32R), ("dwl", [128, 1024], F32R),
                    ("dwst2", [128, 1024], F32), ("dwh2", [128, 1024], F32R), ("dwl2", [128, 1024], F32R),
                    ("guh", [128, 512], F32R), ("gul", [128, 512], F32R),
                    ("guh2", [128, 512], F32R), ("gul2", [128, 512], F32R),
                    ("rt", [128, 512], F32), ("rt2", [128, 512], F32),
                ], keep=("gu", "bfh", "bfl") + (("acc",) if q > 0 else ()),
                    auto_base=ACC_OFF + 32 * 1024)
                gu = self.v_ap("gu").rearrange("p (f w) -> p f w", f=16)
                bgu = self.v_buf("gu")
                acc = self.v_ap("acc").rearrange("p (m w) -> p m w", m=4)
                bacc = self.v_buf("acc")
                for ofh in range(2):
                    o0 = ofh * 1024
                    for fi in range(16):
                        fc = f0 + fi
                        sf = "" if fi % 2 == 0 else "2"
                        dwhn, dwln, guhn, guln = "dwh" + sf, "dwl" + sf, "guh" + sf, "gul" + sf
                        dwh = self.v_ap(dwhn).rearrange("p (n w) -> p n w", n=2)
                        dwl = self.v_ap(dwln).rearrange("p (n w) -> p n w", n=2)
                        self.wtile("wdn", fc * 128, 128, o0, 1024, "dwst" + sf, dwhn, dwln)
                        self.split(self.v_ap(guhn), self.v_ap(guln), gu[:, fi, :],
                                   [bgu], self.v_buf(guhn), self.v_buf(guln))
                        for mt in range(4):
                            for nf in range(2):
                                bank = mt * 2 + nf
                                pairs = [(self.v_ap(guhn)[:, mt * 128:mt * 128 + 128],
                                          self.v_ap(guln)[:, mt * 128:mt * 128 + 128],
                                          dwh[:, nf, :], dwl[:, nf, :])]
                                self.mm3(self.ps_ap(bank), pairs,
                                         [self.v_buf(guhn), self.v_buf(guln),
                                          self.v_buf(dwhn), self.v_buf(dwln)],
                                         self.psb[bank], start=(fi == 0), stop=(fi == 15))
                    for mt in range(4):
                        for nf in range(2):
                            bank = mt * 2 + nf
                            of0 = o0 + nf * 512
                            if q == 0:
                                self.vop(lambda e, mt=mt, of0=of0, bank=bank, acc=acc:
                                         e.tensor_copy(acc[:, mt, of0:of0 + 512], self.ps_ap(bank)),
                                         [self.psb[bank]], [bacc])
                            elif q < 3:
                                self.vop(lambda e, mt=mt, of0=of0, bank=bank, acc=acc:
                                         e.tensor_tensor(acc[:, mt, of0:of0 + 512],
                                                         acc[:, mt, of0:of0 + 512],
                                                         self.ps_ap(bank), ALU.add),
                                         [self.psb[bank], bacc], [bacc])
                            else:
                                r0 = t0 + mt * 128
                                rt, brt = self.v_ap("rt"), self.v_buf("rt")
                                rt2, brt2 = self.v_ap("rt2"), self.v_buf("rt2")
                                self.load(rt, self.dram[resid_src].ap()[r0:r0 + 128, of0:of0 + 512],
                                          self.dbuf(resid_src), brt)
                                self.load(rt2, self.dram["XN"].ap()[1 + r0:1 + r0 + 128, of0:of0 + 512],
                                          self.dbuf("XN"), brt2, q="S")
                                def ev(e, bank=bank, rt=rt, rt2=rt2, mt=mt, of0=of0, acc=acc):
                                    e.tensor_tensor(rt, rt, rt2, ALU.add)
                                    e.drain()
                                    e.tensor_tensor(rt, rt, acc[:, mt, of0:of0 + 512], ALU.add)
                                    e.drain()
                                    return e.tensor_tensor(rt, rt, self.ps_ap(bank), ALU.add)
                                self.vop(ev, [self.psb[bank], brt, brt2, bacc], [brt])
                                self.store(self.dram[resid_dst].ap()[r0:r0 + 128, of0:of0 + 512],
                                           rt, brt, self.dbuf(resid_dst))

    def wcol_fm2(self, wname, j0, w, vst, vhi, vlo, kc0, scale_col=None):
        W = self.dram[wname]
        st_ap = self.v_ap(vst).rearrange("p (k w) -> p k w", k=8)
        src = W.ap().rearrange("(k p) o -> p k o", p=128)[:, kc0:kc0 + 8, j0:j0 + w]
        self.load(st_ap, src, self.dbuf(wname), self.v_buf(vst))
        if scale_col is not None:
            sap, sbuf = scale_col
            def sc(e):
                last = None
                for kc in range(8):
                    last = e.tensor_scalar(st_ap[:, kc, :], st_ap[:, kc, :],
                                           sap[:, kc:kc + 1], None, ALU.mult)
                return last
            self.vop(sc, [self.v_buf(vst), sbuf], [self.v_buf(vst)])
        self.split(self.v_ap(vhi), self.v_ap(vlo), self.v_ap(vst),
                   [self.v_buf(vst)], self.v_buf(vhi), self.v_buf(vlo))

    def wtile_b(self, wname, fc, vst, vhi, vlo):
        W = self.dram[wname]
        st = self.v_ap(vst)
        self.load(st[0:16, :], W.ap()[0:16, fc * 128:fc * 128 + 128], self.dbuf(wname),
                  self.v_buf(vst))
        self.vop(lambda e, st=st: e.tensor_scalar(st[0:16, :], st[0:16, :], 0.1, None, ALU.mult),
                 [self.v_buf(vst)], [self.v_buf(vst)])
        self.split(self.v_ap(vhi)[0:16, :], self.v_ap(vlo)[0:16, :], st[0:16, :],
                   [self.v_buf(vst)], self.v_buf(vhi), self.v_buf(vlo))

    def stage_ffn(self, resid_src, resid_dst):
        self.stage_rms_base()
        self.stage_ffn_attn()
        self.stage_modz()
        self.stage_mlp(resid_src, resid_dst)

    def stage_rwkv(self, resid_src="R", resid_dst="R"):
        self.sc.t_drain = True
        T = self.T
        NCHK = T // 128
        dbg = "DBG1" in self.debug_taps
        if dbg:
            for i in (1, 2, 3, 4):
                self.dram_t(f"DBG{i}", [T, D])
        self.k_S = getattr(self, "k_S", None) or self.sbuf_t("k_S", [128, 1024], F32)
        self.k_Sh = getattr(self, "k_Sh", None) or self.sbuf_t("k_Sh", [128, 1024], F32R)
        self.k_Sl = getattr(self, "k_Sl", None) or self.sbuf_t("k_Sl", [128, 1024], F32R)
        bS, bSh, bSl = self.sbbuf("k_S"), self.sbbuf("k_Sh"), self.sbbuf("k_Sl")
        self.vop(lambda e: e.memset(self.k_S.ap(), 0.0), [], [bS])

        for g in range(T // 256):
            t0 = g * 256
            self.stage([
                ("xf", [128, NCH * 257], F32),
                ("xx", [128, NCH * 256], F32),
                ("xs", [128, NCH * 256], F32),
                ("xsh", [128, NCH * 256], F32R), ("xsl", [128, NCH * 256], F32R),
                ("mc", [128, 96], F32),
                ("w1st", [128, 16 * 160], F32), ("w1h", [128, 16 * 160], F32R), ("w1l", [128, 16 * 160], F32R),
                ("aa", [128, 512], F32), ("aah", [128, 512], F32R), ("aal", [128, 512], F32R),
                ("wb2st", [128, 2048], F32), ("wb2h", [128, 2048], F32R), ("wb2l", [128, 2048], F32R),
                ("vwst", [128, 4 * 512], F32), ("vwh", [128, 4 * 512], F32R), ("vwl", [128, 4 * 512], F32R),
                ("d1", [64, 256], F32), ("d1h", [64, 256], F32R), ("d1l", [64, 256], F32R),
                                ("dkbs", [128, 512], F32),
                ("t1", [128, 512], F32),
            ])
            bXF = self.dbuf("XNF_F")
            xf = self.v_ap("xf").rearrange("p (c w) -> p c w", c=NCH)
            self.load(xf, self.dram["XNF_F"].ap().rearrange("c p t -> p c t")[:, :, t0:t0 + 257],
                      bXF, self.v_buf("xf"))
            self.load(self.v_ap("mc"), self.dram["maa_cols"].ap(), self.dbuf("maa_cols"),
                      self.v_buf("mc"))
            xn = xf[:, :, 1:257]
            xsh_v = xf[:, :, 0:256]
            xx = self.v_ap("xx").rearrange("p (c w) -> p c w", c=NCH)
            self.vop(lambda e, xx=xx, xn=xn, xsh_v=xsh_v: e.tensor_tensor(xx, xsh_v, xn, ALU.subtract),
                     [self.v_buf("xf")], [self.v_buf("xx")])
            xs = self.v_ap("xs").rearrange("p (c w) -> p c w", c=NCH)
            mc = self.v_ap("mc")
            def lerp1(e, mcol, xs=xs, xx=xx, mc=mc):
                last = None
                for c in range(NCH):
                    last = e.tensor_scalar(xs[:, c, :], xx[:, c, :],
                                           mc[:, mcol * 16 + c:mcol * 16 + c + 1], None, ALU.mult)
                return last
            self.vop(lambda e: lerp1(e, 0), [self.v_buf("xx"), self.v_buf("mc")], [self.v_buf("xs")])
            self.vop(lambda e, xs=xs, xn=xn: e.tensor_tensor(xs, xs, xn, ALU.add),
                     [self.v_buf("xf"), self.v_buf("xs")], [self.v_buf("xs")])
            self.split(self.v_ap("xsh"), self.v_ap("xsl"), self.v_ap("xs"),
                       [self.v_buf("xs")], self.v_buf("xsh"), self.v_buf("xsl"))
            self.wcol_fm("maa_w1", 0, 160, "w1st", "w1h", "w1l", kcn=NCH)
            w1h = self.v_ap("w1h").rearrange("p (k w) -> p k w", k=NCH)
            w1l = self.v_ap("w1l").rearrange("p (k w) -> p k w", k=NCH)
            xshr = self.v_ap("xsh").rearrange("p (c w) -> p c w", c=NCH)
            xslr = self.v_ap("xsl").rearrange("p (c w) -> p c w", c=NCH)
            pairsA = [(w1h[:, kc, 0:128], w1l[:, kc, 0:128], xshr[:, kc, :], xslr[:, kc, :])
                      for kc in range(NCH)]
            pairsB = [(w1h[:, kc, 128:160], w1l[:, kc, 128:160], xshr[:, kc, :], xslr[:, kc, :])
                      for kc in range(NCH)]
            wb = [self.v_buf("w1h"), self.v_buf("w1l"), self.v_buf("xsh"), self.v_buf("xsl")]
            self.mm3(self.ps_ap(0, 256), pairsA, wb, self.psb[0])
            self.mm3(self.ps_ap_p(1, 0, 32, 256), pairsB, wb, self.psb[1])
            aa, baa = self.v_ap("aa"), self.v_buf("aa")
            self.sop(lambda e, aa=aa: e.activation(aa[:, 0:256], self.ps_ap(0, 256), AF.Tanh),
                     [self.psb[0]], [baa])
            self.sop(lambda e, aa=aa: e.activation(aa[0:32, 256:512], self.ps_ap_p(1, 0, 32, 256), AF.Tanh),
                     [self.psb[1]], [baa])
            self.split(self.v_ap("aah"), self.v_ap("aal"), aa, [baa],
                       self.v_buf("aah"), self.v_buf("aal"))
            XSTAR = ("xs",)
            for f in range(5):
                pb = f * 32 if f < 4 else 0
                tpf = (96, 0) if f == 3 else None
                w2s = self.v_ap("wb2st")[pb:pb + 32, :]
                self.load(w2s, self.dram["maa_w2"].ap()[f, :, :], self.dbuf("maa_w2"),
                          self.v_buf("wb2st"))
                self.split(self.v_ap("wb2h")[pb:pb + 32, :], self.v_ap("wb2l")[pb:pb + 32, :],
                           w2s, [self.v_buf("wb2st")],
                           self.v_buf("wb2h"), self.v_buf("wb2l"))
                if f < 4:
                    ash = self.v_ap("aah")[f * 32:f * 32 + 32, 0:256]
                    asl = self.v_ap("aal")[f * 32:f * 32 + 32, 0:256]
                else:
                    ash = self.v_ap("aah")[0:32, 256:512]
                    asl = self.v_ap("aal")[0:32, 256:512]
                for mcch in range(NCH):
                    bank = 2 + (mcch % 2)
                    pairs = [(self.v_ap("wb2h")[pb:pb + 32, mcch * 128:mcch * 128 + 128],
                              self.v_ap("wb2l")[pb:pb + 32, mcch * 128:mcch * 128 + 128], ash, asl)]
                    self.mm3(self.ps_ap(bank, 256), pairs,
                             [self.v_buf("wb2h"), self.v_buf("wb2l"), baa,
                              self.v_buf("aah"), self.v_buf("aal")],
                             self.psb[bank], tp=tpf)
                    t1, b1 = self.v_ap("t1"), self.v_buf("t1")
                    def fuse(e, bank=bank, f=f, mcch=mcch, t1=t1, xs=xs, xx=xx, xn=xn, mc=mc):
                        e.tensor_scalar(t1[:, 0:256], self.ps_ap(bank, 256),
                                        mc[:, (f + 1) * 16 + mcch:(f + 1) * 16 + mcch + 1],
                                        None, ALU.add)
                        e.drain()
                        e.tensor_tensor(t1[:, 0:256], xx[:, mcch, :], t1[:, 0:256], ALU.mult)
                        e.drain()
                        return e.tensor_tensor(xs[:, mcch, :], xn[:, mcch, :], t1[:, 0:256], ALU.add)
                    self.vop(fuse, [self.psb[bank], self.v_buf("mc"), self.v_buf("xx"),
                                    self.v_buf("xf")], [b1, self.v_buf("xs")])
                self.split(self.v_ap("xsh"), self.v_ap("xsl"), self.v_ap("xs"),
                           [self.v_buf("xs")], self.v_buf("xsh"), self.v_buf("xsl"))
                if f == 0:
                    self.wcol_fm("dec_w1", 0, 64, "w1st", "w1h", "w1l", kcn=NCH)
                    w1h2 = self.v_ap("w1h").rearrange("p (k w) -> p k w", k=NCH)
                    w1l2 = self.v_ap("w1l").rearrange("p (k w) -> p k w", k=NCH)
                    pairs = [(w1h2[:, kc, 0:64], w1l2[:, kc, 0:64], xshr[:, kc, :], xslr[:, kc, :])
                             for kc in range(NCH)]
                    self.mm3(self.ps_ap_p(4, 0, 64, 256), pairs, wb, self.psb[4])
                    d1, bd1 = self.v_ap("d1"), self.v_buf("d1")
                    self.sop(lambda e, d1=d1: e.activation(d1, self.ps_ap_p(4, 0, 64, 256), AF.Tanh),
                             [self.psb[4]], [bd1])
                    self.split(self.v_ap("d1h"), self.v_ap("d1l"), d1, [bd1],
                               self.v_buf("d1h"), self.v_buf("d1l"))
                    self.load(self.v_ap("wb2st")[0:64, :], self.dram["dec_w2"].ap(), self.dbuf("dec_w2"),
                              self.v_buf("wb2st"))
                    self.split(self.v_ap("wb2h")[0:64, :], self.v_ap("wb2l")[0:64, :],
                               self.v_ap("wb2st")[0:64, :],
                               [self.v_buf("wb2st")], self.v_buf("wb2h"), self.v_buf("wb2l"))
                    for mt in range(2):
                        for nf in range(4):
                            bank = 4 + (mt * 4 + nf) % 2
                            pairs = [(self.v_ap("d1h")[:, mt * 128:mt * 128 + 128],
                                      self.v_ap("d1l")[:, mt * 128:mt * 128 + 128],
                                      self.v_ap("wb2h")[0:64, nf * 512:nf * 512 + 512],
                                      self.v_ap("wb2l")[0:64, nf * 512:nf * 512 + 512])]
                            self.mm3(self.ps_ap(bank), pairs,
                                     [self.v_buf("d1h"), self.v_buf("d1l"),
                                      self.v_buf("wb2h"), self.v_buf("wb2l")],
                                     self.psb[bank])
                            t1, b1 = self.v_ap("t1"), self.v_buf("t1")
                            dkbs, bdkbs = self.v_ap("dkbs"), self.v_buf("dkbs")
                            self.load(dkbs, self.dram["decay_row"].ap()
                                      [0:1, nf * 512:nf * 512 + 512].broadcast_to([128, 512]),
                                      self.dbuf("decay_row"), bdkbs)
                            self.vop(lambda e, bank=bank, t1=t1, dkbs=dkbs:
                                     e.tensor_tensor(t1, self.ps_ap(bank), dkbs, ALU.add),
                                     [self.psb[bank], bdkbs], [b1])
                            r0 = t0 + mt * 128
                            self.store(self.dram["RW_W"].ap()[r0:r0 + 128, nf * 512:nf * 512 + 512],
                                       t1, b1, self.dbuf("RW_W"))
                elif f in (1, 2):
                    wname = "wk_r" if f == 1 else "wv_r"
                    dstn = "RW_K" if f == 1 else "RW_V"
                    for of in range(4):
                        for kh_ in range(4):
                            self.wtile8(wname, kh_ * 4, of * 512, "vwst", "vwh", "vwl")
                            vwh = self.v_ap("vwh").rearrange("p (k w) -> p k w", k=4)
                            vwl = self.v_ap("vwl").rearrange("p (k w) -> p k w", k=4)
                            for mt in range(2):
                                msl = slice(mt * 128, mt * 128 + 128)
                                pairs = [(xshr[:, kh_ * 4 + kc, msl], xslr[:, kh_ * 4 + kc, msl],
                                          vwh[:, kc, :], vwl[:, kc, :]) for kc in range(4)]
                                self.mm3(self.ps_ap(4 + mt), pairs,
                                         [self.v_buf("vwh"), self.v_buf("vwl"),
                                          self.v_buf("xsh"), self.v_buf("xsl")],
                                         self.psb[4 + mt], start=(kh_ == 0), stop=(kh_ == 3))
                        for mt in range(2):
                            t1, b1 = self.v_ap("t1"), self.v_buf("t1")
                            self.vop(lambda e, mt=mt, t1=t1: e.tensor_copy(t1, self.ps_ap(4 + mt)),
                                     [self.psb[4 + mt]], [b1])
                            r0 = t0 + mt * 128
                            self.store(self.dram[dstn].ap()[r0:r0 + 128, of * 512:of * 512 + 512],
                                       t1, b1, self.dbuf(dstn))
                else:
                    wname = "wr_r" if f == 3 else "wg_r"
                    dstn = "RW_R" if f == 3 else "RW_G"
                    for h in range(NCH):
                        self.wcol_fm(wname, h * 128, 128, "w1st", "w1h", "w1l", kcn=NCH)
                        w1h3 = self.v_ap("w1h").rearrange("p (k w) -> p k w", k=NCH)
                        w1l3 = self.v_ap("w1l").rearrange("p (k w) -> p k w", k=NCH)
                        pairs = [(w1h3[:, kc, 0:128], w1l3[:, kc, 0:128],
                                  xshr[:, kc, :], xslr[:, kc, :]) for kc in range(NCH)]
                        bank = 4 + h % 2
                        self.mm3(self.ps_ap(bank, 256), pairs, wb, self.psb[bank])
                        t1, b1 = self.v_ap("t1"), self.v_buf("t1")
                        if f == 4:
                            self.sop(lambda e, bank=bank, t1=t1: e.activation(t1[:, 0:256],
                                                                              self.ps_ap(bank, 256), AF.Silu),
                                     [self.psb[bank]], [b1])
                        else:
                            self.vop(lambda e, bank=bank, t1=t1: e.tensor_copy(t1[:, 0:256],
                                                                               self.ps_ap(bank, 256)),
                                     [self.psb[bank]], [b1])
                        self.store(self.dram[dstn].ap()[h, :, t0:t0 + 256], t1[:, 0:256],
                                   b1, self.dbuf(dstn))

        H64 = 32
        for cc in range(NCHK):
            c0 = cc * 128
            self.stage(
                [(f"F{i}", [128, 2048], F32) for i in range(10)] +
                [(f"R{i}", [128, 2048], F32R) for i in range(8)] +
                [("sm", [128, 128], F32), ("smh", [128, 128], F32R), ("sml", [128, 128], F32R),
                 ("dv", [128, 32], F32), ("gst", [128, 128], F32),
                 ("pcol", [128, 16], F32), ("pcol2", [128, 16], F32),
                 ("ucol", [128, 16], F32), ("lng", [128, 32], F32)])
            A = self.v_ap
            B = self.v_buf
            self.load(A("F0").rearrange("p (c w) -> p c w", c=NCH),
                      self.dram["RW_R"].ap().rearrange("c p t -> p c t")[:, :, c0:c0 + 128],
                      self.dbuf("RW_R"), B("F0"))
            self.load(A("F1"), self.dram["RW_K"].ap()[c0:c0 + 128, :], self.dbuf("RW_K"), B("F1"))
            self.load(A("F2"), self.dram["RW_V"].ap()[c0:c0 + 128, :], self.dbuf("RW_V"), B("F2"), q="S")
            self.load(A("F3"), self.dram["RW_W"].ap()[c0:c0 + 128, :], self.dbuf("RW_W"), B("F3"), q="S")
            self.load(A("ucol"), self.dram["u_col"].ap(), self.dbuf("u_col"), B("ucol"))
            self.load(A("lng"), self.dram["lnr_cols"].ap(), self.dbuf("lnr_cols"), B("lng"))
            self.sop(lambda e, a=A("F3"): e.activation(a, a, AF.Exp), [B("F3")], [B("F3")])
            self.split(A("R0"), A("R1"), A("F3"), [B("F3")], B("R0"), B("R1"))
            for nf in range(4):
                pairs = [(self.k_tri.ap(), A("R0")[:, nf * 512:nf * 512 + 512]),
                         (self.k_tri.ap(), A("R1")[:, nf * 512:nf * 512 + 512])]
                self.mm1(self.ps_ap(nf), pairs,
                         [self.sbbuf("k_tri"), B("R0"), B("R1")], self.psb[nf])
            for nf in range(4):
                pairs = [(self.k_ones.ap(), A("R0")[:, nf * 512:nf * 512 + 512]),
                         (self.k_ones.ap(), A("R1")[:, nf * 512:nf * 512 + 512])]
                self.mm1(self.ps_ap(4 + nf), pairs,
                         [self.sbbuf("k_ones"), B("R0"), B("R1")], self.psb[4 + nf])
            def totcp(e, t=A("F4")):
                last = None
                for nf in range(4):
                    last = e.tensor_copy(t[:, nf * 512:nf * 512 + 512], self.ps_ap(4 + nf))
                return last
            self.vop(totcp, [self.psb[4], self.psb[5], self.psb[6], self.psb[7]], [B("F4")])
            self.sop(lambda e, a=A("F5"), t=A("F4"): e.activation(a, t, AF.Exp, scale=-1.0),
                     [B("F4")], [B("F5")])
            self.store(self.dram["ROWTMP"].ap()[1:2, :], A("F5")[0:1, :], B("F5"), self.dbuf("ROWTMP"))
            self.load(A("pcol"), self.dram["ROWTMP"].ap()[1:2, :].rearrange("o (c p) -> (o p) c", p=128),
                      self.dbuf("ROWTMP"), B("pcol"), slow=True)
            self.vop(lambda e, t=A("F5"), s=A("F4"): e.tensor_scalar(t, s, 0.5, None, ALU.mult),
                     [B("F4")], [B("F5")])
            def aarg(e, a=A("F6"), toth=A("F5")):
                last = None
                for nf in range(4):
                    last = e.tensor_tensor(a[:, nf * 512:nf * 512 + 512],
                                           toth[:, nf * 512:nf * 512 + 512],
                                           self.ps_ap(nf), ALU.subtract)
                return last
            self.vop(aarg, [self.psb[0], self.psb[1], self.psb[2], self.psb[3], B("F5")], [B("F6")])
            self.sop(lambda e, a=A("F6"): e.activation(a, a, AF.Exp), [B("F6")], [B("F6")])
            def iarg(e, a=A("F7"), wc=A("F3"), toth=A("F5")):
                for nf in range(4):
                    sl = slice(nf * 512, nf * 512 + 512)
                    e.tensor_tensor(a[:, sl], self.ps_ap(nf), wc[:, sl], ALU.add)
                e.drain()
                return e.tensor_tensor(a, a, toth, ALU.subtract)
            self.vop(iarg, [self.psb[0], self.psb[1], self.psb[2], self.psb[3], B("F3"), B("F5")], [B("F7")])
            self.sop(lambda e, a=A("F7"): e.activation(a, a, AF.Exp), [B("F7")], [B("F7")])
            if dbg:
                self.store(self.dram["DBG1"].ap()[c0:c0 + 128, :], A("F6"), B("F6"), self.dbuf("DBG1"))
                self.store(self.dram["DBG2"].ap()[c0:c0 + 128, :], A("F7"), B("F7"), self.dbuf("DBG2"))
            self.sop(lambda e, a=A("F9"), t=A("F5"): e.activation(a, t, AF.Exp, scale=-1.0),
                     [B("F5")], [B("F9")])
            self.store(self.dram["ROWTMP"].ap()[2:3, :], A("F9")[0:1, :], B("F9"), self.dbuf("ROWTMP"))
            self.load(A("pcol2"), self.dram["ROWTMP"].ap()[2:3, :].rearrange("o (c p) -> (o p) c", p=128),
                      self.dbuf("ROWTMP"), B("pcol2"), slow=True)
            self.vop(lambda e, o=A("F8"), k=A("F1"), iv=A("F7"): e.tensor_tensor(o, k, iv, ALU.mult),
                     [B("F1"), B("F7")], [B("F8")])
            self.vop(lambda e, o=A("F9"), kh=A("F8"): e.tensor_tensor(o, kh, o, ALU.mult),
                     [B("F8"), B("F9")], [B("F9")])
            for (srcv, dstv) in [("F6", "F7"), ("F8", "F8"), ("F1", "F3")]:
                sap, dap = A(srcv), A(dstv)
                sb_, db_ = B(srcv), B(dstv)
                for grp in range(4):
                    bank = 4 + grp % 2
                    srcs = [sap[:, (grp * 4 + i) * 128:(grp * 4 + i) * 128 + 128] for i in range(4)]
                    self.transpose_group(bank, srcs, [sb_])
                    self.vop(lambda e, dap=dap, grp=grp, bank=bank:
                             e.tensor_copy(dap[:, grp * 512:grp * 512 + 512], self.ps_ap(bank)),
                             [self.psb[bank]], [db_])
            def mkrku(e, o=A("F1"), r=A("F0"), kfm=A("F3"), ucol=A("ucol")):
                e.tensor_tensor(o, r, kfm, ALU.mult)
                e.drain()
                last = None
                for c in range(NCH):
                    last = e.tensor_scalar(o[:, c * 128:c * 128 + 128], o[:, c * 128:c * 128 + 128],
                                           ucol[:, c:c + 1], None, ALU.mult)
                return last
            self.vop(mkrku, [B("F0"), B("F3"), B("ucol")], [B("F1")])
            self.split(A("R2"), A("R3"), A("F1"), [B("F1")], B("R2"), B("R3"))
            for h in range(H64):
                par = (h % 2) * 64
                hp = h // 2
                pairs = [(A("R2")[par:par + 64, hp * 128:hp * 128 + 128],
                          self.k_ones.ap()[par:par + 64, 0:2]),
                         (A("R3")[par:par + 64, hp * 128:hp * 128 + 128],
                          self.k_ones.ap()[par:par + 64, 0:2])]
                self.mm1(self.ps_ap(5, 2, 2 * h), pairs,
                         [B("R2"), B("R3"), self.sbbuf("k_ones")], self.psb[5])
            self.vop(lambda e, dv=A("dv"): e.tensor_copy(
                dv, self.ps_ap(5, 64, 0).rearrange("p (c two) -> p c two", two=2)[:, :, 0]),
                [self.psb[5]], [B("dv")])
            self.split(A("R0"), A("R1"), A("F8"), [B("F8")], B("R0"), B("R1"))
            self.split(A("R2"), A("R3"), A("F9"), [B("F9")], B("R2"), B("R3"))
            self.split(A("R4"), A("R5"), A("F2"), [B("F2")], B("R4"), B("R5"))
            self.vop(lambda e, r=A("F0"), afm=A("F7"): e.tensor_tensor(r, r, afm, ALU.mult),
                     [B("F7"), B("F0")], [B("F0")])
            self.split(A("R6"), A("R7"), A("F0"), [B("F0")], B("R6"), B("R7"))
            p2b = A("pcol2")[:, :, None].broadcast_to([128, 16, 64])
            self.vop(lambda e, o=A("F5"), p2b=p2b: e.tensor_tensor(
                o[:, 0:1024].rearrange("p (m i) -> p m i", m=16),
                self.k_S.ap().rearrange("p (m i) -> p m i", m=16), p2b, ALU.mult),
                [bS, B("pcol2")], [B("F5")])
            self.split(self.k_Sh.ap(), self.k_Sl.ap(), A("F5")[:, 0:1024], [B("F5")], bSh, bSl)
            for h in range(H64):
                par = (h % 2) * 64
                hp = h // 2
                sb_sc = 6 + h % 2
                pairs = [(A("R0")[par:par + 64, hp * 128:hp * 128 + 128],
                          A("R1")[par:par + 64, hp * 128:hp * 128 + 128],
                          A("R6")[par:par + 64, hp * 128:hp * 128 + 128],
                          A("R7")[par:par + 64, hp * 128:hp * 128 + 128])]
                self.mm3(self.ps_ap(sb_sc, 128), pairs,
                         [B("R0"), B("R1"), B("R6"), B("R7")], self.psb[sb_sc])
                self.vop(lambda e, sm=A("sm"), sb_sc=sb_sc: e.select(
                    sm, self.k_strictu.ap(), self.ps_ap(sb_sc, 128), self.k_zero.ap()),
                    [self.psb[sb_sc], self.sbbuf("k_strictu"), self.sbbuf("k_zero")], [B("sm")])
                self.split(A("smh"), A("sml"), A("sm"), [B("sm")], B("smh"), B("sml"))
                ybank = h // 8
                ycol = (h % 8) * 64
                pairs = [(A("smh"), A("sml"),
                          A("R4")[:, h * 64:h * 64 + 64], A("R5")[:, h * 64:h * 64 + 64])]
                self.mm3(self.ps_ap(ybank, 64, ycol), pairs,
                         [B("smh"), B("sml"), B("R4"), B("R5")],
                         self.psb[ybank], start=True, stop=False)
                pairs = [(A("R6")[par:par + 64, hp * 128:hp * 128 + 128],
                          A("R7")[par:par + 64, hp * 128:hp * 128 + 128],
                          self.k_Sh.ap()[par:par + 64, hp * 64:hp * 64 + 64],
                          self.k_Sl.ap()[par:par + 64, hp * 64:hp * 64 + 64])]
                self.mm3(self.ps_ap(ybank, 64, ycol), pairs,
                         [B("R6"), B("R7"), bSh, bSl],
                         self.psb[ybank], start=False, stop=True)
            for rnd in range(2):
                for mp in range(rnd * 8, rnd * 8 + 8):
                    k = mp % 8
                    bank = 4 + k // 4
                    dcol = (k % 4) * 128
                    pairs = [(A("R2")[:, mp * 128:mp * 128 + 128],
                              A("R3")[:, mp * 128:mp * 128 + 128],
                              A("R4")[:, mp * 128:mp * 128 + 128],
                              A("R5")[:, mp * 128:mp * 128 + 128])]
                    self.mm3(self.ps_ap(bank, 128, dcol), pairs,
                             [B("R2"), B("R3"), B("R4"), B("R5")], self.psb[bank])
                scol = rnd * 512
                Sap = self.k_S.ap()[:, scol:scol + 512]
                pcol_b = (A("pcol")[:, rnd * 8:rnd * 8 + 8, None].broadcast_to([128, 8, 64]))
                def supd(e, Sap=Sap, pcol_b=pcol_b, rnd=rnd):
                    e.tensor_tensor(Sap.rearrange("p (m i) -> p m i", m=8),
                                    Sap.rearrange("p (m i) -> p m i", m=8), pcol_b, ALU.mult)
                    e.drain()
                    ps45 = self.psum_h[:, 4 * 512: 6 * 512]
                    even = ps45[0:64, :].rearrange("p (m i) -> p m i", i=128)[:, :, 0:64]
                    odd = ps45[64:128, :].rearrange("p (m i) -> p m i", i=128)[:, :, 64:128]
                    e.tensor_tensor(Sap[0:64, :].rearrange("p (m i) -> p m i", m=8),
                                    Sap[0:64, :].rearrange("p (m i) -> p m i", m=8), even, ALU.add)
                    e.drain()
                    return e.tensor_tensor(Sap[64:128, :].rearrange("p (m i) -> p m i", m=8),
                                           Sap[64:128, :].rearrange("p (m i) -> p m i", m=8), odd, ALU.add)
                self.vop(supd, [self.psb[4], self.psb[5], B("pcol"), bS], [bS])
            dvb = A("dv")[:, :, None].broadcast_to([128, 32, 64])
            self.vop(lambda e, yt=A("F3"), dvb=dvb, vc_=A("F2"):
                     e.tensor_tensor(yt.rearrange("p (c i) -> p c i", c=32),
                                     vc_.rearrange("p (c i) -> p c i", c=32), dvb, ALU.mult),
                     [B("dv"), B("F2")], [B("F3")])
            def yev(e, yf=A("F6"), yt=A("F3")):
                last = None
                for b in range(4):
                    last = e.tensor_tensor(yf[:, b * 512:b * 512 + 512], self.ps_ap(b),
                                           yt[:, b * 512:b * 512 + 512], ALU.add)
                return last
            self.vop(yev, [B("F3"), self.psb[0], self.psb[1], self.psb[2], self.psb[3]], [B("F6")])
            if dbg:
                self.store(self.dram["DBG3"].ap()[c0:c0 + 128, :], A("F6"), B("F6"), self.dbuf("DBG3"))
            yf, byf = A("F6"), B("F6")
            yt, byt = A("F3"), B("F3")
            gst, bgst = A("gst"), B("gst")
            yf3 = yf.rearrange("p (c i) -> p c i", c=32)
            self.sop(lambda e, yt=yt, yf=yf: e.activation(yt, yf, AF.Square), [byf], [byt])
            yt3 = yt.rearrange("p (c i) -> p c i", c=32)
            def gred(e, gst=gst, yf3=yf3, yt3=yt3):
                e.tensor_reduce(gst[:, 0:32], yf3, AX.X, ALU.add)
                e.tensor_reduce(gst[:, 32:64], yt3, AX.X, ALU.add)
                e.drain()
                e.tensor_scalar(gst[:, 0:32], gst[:, 0:32], 1.0 / 64, None, ALU.mult)
                e.tensor_scalar(gst[:, 32:64], gst[:, 32:64], 1.0 / 64, None, ALU.mult)
                e.drain()
                e.tensor_tensor(gst[:, 64:96], gst[:, 0:32], gst[:, 0:32], ALU.mult)
                e.drain()
                return e.tensor_tensor(gst[:, 64:96], gst[:, 32:64], gst[:, 64:96], ALU.subtract)
            self.vop(gred, [byf, byt], [bgst])
            def grs(e, gst=gst):
                e.activation(gst[:, 96:128], gst[:, 64:96], AF.Ln, bias=float(EPS_GN))
                e.drain()
                return e.activation(gst[:, 96:128], gst[:, 96:128], AF.Exp, scale=-0.5)
            self.sop(grs, [bgst], [bgst])
            def ynorm(e, yf3=yf3, gst=gst):
                e.tensor_tensor(yf3, yf3,
                                gst[:, 0:32][:, :, None].broadcast_to([128, 32, 64]),
                                ALU.subtract)
                e.drain()
                return e.tensor_tensor(yf3, yf3,
                                       gst[:, 96:128][:, :, None].broadcast_to([128, 32, 64]),
                                       ALU.mult)
            self.vop(ynorm, [bgst, byf], [byf])
            self.load(A("F4").rearrange("p (c w) -> p c w", c=NCH),
                      self.dram["RW_G"].ap().rearrange("c p t -> p c t")[:, :, c0:c0 + 128],
                      self.dbuf("RW_G"), B("F4"))
            lng = A("lng")
            for c in range(NCH):
                bank = 4 + c % 2
                self.transpose_group(bank, [yf[:, c * 128:c * 128 + 128]], [byf])
                def ynev(e, bank=bank, c=c, t1=A("sm"), lng=lng, gc_=A("F4")):
                    e.tensor_scalar(t1, self.ps_ap(bank, 128), lng[:, c:c + 1],
                                    lng[:, 16 + c:16 + c + 1], ALU.mult, ALU.add)
                    e.drain()
                    return e.tensor_tensor(t1, t1, gc_[:, c * 128:c * 128 + 128], ALU.mult)
                self.vop(ynev, [self.psb[bank], B("lng"), B("F4")], [B("sm")])
                self.split(A("smh"), A("sml"), A("sm"), [B("sm")], B("smh"), B("sml"))
                self.store(self.dram["AOF_H"].ap()[c, :, c0:c0 + 128], A("smh"),
                           B("smh"), self.dbuf("AOF_H"))
                self.store(self.dram["AOF_L"].ap()[c, :, c0:c0 + 128], A("sml"),
                           B("sml"), self.dbuf("AOF_L"), q="S")
        self.sc.t_drain = False
        self.pass_out_proj("AOF_H", "AOF_L", "wo_r", resid_src, resid_dst)


def build_full(T_core=2048, npass=NPASS, debug_taps=()):
    m = MK(T_core=T_core, npass=npass, debug_taps=debug_taps)
    m._init_psum()
    m._init_arena()
    m._init_fconsts()
    m._decl_inputs()
    m._decl_scratch()
    m._init_consts()
    m.stage_zero_pads()
    m.stage_woa()
    m.stage_ln("x")
    m.stage_attn_a(resid_src="x", resid_dst="R")
    m.stage_ln("R")
    m.stage_ffn(resid_src="R", resid_dst="R")
    m.stage_ln("R", write_f32=True)
    m.stage_rwkv(resid_src="R", resid_dst="R")
    m.stage_ln("R")
    m.stage_ffn(resid_src="R", resid_dst="out")
    nc = m.finalize()
    return nc, m


def make_inmap(m, x_seq, w):
    perm = np.concatenate([np.arange(64, 128), np.arange(64)])
    permD = (np.arange(D).reshape(NCH, 128)[:, perm]).reshape(-1)
    def fmcol(v):
        return np.ascontiguousarray(v.reshape(NCH, 128).T)
    T_ = lambda a: np.ascontiguousarray(np.asarray(a).T)
    C_ = lambda a: np.ascontiguousarray(np.asarray(a))
    maa = np.concatenate([fmcol(np.asarray(w[f"time_maa_{k}"]).reshape(-1))
                          for k in ("x", "w", "k", "v", "r", "g")], 1)
    return {
        "x": np.ascontiguousarray(x_seq),
        "wq_a": T_(w["a_Wq"]), "wqs_a": T_(np.asarray(w["a_Wq"])[permD]),
        "wk_a": T_(w["a_Wk"]), "wks_a": T_(np.asarray(w["a_Wk"])[permD]),
        "wv_a": T_(w["a_Wv"]), "wo_a": T_(w["a_Wo"]),
        "wq_f": T_(w["f_Wq"]), "wk_f": T_(w["f_Wk"]), "wv_f": T_(w["f_Wv"]),
        "wo_f_nat": C_(w["f_Wo"]), "a_f": C_(w["f_A"]), "b_f": C_(w["f_B"]),
        "win": T_(w["f_Win"]), "wup": T_(w["f_Wup"]), "wdn": T_(w["f_Wdn"]),
        "ln1_row": np.asarray(w["f_ln1"]).reshape(1, D).astype(np.float32),
        "ln2_row": np.asarray(w["f_ln2"]).reshape(1, D).astype(np.float32),
        "wr_r": T_(w["r_Wr"]), "wk_r": T_(w["r_Wk"]), "wv_r": T_(w["r_Wv"]),
        "wg_r": T_(w["r_Wg"]), "wo_r": T_(w["r_Wo"]),
        "maa_w1": C_(w["time_maa_w1"]), "maa_w2": C_(w["time_maa_w2"]),
        "dec_w1": C_(w["time_decay_w1"]), "dec_w2": C_(w["time_decay_w2"]),
        "maa_cols": maa,
        "decay_row": np.asarray(w["time_decay"]).reshape(1, D).astype(np.float32),
        "u_col": fmcol(np.asarray(w["time_faaaa"]).reshape(-1)),
        "lnr_cols": np.concatenate([fmcol(np.asarray(w["r_ln_gamma"]).reshape(-1)),
                                    fmcol(np.asarray(w["r_ln_beta"]).reshape(-1))], 1),
    }

_CACHE = {}

def kernel(**inputs):
    if "nc" not in _CACHE:
        _CACHE["nc"], _CACHE["m"] = build_full(T_core=2048)
    nc, m = _CACHE["nc"], _CACHE["m"]
    w = {
        "a_Wq": inputs["L0_attn_a_Wq"], "a_Wk": inputs["L0_attn_a_Wk"],
        "a_Wv": inputs["L0_attn_a_Wv"], "a_Wo": inputs["L0_attn_a_Wo"],
        "f_Wq": inputs["L0_ffn_Wq"], "f_Wk": inputs["L0_ffn_Wk"], "f_Wv": inputs["L0_ffn_Wv"],
        "f_Wo": inputs["L0_ffn_Wo"], "f_A": inputs["L0_ffn_A"], "f_B": inputs["L0_ffn_B"],
        "f_Win": inputs["L0_ffn_W_in_base"], "f_Wup": inputs["L0_ffn_W_up_base"],
        "f_Wdn": inputs["L0_ffn_W_down"], "f_ln1": inputs["L0_ffn_ln1_w"],
        "f_ln2": inputs["L0_ffn_ln2_w"],
        "r_Wr": inputs["L0_attn_b_W_r"], "r_Wk": inputs["L0_attn_b_W_k"],
        "r_Wv": inputs["L0_attn_b_W_v"], "r_Wg": inputs["L0_attn_b_W_g"],
        "r_Wo": inputs["L0_attn_b_W_o"],
        "r_ln_gamma": inputs["L0_attn_b_ln_gamma"], "r_ln_beta": inputs["L0_attn_b_ln_beta"],
        "time_maa_x": inputs["L0_attn_b_time_maa_x"], "time_maa_w": inputs["L0_attn_b_time_maa_w"],
        "time_maa_k": inputs["L0_attn_b_time_maa_k"], "time_maa_v": inputs["L0_attn_b_time_maa_v"],
        "time_maa_r": inputs["L0_attn_b_time_maa_r"], "time_maa_g": inputs["L0_attn_b_time_maa_g"],
        "time_maa_w1": inputs["L0_attn_b_time_maa_w1"], "time_maa_w2": inputs["L0_attn_b_time_maa_w2"],
        "time_decay": inputs["L0_attn_b_time_decay"], "time_decay_w1": inputs["L0_attn_b_time_decay_w1"],
        "time_decay_w2": inputs["L0_attn_b_time_decay_w2"], "time_faaaa": inputs["L0_attn_b_time_faaaa"],
    }
    x = np.asarray(inputs["L0_attn_a_x"])
    base = make_inmap(m, x[0], w)
    in_maps = []
    for c in range(8):
        s = c // 2
        im = dict(base)
        im["x"] = np.ascontiguousarray(x[s])
        in_maps.append(im)
    res = run_bass_kernel_spmd(nc, in_maps, core_ids=list(range(8)))
    return np.stack([np.asarray(res.results[2 * s]["out"]) for s in range(4)], 0)
